# version 04
# baseline (speedup 1.0000x reference)
import numpy as np
import ml_dtypes
from contextlib import ExitStack

import concourse.bass as bass
import concourse.mybir as mybir
from concourse.bass_utils import run_bass_kernel_spmd

F32 = mybir.dt.float32
BF16 = mybir.dt.bfloat16
AF = mybir.ActivationFunctionType
ALU = mybir.AluOpType
AX = mybir.AxisListType

D = 1024
H = 16
DH = 64
M = 266
T = 4096
EPS = 1e-4
RATIO = float(M) ** -0.5
DN = float(DH) ** -0.25
LN_EPS = 1e-5
FF = 4096

NT1 = 8
NT2 = 16
MCH = [(0, 128), (128, 256), (256, 266)]

DEBUG_TAPS = False
PHASE = 99
RG = [[0, 1], [2, 3], [4, 5], [6, 7]]


class Prog:
    ENGINES = ("sync", "gpsimd", "scalar", "vector", "tensor")

    def __init__(self, nc, es):
        self.nc = nc
        self.es = es
        self.streams = {e: [] for e in self.ENGINES}
        self.sems = {}
        self.cnt = {}
        self.guar = {}
        self.engsem = {"scalar": "Cs", "vector": "Cv", "tensor": "Ct", "gpsimd": "Cg"}
        for s in self.engsem.values():
            self.new_sem(s)

    def new_sem(self, name):
        self.sems[name] = self.es.enter_context(self.nc.semaphore(name))
        self.cnt[name] = 0

    def wait(self, eng, dep):
        if dep is None:
            return
        s, v = dep
        if v <= 0:
            return
        if self.engsem.get(eng) == s:
            return
        key = (eng, s)
        if self.guar.get(key, 0) >= v:
            return
        self.guar[key] = v
        h = self.sems[s]
        self.streams[eng].append(lambda e, h=h, v=v: e.wait_ge(h, v))

    def emit(self, eng, fn, deps=()):
        for d in deps:
            self.wait(eng, d)
        s = self.engsem[eng]
        h = self.sems[s]
        if eng in ("vector", "scalar", "gpsimd"):
            cur = self.cnt[s]
            if cur > 0:
                self.streams[eng].append(lambda e, h=h, cur=cur: e.wait_ge(h, cur))
        self.cnt[s] += 1
        v = self.cnt[s]
        self.streams[eng].append(lambda e, fn=fn, h=h: fn(e).then_inc(h, 1))
        return (s, v)

    def dma(self, eng, fn, sem, deps=(), step=16):
        for d in deps:
            self.wait(eng, d)
        if sem not in self.sems:
            self.new_sem(sem)
        self.cnt[sem] += step
        v = self.cnt[sem]
        h = self.sems[sem]
        self.streams[eng].append(lambda e, fn=fn, h=h, step=step: fn(e).then_inc(h, step))
        return (sem, v)

    def barrier(self):
        snap = [(s, c) for s, c in self.cnt.items() if c > 0]
        for e in self.ENGINES:
            for s, c in snap:
                self.wait(e, (s, c))


def build_nc():
    nc = bass.Bass("TRN2", target_bir_lowering=False)
    es = ExitStack()
    P = Prog(nc, es)

    def _finish():
        with nc.Block() as block:
            for eng in Prog.ENGINES:
                stream = P.streams[eng]
                def runner(e, stream=stream):
                    for f in stream:
                        f(e)
                getattr(block, eng)(runner)
        return nc
    tap_kind = "ExternalOutput" if DEBUG_TAPS else "Internal"

    xs = nc.dram_tensor("xs", [T, D], F32, kind="ExternalInput").ap()
    Wq = nc.dram_tensor("Wq", [D, D], F32, kind="ExternalInput").ap()
    Wk = nc.dram_tensor("Wk", [D, D], F32, kind="ExternalInput").ap()
    Wv = nc.dram_tensor("Wv", [D, D], F32, kind="ExternalInput").ap()
    Wo = nc.dram_tensor("Wo", [D, D], F32, kind="ExternalInput").ap()
    bo = nc.dram_tensor("bo", [1, D], F32, kind="ExternalInput").ap()
    proj = nc.dram_tensor("proj", [M, DH], F32, kind="ExternalInput").ap()
    W1 = nc.dram_tensor("W1", [FF, D], F32, kind="ExternalInput").ap()
    b1 = nc.dram_tensor("b1", [FF], F32, kind="ExternalInput").ap()
    W2 = nc.dram_tensor("W2", [D, FF], F32, kind="ExternalInput").ap()
    b2 = nc.dram_tensor("b2", [1, D], F32, kind="ExternalInput").ap()
    ident_bf_in = nc.dram_tensor("ident_bf", [128, 128], BF16, kind="ExternalInput").ap()
    ident_f_in = nc.dram_tensor("ident_f", [128, 128], F32, kind="ExternalInput").ap()
    ones_bf_in = nc.dram_tensor("ones_bf", [128, 128], BF16, kind="ExternalInput").ap()
    ones_f_in = nc.dram_tensor("ones_f", [1, 128], F32, kind="ExternalInput").ap()
    onesER_in = nc.dram_tensor("onesER", [1, 128], F32, kind="ExternalInput").ap()
    onesc_f_in = nc.dram_tensor("onesc_f", [128, 2], F32, kind="ExternalInput").ap()
    bdiag_in = nc.dram_tensor("bdiag", [128, 8, 16], BF16, kind="ExternalInput").ap()
    cpool_in = nc.dram_tensor("cpool", [128, 8], F32, kind="ExternalInput").ap()

    out = nc.dram_tensor("out", [T, D], F32, kind="ExternalOutput").ap()

    xnt_s = nc.dram_tensor("xnt_s", [8, 128, T], BF16, kind=tap_kind).ap()
    k_s = nc.dram_tensor("k_s", [8, 128, T], BF16, kind=tap_kind).ap()
    v_s = nc.dram_tensor("v_s", [32, 128, 16 * 66], BF16, kind=tap_kind).ap()
    attn_s = nc.dram_tensor("attn_s", [32, 128, D], BF16, kind=tap_kind).ap()
    cc1i = nc.dram_tensor("cc1i", [1, 16], F32)
    cc1o = nc.dram_tensor("cc1o", [1, 16], F32)
    cc2i = nc.dram_tensor("cc2i", [3, 16, 128, 65], F32)
    cc2o = nc.dram_tensor("cc2o", [3, 16, 128, 65], F32).ap()
    ctxg_s = (nc.dram_tensor("ctxg_s", [3, 16, 128, 65], F32, kind="ExternalOutput").ap()
              if DEBUG_TAPS else None)
    dbg_s = (nc.dram_tensor("dbg_s", [3, 128, 512], F32, kind="ExternalOutput").ap()
             if DEBUG_TAPS else None)

    ident_bf = nc.alloc_sbuf_tensor("ident_bf_sb", [128, 128], BF16)
    ident_f = nc.alloc_sbuf_tensor("ident_f_sb", [128, 128], F32)
    ones_bf = nc.alloc_sbuf_tensor("ones_bf_sb", [128, 128], BF16)
    ones_f = nc.alloc_sbuf_tensor("ones_f_sb", [1, 128], F32)
    onesER = nc.alloc_sbuf_tensor("onesER_sb", [1, 128], F32)
    onesc_f = nc.alloc_sbuf_tensor("onesc_f_sb", [128, 2], F32)
    cpool = nc.alloc_sbuf_tensor("cpool_sb", [128, 8], F32)
    bdiag = nc.alloc_sbuf_tensor("bdiag_sb", [128, 8, 16], BF16)
    projT = nc.alloc_sbuf_tensor("projT_sb", [128, 272], BF16)
    maxacc = nc.alloc_sbuf_tensor("maxacc_sb", [128, 16], F32)
    tmp16 = nc.alloc_sbuf_tensor("tmp16_sb", [128, 16], F32)
    stab16 = nc.alloc_sbuf_tensor("stab16_sb", [16, 1], F32)
    stabrow = nc.alloc_sbuf_tensor("stabrow_sb", [1, 16], F32)
    stabB = nc.alloc_sbuf_tensor("stabB_sb", [128, 16], F32)
    b1_sb = nc.alloc_sbuf_tensor("b1_sb", [128, 32], F32)
    brow = nc.alloc_sbuf_tensor("brow_sb", [1, 1024], BF16)
    browf = nc.alloc_sbuf_tensor("browf_sb", [1, 2048], F32)
    thr_sb = nc.alloc_sbuf_tensor("thr_sb", [1, 2], F32)
    thrB = nc.alloc_sbuf_tensor("thrB_sb", [128, 2], F32)
    nthrB = nc.alloc_sbuf_tensor("nthrB_sb", [128, 2], F32)
    acc4 = nc.alloc_sbuf_tensor("acc4_sb", [128, 4], F32)
    acc1 = nc.alloc_sbuf_tensor("acc1_sb", [128, 1], F32)

    xs3 = xs.rearrange("(c p) d -> p c d", p=128)
    out3 = out.rearrange("(c p) d -> p c d", p=128)

    es_mid = ExitStack()
    diagk = es_mid.enter_context(nc.sbuf_tensor("diagk_sb", [128, 32, 16], F32))
    biask = es_mid.enter_context(nc.sbuf_tensor("biask_sb", [128, 32, 16], F32))
    statf = es_mid.enter_context(nc.sbuf_tensor("statf_sb", [128, 3, 16, 128], BF16))
    ctxl = es_mid.enter_context(nc.sbuf_tensor("ctxl_sb", [128, 3, 16, 65], F32))

    es_w = ExitStack()
    Wq_bf = es_w.enter_context(nc.sbuf_tensor("Wq_bf", [128, 8, 1024], BF16))
    Wk_bf = es_w.enter_context(nc.sbuf_tensor("Wk_bf", [128, 8, 1024], BF16))
    Wv_bf = es_w.enter_context(nc.sbuf_tensor("Wv_bf", [128, 8, 1024], BF16))
    Wo_bf = es_w.enter_context(nc.sbuf_tensor("Wo_bf", [128, 8, 1024], BF16))
    wstage = es_w.enter_context(nc.sbuf_tensor("wstage", [128, 2, 1024], F32))
    pj_f = es_w.enter_context(nc.sbuf_tensor("pj_f", [128, 3, 64], F32))
    pj_bf = es_w.enter_context(nc.sbuf_tensor("pj_bf", [128, 3, 64], BF16))

    es_p1 = ExitStack()
    PS_XNT = es_p1.enter_context(nc.psum_tensor("PS_XNT", [128, 1024], BF16))
    PS_K = [es_p1.enter_context(nc.psum_tensor(f"PS_K{i}", [128, 512], F32))
            for i in range(2)]
    PS_MIX = es_p1.enter_context(nc.psum_tensor("PS_MIX", [128, 512], F32))
    PS_VD = es_p1.enter_context(nc.psum_tensor("PS_VD", [128, 1024], F32))
    PS_DA = es_p1.enter_context(nc.psum_tensor("PS_DA", [128, 1024], F32))

    t_cpool = P.dma("sync", lambda e: e.dma_start(out=cpool[:], in_=cpool_in), "DW")
    for _i, _v in enumerate([0.0, 1.0, -0.5, LN_EPS, EPS, RATIO, -1.0, 2.0]):
        nc.const_aps.register(F32, _v, cpool[:, _i:_i + 1])
    P.wait("scalar", t_cpool)
    P.wait("vector", t_cpool)
    P.wait("gpsimd", t_cpool)
    t_const = P.dma("sync", lambda e: e.dma_start(out=ident_bf[:], in_=ident_bf_in), "DW")
    P.dma("sync", lambda e: e.dma_start(out=ident_f[:], in_=ident_f_in), "DW")
    P.dma("sync", lambda e: e.dma_start(out=ones_bf[:], in_=ones_bf_in), "DW")
    P.dma("sync", lambda e: e.dma_start(out=ones_f[:], in_=ones_f_in), "DW")
    P.dma("sync", lambda e: e.dma_start(out=onesER[:], in_=onesER_in), "DW")
    P.dma("sync", lambda e: e.dma_start(out=onesc_f[:], in_=onesc_f_in), "DW")
    P.dma("sync", lambda e: e.dma_start(out=bdiag[:], in_=bdiag_in), "DW")
    P.dma("sync", lambda e: e.dma_start(out=pj_f[0:128, 0, :], in_=proj[0:128, :]), "DW")
    P.dma("sync", lambda e: e.dma_start(out=pj_f[0:128, 1, :], in_=proj[128:256, :]), "DW")
    P.dma("sync", lambda e: e.dma_start(out=pj_f[0:10, 2, :], in_=proj[256:266, :]), "DW")
    P.dma("sync", lambda e: e.dma_start(out=browf[0:1, 0:1024], in_=bo), "DW")
    P.dma("sync", lambda e: e.dma_start(out=browf[0:1, 1024:2048], in_=b2), "DW")
    t_cdone = P.dma("sync", lambda e: e.dma_start(
        out=b1_sb[:], in_=b1.rearrange("(c p) -> p c", p=128),
        allow_slow_non_contiguous=True), "DW")

    tok_cast = []
    for wi, (wsrc, wdst) in enumerate([(Wq, Wq_bf), (Wk, Wk_bf), (Wv, Wv_bf), (Wo, Wo_bf)]):
        src3 = wsrc.rearrange("(c p) o -> c p o", p=128)
        for ci in range(8):
            slot = (wi * 8 + ci) % 2
            dep = [tok_cast[-2]] if len(tok_cast) >= 2 else []
            tl = P.dma("sync", lambda e, ci=ci, slot=slot, src3=src3:
                       e.dma_start(out=wstage[:, slot, :], in_=src3[ci]),
                       f"WS{slot}", deps=dep)
            tc = P.emit("vector", lambda e, ci=ci, slot=slot, wdst=wdst:
                        e.tensor_copy(wdst[:, ci, :], wstage[:, slot, :]), deps=[tl])
            tok_cast.append(tc)
    cast_q_done = tok_cast[7]
    cast_k_done = tok_cast[15]
    cast_v_done = tok_cast[23]
    cast_o_done = tok_cast[31]

    t1 = P.emit("vector", lambda e: e.tensor_scalar(pj_bf[:], pj_f[:], DN, None, ALU.mult),
                deps=[t_cdone])
    tt_pj = []
    for c, (m0, m1) in enumerate(MCH):
        rc = m1 - m0
        tt_pj.append(P.emit("tensor", lambda e, c=c, rc=rc: e.transpose(
            PS_XNT[0:64, c * 128:c * 128 + rc], pj_bf[0:rc, c, :], ident_bf[0:rc, 0:rc]),
            deps=[t1, t_const]))
    t2 = P.emit("vector", lambda e: e.tensor_copy(projT[0:64, 0:266], PS_XNT[0:64, 0:266]),
                deps=tt_pj)
    tok_projT = P.emit("vector", lambda e: e.tensor_copy(projT[64:128, 0:266],
                                                         projT[0:64, 0:266]))
    t4 = P.emit("vector", lambda e: e.tensor_add(browf[0:1, 0:1024], browf[0:1, 0:1024],
                                                 browf[0:1, 1024:2048]))
    tok_brow = P.emit("vector", lambda e: e.tensor_copy(brow[0:1, :], browf[0:1, 0:1024]))
    tok_maxinit = P.emit("vector", lambda e: e.memset(maxacc[:], -3.0e38))

    es_a = ExitStack()
    xb = es_a.enter_context(nc.sbuf_tensor("xb", [128, 2, 4, 1024], F32))
    xn_bf = es_a.enter_context(nc.sbuf_tensor("xn_bf", [128, 4, 1024], BF16))
    sqs = es_a.enter_context(nc.sbuf_tensor("sqs", [128, 4, 1024], F32))
    stats = es_a.enter_context(nc.sbuf_tensor("stats", [128, 6, 4], F32))
    xnt_t = es_a.enter_context(nc.sbuf_tensor("xnt_t", [128, 8, 512], BF16))
    kt = es_a.enter_context(nc.sbuf_tensor("kt", [128, 8, 512], BF16))
    k2t = es_a.enter_context(nc.sbuf_tensor("k2t", [128, 2, 512], BF16))
    vt = es_a.enter_context(nc.sbuf_tensor("vt", [128, 4, 1056], BF16))
    d16 = es_a.enter_context(nc.sbuf_tensor("d16", [16, 512], F32))
    maxtile = es_a.enter_context(nc.sbuf_tensor("maxtile", [128, 16, 4], F32))

    tok_vones = P.emit("vector", lambda e: e.memset(
        vt[:].rearrange("p c (h x) -> p c h x", h=16)[:, :, :, 64:65], 1.0))

    last = {}
    tok_xnt_evac = t2
    for i in range(NT1):
        par = i % 2
        deps = []
        if ("xuse_s", i - 2) in last:
            deps += [last[("xuse_s", i - 2)], last[("xuse_v", i - 2)]]
        tx = P.dma("sync", lambda e, i=i, par=par: e.dma_start(
            out=xb[:, par, :, :], in_=xs3[:, 4 * i:4 * i + 4, :]), f"XL{par}", deps=deps)
        tmean = P.emit("vector", lambda e, par=par: e.reduce_sum(
            stats[:, 0, :], xb[:, par, :, :], axis=AX.X), deps=[tx])
        tsq = []
        for c in range(4):
            tsq.append(P.emit("scalar", lambda e, par=par, c=c: e.activation(
                sqs[:, c, :], xb[:, par, c, :], AF.Square,
                accum_out=stats[:, 1, c:c + 1]), deps=[tx]))
        P.emit("vector", lambda e: e.tensor_scalar(
            stats[:, 2, :], stats[:, 0, :], 1.0 / D, None, ALU.mult))
        P.emit("vector", lambda e: e.tensor_mul(
            stats[:, 3, :], stats[:, 2, :], stats[:, 2, :]))
        tev = P.emit("vector", lambda e: e.scalar_tensor_tensor(
            stats[:, 3, :], stats[:, 1, :], 1.0 / D, stats[:, 3, :],
            ALU.mult, ALU.subtract), deps=tsq)
        tln = P.emit("scalar", lambda e: e.activation(
            stats[:, 4, :], stats[:, 3, :], AF.Ln, bias=LN_EPS), deps=[tev])
        trs = P.emit("scalar", lambda e: e.activation(
            stats[:, 4, :], stats[:, 4, :], AF.Exp, scale=-0.5))
        tnmean = P.emit("vector", lambda e: e.tensor_scalar(
            stats[:, 5, :], stats[:, 2, :], -1.0, None, ALU.mult))
        txn = []
        for c in range(4):
            txn.append(P.emit("vector", lambda e, par=par, c=c: e.tensor_scalar(
                xn_bf[:, c, :], xb[:, par, c, :],
                stats[:, 5, c:c + 1], stats[:, 4, c:c + 1], ALU.add, ALU.mult),
                deps=[trs]))
        last[("xuse_s", i)] = tsq[-1]
        last[("xuse_v", i)] = txn[-1]

        reuse = [last[("xnt_w", i - 1)]] if ("xnt_w", i - 1) in last else []
        tev8 = []
        for dc in range(8):
            tts = []
            for c in range(4):
                tts.append(P.emit("tensor", lambda e, c=c, dc=dc: e.transpose(
                    PS_XNT[:, c * 128:(c + 1) * 128],
                    xn_bf[:, c, dc * 128:(dc + 1) * 128], ident_bf[:]),
                    deps=[txn[c], tok_xnt_evac, t_const]))
            tok_xnt_evac = P.emit("vector", lambda e, dc=dc: e.tensor_copy(
                xnt_t[:, dc, :], PS_XNT[:, 0:512]), deps=tts + reuse)
            tev8.append(tok_xnt_evac)
        txw = P.dma("sync", lambda e, i=i: e.dma_start(
            out=xnt_s[:, :, i * 512:(i + 1) * 512].rearrange("a p c -> p a c"),
            in_=xnt_t[:]), f"XNW{par}", deps=tev8)
        last[("xnt_w", i)] = txw

        tkevac = []
        tdiag_mm = []
        dprev = last.get(("dtdone", i - 1))
        for co in range(8):
            pk = co % 2
            dd0 = [tev8[7], cast_k_done]
            if co >= 2:
                dd0.append(tkevac[co - 2])
            elif ("kevac", i - 1, co) in last:
                dd0.append(last[("kevac", i - 1, co)])
            mm = None
            for ci in range(8):
                mm = P.emit("tensor", lambda e, co=co, ci=ci, pk=pk: e.matmul(
                    PS_K[pk][:, :], Wk_bf[:, ci, co * 128:(co + 1) * 128],
                    xnt_t[:, ci, :], start=(ci == 0), stop=(ci == 7)),
                    deps=(dd0 if ci == 0 else []))
            tke = P.emit("vector", lambda e, co=co, pk=pk: e.tensor_copy(
                kt[:, co, :], PS_K[pk][:, :]), deps=[mm])
            tkevac.append(tke)
            dd = []
            if co >= 2:
                dd.append(tdiag_mm[co - 2])
            tk2 = P.emit("vector", lambda e, co=co, pk=pk: e.tensor_mul(
                k2t[:, pk, :], kt[:, co, :], kt[:, co, :]), deps=dd)
            dd = [tk2]
            if co == 0 and dprev is not None:
                dd.append(dprev)
            tdiag_mm.append(P.emit("tensor", lambda e, co=co, pk=pk: e.matmul(
                PS_MIX[0:16, :], bdiag[:, co, :], k2t[:, pk, :],
                start=(co == 0), stop=(co == 7)), deps=dd))
        last[("kevac", i, 0)] = tkevac[6]
        last[("kevac", i, 1)] = tkevac[7]
        tkw = P.dma("sync", lambda e, i=i: e.dma_start(
            out=k_s[:, :, i * 512:(i + 1) * 512].rearrange("a p c -> p a c"),
            in_=kt[:]), f"KW{par}", deps=[tkevac[-1]])

        td16 = P.emit("vector", lambda e: e.tensor_copy(d16[:], PS_MIX[0:16, :]),
                      deps=[tdiag_mm[-1]])
        tprev = td16
        for c in range(4):
            ttr = P.emit("tensor", lambda e, c=c: e.transpose(
                PS_MIX[:, 0:16], d16[0:16, c * 128:(c + 1) * 128],
                ident_f[0:16, 0:16]), deps=[tprev])
            tprev = P.emit("vector", lambda e, i=i, c=c: e.tensor_copy(
                diagk[:, 4 * i + c, :], PS_MIX[:, 0:16]), deps=[ttr])
        last[("dtdone", i)] = tprev

        tvev = []
        for c in range(4):
            dd0 = [tev8[7], cast_v_done, tok_vones]
            if c == 0:
                if ("dred", True) in last:
                    dd0.append(last[("dred", True)])
                if ("vw", i - 1) in last:
                    dd0.append(last[("vw", i - 1)])
            else:
                dd0.append(tvev[-1])
            mm = None
            for ci in range(8):
                for s in range(2):
                    mm = P.emit("tensor", lambda e, c=c, ci=ci, s=s: e.matmul(
                        PS_VD[:, s * 512:(s + 1) * 512],
                        xnt_t[:, ci, c * 128:(c + 1) * 128],
                        Wv_bf[:, ci, s * 512:(s + 1) * 512],
                        start=(ci == 0), stop=(ci == 7)),
                        deps=(dd0 if (ci == 0 and s == 0) else []))
            tvev.append(P.emit("vector", lambda e, c=c: e.tensor_copy(
                vt[:, c, :].rearrange("p (h x) -> p h x", h=16)[:, :, 0:64],
                PS_VD[:].rearrange("p (h x) -> p h x", h=16)), deps=[mm]))
        tvw = P.dma("sync", lambda e, i=i: e.dma_start(
            out=v_s[4 * i:4 * i + 4, :, :].rearrange("a p c -> p a c"),
            in_=vt[:]), f"VW{par}", deps=[tvev[-1]])
        last[("vw", i)] = tvw

        for j in range(8):
            for c in range(4):
                use_vd = ((j * 4 + c) % 2 == 0)
                pd = PS_VD if use_vd else PS_DA
                dd = [tkevac[j], tok_projT]
                if use_vd:
                    dd.append(tvev[-1])
                if ("dred", use_vd) in last:
                    dd.append(last[("dred", use_vd)])
                P.emit("tensor", lambda e, j=j, c=c, pd=pd: e.matmul(
                    pd[:, 0:266], kt[0:64, j, c * 128:(c + 1) * 128],
                    projT[0:64, 0:266], start=True, stop=True,
                    tile_position=(0, 0)), deps=dd)
                mb = P.emit("tensor", lambda e, j=j, c=c, pd=pd: e.matmul(
                    pd[:, 512:778], kt[64:128, j, c * 128:(c + 1) * 128],
                    projT[64:128, 0:266], start=True, stop=True,
                    tile_position=(64, 0)))
                tr = P.emit("vector", lambda e, j=j, c=c, pd=pd: e.reduce_max(
                    maxtile[:, 2 * j:2 * j + 2, c],
                    pd[:].rearrange("p (a x) -> p a x", a=2)[:, :, 0:266],
                    axis=AX.X), deps=[mb, tok_maxinit])
                last[("dred", use_vd)] = tr
        P.emit("vector", lambda e: e.reduce_max(tmp16[:], maxtile[:], axis=AX.X))
        tfold = P.emit("vector", lambda e: e.tensor_max(maxacc[:], maxacc[:], tmp16[:]))
        last[("fold", i)] = tfold

    tms = P.emit("tensor", lambda e: e.transpose(
        PS_MIX[0:16, 0:128], maxacc[:], ident_f[:]),
        deps=[last[("fold", NT1 - 1)], last[("dtdone", NT1 - 1)]])
    trd = P.emit("vector", lambda e: e.reduce_max(stab16[:], PS_MIX[0:16, 0:128],
                                                  axis=AX.X), deps=[tms])
    tg1 = P.dma("gpsimd", lambda e: e.dma_start(
        out=cc1i.ap().rearrange("a m -> m a"), in_=stab16[:]), "GS", deps=[trd])
    tcc1 = P.dma("gpsimd", lambda e: e.collective_compute(
        "AllReduce", ALU.max, replica_groups=RG,
        ins=[cc1i.ap().opt()], outs=[cc1o.ap().opt()]), "CCS", deps=[tg1], step=1)
    tg2 = P.dma("gpsimd", lambda e: e.dma_start(out=stabrow[:], in_=cc1o.ap()), "GS",
                deps=[tcc1])
    tbc = P.emit("tensor", lambda e: e.matmul(
        PS_MIX[:, 0:16], ones_f[0:1, :], stabrow[0:1, :], start=True, stop=True),
        deps=[tg2, trd])
    tsb = P.emit("vector", lambda e: e.tensor_copy(stabB[:], PS_MIX[:, 0:16]), deps=[tbc])
    tok_biask = P.emit("vector", lambda e: e.scalar_tensor_tensor(
        biask[:], diagk[:], -1.0,
        stabB[:].unsqueeze(1).broadcast_to([128, 32, 16]),
        ALU.mult, ALU.subtract))

    if PHASE <= 1:
        return _finish()
    P.barrier()
    es_a.close()
    es_p1.close()

    es_b = ExitStack()
    kr = es_b.enter_context(nc.sbuf_tensor("kr", [128, 2, 4, 512], BF16))
    vr = es_b.enter_context(nc.sbuf_tensor("vr", [128, 2, 4, 528], BF16))
    ek = es_b.enter_context(nc.sbuf_tensor("ek", [128, 2, 2, 272], BF16))
    es_p2 = ExitStack()
    PS_D2 = [es_p2.enter_context(nc.psum_tensor(f"PS_D2{i}", [128, 1024], F32))
             for i in range(2)]
    PS_CTX = es_p2.enter_context(nc.psum_tensor("PS_CTX", [128, 2048], F32))
    ctx_h = PS_CTX[:].rearrange("p (h x) -> p h x", h=8)

    tok_ek_ones = P.emit("vector", lambda e: e.memset(ek[:, :, :, 266:267], 1.0))

    tok_ctx_evac = [None, None, None]
    for g in range(2):
        started_banks = set()
        evac_dep = tok_ctx_evac[2]
        for i in range(NT1):
            par = i % 2
            dd = []
            if ("kruse", g, i - 2) in last:
                dd.append(last[("kruse", g, i - 2)])
            elif g == 1 and ("kruse", 0, NT1 - 2 + i) in last:
                dd.append(last[("kruse", 0, NT1 - 2 + i)])
            tkr = P.dma("sync", lambda e, g=g, i=i, par=par: e.dma_start(
                out=kr[:, par, :, :],
                in_=k_s[4 * g:4 * g + 4, :, i * 512:(i + 1) * 512]
                .rearrange("a p c -> p a c")), f"KR{par}", deps=dd)
            dd = []
            if ("vruse", g, i - 2) in last:
                dd.append(last[("vruse", g, i - 2)])
            elif g == 1 and ("vruse", 0, NT1 - 2 + i) in last:
                dd.append(last[("vruse", 0, NT1 - 2 + i)])
            tvr = P.dma("sync", lambda e, g=g, i=i, par=par: e.dma_start(
                out=vr[:, par, :, :],
                in_=v_s[4 * i:4 * i + 4, :, 528 * g:528 * (g + 1)]
                .rearrange("a p c -> p a c")), f"VR{par}", deps=dd)
            for jl in range(4):
                for c in range(4):
                    sl = (jl * 4 + c) % 2
                    pd = PS_D2[sl]
                    dd = [tkr, tok_projT]
                    if ("eexp", sl) in last:
                        dd.append(last[("eexp", sl)])
                    P.emit("tensor", lambda e, jl=jl, c=c, pd=pd, par=par: e.matmul(
                        pd[:, 0:266], kr[0:64, par, jl, c * 128:(c + 1) * 128],
                        projT[0:64, 0:266], start=True, stop=True,
                        tile_position=(0, 0)), deps=dd)
                    mb = P.emit("tensor", lambda e, jl=jl, c=c, pd=pd, par=par: e.matmul(
                        pd[:, 512:778], kr[64:128, par, jl, c * 128:(c + 1) * 128],
                        projT[64:128, 0:266], start=True, stop=True,
                        tile_position=(64, 0)))
                    texp = []
                    for s in range(2):
                        h = 8 * g + 2 * jl + s
                        dd2 = [mb, tok_biask, tok_ek_ones]
                        if ("ekuse", sl) in last:
                            dd2.append(last[("ekuse", sl)])
                        texp.append(P.emit("scalar",
                                    lambda e, s=s, sl=sl, c=c, h=h, i=i, pd=pd:
                                    e.activation(
                                        ek[:, sl, s, 0:266],
                                        pd[:, 512 * s:512 * s + 266], AF.Exp,
                                        bias=biask[:, 4 * i + c, h:h + 1]), deps=dd2))
                    last[("eexp", sl)] = texp[-1]
                    mmlast = None
                    for s in range(2):
                        hh = 2 * jl + s
                        for ch, (m0, m1) in enumerate(MCH):
                            bank_key = hh // 2
                            first = bank_key not in started_banks
                            if first:
                                started_banks.add(bank_key)
                            dd3 = [texp[s], tvr]
                            if evac_dep is not None:
                                dd3.append(evac_dep)
                            lo = (hh // 2) * 512 + 256 * (hh % 2) + 68 * ch
                            rows = 128 if ch < 2 else 11
                            e0, e1 = (m0, m1 + 1) if ch == 2 else (m0, m1)
                            mmlast = P.emit("tensor",
                                lambda e, sl=sl, s=s, ch=ch, c=c, par=par,
                                       first=first, lo=lo, rows=rows, e0=e0, e1=e1,
                                       jl=jl: e.matmul(
                                    PS_CTX[0:rows, lo:lo + 65],
                                    ek[:, sl, s, e0:e1],
                                    vr[:, par, c, (2 * jl + s) * 66:(2 * jl + s) * 66 + 65],
                                    start=first, stop=False,
                                    skip_group_check=True), deps=dd3)
                    last[("ekuse", sl)] = mmlast
            last[("kruse", g, i)] = last[("eexp", (jl * 4 + 3) % 2)]
            last[("vruse", g, i)] = last[("ekuse", (jl * 4 + 3) % 2)]
        for ch in range(3):
            tok_ctx_evac[ch] = P.emit("vector", lambda e, g=g, ch=ch: e.tensor_copy(
                ctxl[:, ch, 8 * g:8 * g + 8, :],
                ctx_h[:, :, 68 * ch:68 * ch + 65]),
                deps=[last[("ekuse", 0)], last[("ekuse", 1)]])

    if DEBUG_TAPS:
        P.dma("sync", lambda e: e.dma_start(
            out=dbg_s[0], in_=diagk[:].rearrange("p a b -> p (a b)")), "DBG",
            deps=[tok_biask])
        P.dma("sync", lambda e: e.dma_start(
            out=dbg_s[1], in_=biask[:].rearrange("p a b -> p (a b)")), "DBG",
            deps=[tok_biask])
        P.dma("sync", lambda e: e.dma_start(
            out=dbg_s[2][0:1, 0:16], in_=stabrow[:]), "DBG", deps=[tok_biask])
        P.dma("sync", lambda e: e.dma_start(
            out=dbg_s[2][1:2, 0:16].rearrange("a b -> b a"), in_=stab16[:]), "DBG",
            deps=[tok_biask])
        P.dma("sync", lambda e: e.dma_start(
            out=dbg_s[2][2:3, 0:16], in_=maxacc[0:1, :]), "DBG", deps=[tok_biask])
        P.dma("sync", lambda e: e.dma_start(
            out=dbg_s[2][3:4, 0:16], in_=cc1i.ap()), "DBG", deps=[tok_biask])
        P.dma("sync", lambda e: e.dma_start(
            out=dbg_s[2][4:5, 0:16], in_=cc1o.ap()), "DBG", deps=[tok_biask])
    if PHASE <= 2:
        return _finish()
    tgw = P.dma("gpsimd", lambda e: e.dma_start(
        out=cc2i.ap().rearrange("a b p c -> p a b c"), in_=ctxl[:]), "GS",
        deps=tok_ctx_evac)
    tcc2 = P.dma("gpsimd", lambda e: e.collective_compute(
        "AllReduce", ALU.add, replica_groups=RG,
        ins=[cc2i.ap().opt()], outs=[cc2o.opt()]), "CCS", deps=[tgw], step=1)
    tgr = P.dma("gpsimd", lambda e: e.dma_start(
        out=ctxl[:], in_=cc2o.rearrange("a b p c -> p a b c")), "GS", deps=[tcc2])
    if DEBUG_TAPS:
        P.dma("gpsimd", lambda e: e.dma_start(out=ctxg_s, in_=cc2o), "GS", deps=[tcc2])

    P.barrier()
    es_b.close()
    es_p2.close()
    es_fx = ExitStack()
    PS_FIX = es_fx.enter_context(nc.psum_tensor("PS_FIX", [128, 512], F32))
    PS_CS = es_fx.enter_context(nc.psum_tensor("PS_CS", [1, 128], F32))
    vrows = es_fx.enter_context(nc.sbuf_tensor("vrows", [1, 16, 65], F32))
    csr_sb = es_fx.enter_context(nc.sbuf_tensor("csr_sb", [1, 16, 128], BF16))

    tvrows = P.dma("sync", lambda e: e.dma_start(
        out=vrows[0:1, :, :], in_=ctxl[10:11, 2, :, :]), "FXD", deps=[tgr])
    for h in range(H):
        dd = [tgr]
        if h >= 1:
            dd.append(last[("fixv", h - 1)])
        dd.append(tvrows)
        tvb = P.emit("tensor", lambda e, h=h: e.matmul(
            PS_FIX[:, 0:65], onesER[0:1, :], vrows[0:1, h, :],
            start=True, stop=True), deps=dd)
        tst = None
        for ch in range(3):
            P.emit("vector", lambda e, h=h, ch=ch: e.scalar_tensor_tensor(
                statf[:, ch, h, 0:65], ctxl[:, ch, h, :], RATIO, PS_FIX[:, 0:65],
                ALU.mult, ALU.add), deps=[tvb, tgr])
            tst = P.emit("vector", lambda e, h=h, ch=ch: e.tensor_copy(
                statf[:, ch, h, 65:128],
                statf[:, ch, h, 64:65].broadcast_to([128, 63])))
        last[("fixv", h)] = tst
        dd2 = [tst]
        if h >= 1:
            dd2.append(last[("csrow", h - 1)])
        P.emit("tensor", lambda e, h=h: e.matmul(
            PS_CS[0:1, :], ones_bf[:, 0:1], statf[:, 0, h, :],
            start=True, stop=False), deps=dd2)
        P.emit("tensor", lambda e, h=h: e.matmul(
            PS_CS[0:1, :], ones_bf[:, 0:1], statf[:, 1, h, :],
            start=False, stop=False))
        tc2 = P.emit("tensor", lambda e, h=h: e.matmul(
            PS_CS[0:1, :], ones_bf[0:10, 0:1], statf[0:10, 2, h, :],
            start=False, stop=True))
        last[("csrow", h)] = P.emit("vector", lambda e, h=h: e.tensor_scalar(
            csr_sb[0:1, h, :], PS_CS[0:1, :], EPS, None, ALU.mult), deps=[tc2])
    tok_statf = P.dma("sync", lambda e: e.dma_start(
        out=statf[10:11, 2, :, :], in_=csr_sb[0:1, :, :]), "FXD",
        deps=[last[("csrow", H - 1)]])

    if PHASE <= 3:
        return _finish()
    P.barrier()
    es_fx.close()
    es_c = ExitStack()
    xr = es_c.enter_context(nc.sbuf_tensor("xr", [128, 2, 8, 512], BF16))
    q_sb = es_c.enter_context(nc.sbuf_tensor("q_sb", [128, 8, 512], BF16))
    q2t = es_c.enter_context(nc.sbuf_tensor("q2t", [128, 2, 512], BF16))
    dq16 = es_c.enter_context(nc.sbuf_tensor("dq16", [16, 512], F32))
    diagq = es_c.enter_context(nc.sbuf_tensor("diagq", [128, 4, 16], F32))
    negq = es_c.enter_context(nc.sbuf_tensor("negq", [128, 4, 16], F32))
    biasq = es_c.enter_context(nc.sbuf_tensor("biasq", [128, 4, 16], F32))
    eq = es_c.enter_context(nc.sbuf_tensor("eq", [128, 2, 2, 272], BF16))
    qpT = es_c.enter_context(nc.sbuf_tensor("qpT", [128, 2, 3, 512], BF16))
    recp = es_c.enter_context(nc.sbuf_tensor("recp", [64, 512], F32))
    outall = es_c.enter_context(nc.sbuf_tensor("outall", [128, 8, 512], BF16))
    attn_t = es_c.enter_context(nc.sbuf_tensor("attn_t", [128, 4, 1024], BF16))
    es_p3 = ExitStack()
    PS_QD = es_p3.enter_context(nc.psum_tensor("PS_QD", [128, 1024], F32))
    PS_DSH = es_p3.enter_context(nc.psum_tensor("PS_DSH", [128, 1024], F32))
    PS_QPT = es_p3.enter_context(nc.psum_tensor("PS_QPT", [128, 3072], BF16))
    PS_OUT = es_p3.enter_context(nc.psum_tensor("PS_OUT", [128, 512], F32))

    tok_qpt_ones = None
    for _js in range(2):
        tok_qpt_ones = P.dma("sync", lambda e, _js=_js: e.dma_start(
            out=qpT[10:11, _js, 2, :].rearrange("p (a b) -> p a b", a=4),
            in_=ones_bf[0:1, :].rearrange("p (a b) -> p a b", a=1)
            .broadcast_to([1, 4, 128])), "QPO")

    for i in range(NT1):
        par = i % 2
        dd = []
        if ("xruse", i - 2) in last:
            dd.append(last[("xruse", i - 2)])
        txr = P.dma("sync", lambda e, i=i, par=par: e.dma_start(
            out=xr[:, par, :, :],
            in_=xnt_s[:, :, i * 512:(i + 1) * 512].rearrange("a p c -> p a c")),
            f"XR{par}", deps=dd)
        tqevac = []
        tdq_mm = []
        for co in range(8):
            pk = co % 2
            dd0 = [txr, cast_q_done]
            if co >= 2:
                dd0.append(tqevac[co - 2])
            elif ("qfree", i - 1) in last:
                dd0.append(last[("qfree", i - 1)])
            mm = None
            for ci in range(8):
                mm = P.emit("tensor", lambda e, co=co, ci=ci, pk=pk, par=par: e.matmul(
                    PS_QD[:, pk * 512:(pk + 1) * 512],
                    Wq_bf[:, ci, co * 128:(co + 1) * 128], xr[:, par, ci, :],
                    start=(ci == 0), stop=(ci == 7)), deps=(dd0 if ci == 0 else []))
            tqe = P.emit("vector", lambda e, co=co, pk=pk: e.tensor_copy(
                q_sb[:, co, :], PS_QD[:, pk * 512:(pk + 1) * 512]), deps=[mm])
            tqevac.append(tqe)
            dd = []
            if co >= 2:
                dd.append(tdq_mm[co - 2])
            tq2 = P.emit("vector", lambda e, co=co, pk=pk: e.tensor_mul(
                q2t[:, pk, :], q_sb[:, co, :], q_sb[:, co, :]), deps=dd)
            dd = [tq2]
            if co == 0 and ("outev",) in last:
                dd.append(last[("outev",)])
            tdq_mm.append(P.emit("tensor", lambda e, co=co, pk=pk: e.matmul(
                PS_OUT[0:16, :], bdiag[:, co, :], q2t[:, pk, :],
                start=(co == 0), stop=(co == 7)), deps=dd))
        tdq16 = P.emit("vector", lambda e: e.tensor_copy(dq16[:], PS_OUT[0:16, :]),
                       deps=[tdq_mm[-1]])
        tprev = tdq16
        for c in range(4):
            ttr = P.emit("tensor", lambda e, c=c: e.transpose(
                PS_OUT[:, 0:16], dq16[0:16, c * 128:(c + 1) * 128],
                ident_f[0:16, 0:16]), deps=[tprev])
            tprev = P.emit("vector", lambda e, c=c: e.tensor_copy(
                diagq[:, c, :], PS_OUT[:, 0:16]), deps=[ttr])
        tok_diagq = tprev

        for j in range(8):
            js = j % 2
            for c in range(4):
                use_qd = ((j * 4 + c) % 2 == 0)
                sl = (j * 4 + c) % 2
                pd = PS_QD if use_qd else PS_DSH
                dd = [tqevac[j], tok_projT, tqevac[-1]]
                if ("qdred", use_qd) in last:
                    dd.append(last[("qdred", use_qd)])
                P.emit("tensor", lambda e, j=j, c=c, pd=pd: e.matmul(
                    pd[:, 0:266], q_sb[0:64, j, c * 128:(c + 1) * 128],
                    projT[0:64, 0:266], start=True, stop=True,
                    tile_position=(0, 0)), deps=dd)
                mb = P.emit("tensor", lambda e, j=j, c=c, pd=pd: e.matmul(
                    pd[:, 512:778], q_sb[64:128, j, c * 128:(c + 1) * 128],
                    projT[64:128, 0:266], start=True, stop=True,
                    tile_position=(64, 0)))
                P.emit("vector", lambda e, j=j, c=c, pd=pd: e.reduce_max(
                    negq[:, c, 2 * j:2 * j + 2],
                    pd[:].rearrange("p (a x) -> p a x", a=2)[:, :, 0:266],
                    axis=AX.X, negate=True), deps=[mb, tok_diagq])
                tbq = P.emit("vector", lambda e, j=j, c=c: e.scalar_tensor_tensor(
                    biasq[:, c, 2 * j:2 * j + 2], diagq[:, c, 2 * j:2 * j + 2], -1.0,
                    negq[:, c, 2 * j:2 * j + 2], ALU.mult, ALU.add))
                texp = []
                for s in range(2):
                    dd2 = [tbq]
                    if ("equse", sl) in last:
                        dd2.append(last[("equse", sl)])
                    texp.append(P.emit("scalar",
                                lambda e, s=s, sl=sl, c=c, j=j, pd=pd: e.activation(
                                    eq[:, sl, s, 0:266], pd[:, 512 * s:512 * s + 266],
                                    AF.Exp, bias=biasq[:, c, 2 * j + s:2 * j + s + 1]),
                                deps=dd2))
                last[("qdred", use_qd)] = texp[-1]
                for s in range(2):
                    for mkc, (m0, m1) in enumerate(MCH):
                        rc = m1 - m0
                        reg = (s * 3 + mkc) * 512 + c * 128
                        dd3 = [texp[s]]
                        if ("qptevall",) in last:
                            dd3.append(last[("qptevall",)])
                        tq = P.emit("tensor", lambda e, s=s, mkc=mkc, m0=m0, rc=rc,
                                    reg=reg, sl=sl, j=j: e.transpose(
                                        PS_QPT[0:rc, reg:reg + 128],
                                        eq[:, sl, s, m0:m0 + rc], ident_bf[:]),
                                    deps=dd3)
                        last[("equse", sl)] = tq
                        last[("qpt_w", s, mkc)] = tq
            for s in range(2):
                h = 2 * j + s
                tevs = []
                for mkc in range(3):
                    rc2 = 128 if mkc < 2 else 10
                    dd4 = [last[("qpt_w", s, mkc)], tok_qpt_ones]
                    if ("qpT_r", js) in last:
                        dd4.append(last[("qpT_r", js)])
                    tevs.append(P.emit("vector", lambda e, s=s, mkc=mkc, js=js, rc2=rc2:
                                e.tensor_copy(qpT[0:rc2, js, mkc, :],
                                              PS_QPT[0:rc2, (s * 3 + mkc) * 512:
                                                     (s * 3 + mkc) * 512 + 512]),
                                deps=dd4))
                    last[("qptevall",)] = tevs[-1]
                dd5 = tevs + [tok_statf, tok_diagq]
                if ("outev",) in last:
                    dd5.append(last[("outev",)])
                mo = None
                for mkc in range(3):
                    kk = 128 if mkc < 2 else 11
                    mo = P.emit("tensor", lambda e, h=h, mkc=mkc, js=js, kk=kk: e.matmul(
                        PS_OUT[:, :], statf[0:kk, mkc, h, :], qpT[0:kk, js, mkc, :],
                        start=(mkc == 0), stop=(mkc == 2)),
                        deps=(dd5 if mkc == 0 else []))
                last[("qpT_r", js)] = mo
                P.emit("vector", lambda e: e.reciprocal(
                    recp[:], PS_OUT[64:128, :]), deps=[mo])
                tmu = P.emit("vector", lambda e, h=h: e.tensor_mul(
                    outall[(h % 2) * 64:(h % 2) * 64 + 64, h // 2, :],
                    PS_OUT[0:64, :], recp[:]))
                last[("outev",)] = tmu
        tatt = []
        for tc in range(4):
            dd6 = [last[("outev",)], cast_o_done, last[("qdred", False)]]
            if tatt:
                dd6.append(tatt[-1])
            if tc == 0 and ("attnw", i - 1) in last:
                dd6.append(last[("attnw", i - 1)])
            mm = None
            for co in range(8):
                for s in range(2):
                    mm = P.emit("tensor", lambda e, tc=tc, co=co, s=s: e.matmul(
                        PS_DSH[:, s * 512:(s + 1) * 512],
                        outall[:, co, tc * 128:(tc + 1) * 128],
                        Wo_bf[:, co, s * 512:(s + 1) * 512],
                        start=(co == 0), stop=(co == 7)),
                        deps=(dd6 if (co == 0 and s == 0) else []))
            tatt.append(P.emit("vector", lambda e, tc=tc: e.tensor_copy(
                attn_t[:, tc, :], PS_DSH[:, :]), deps=[mm]))
        last[("qdred", False)] = tatt[-1]
        taw = P.dma("sync", lambda e, i=i: e.dma_start(
            out=attn_s[4 * i:4 * i + 4, :, :].rearrange("a p c -> p a c"),
            in_=attn_t[:]), f"AW{par}",
            deps=[tatt[-1]])
        last[("attnw", i)] = taw
        last[("xruse", i)] = last[("qdred", True)]
        last[("qfree", i)] = last[("qdred", True)]

    if PHASE <= 4:
        return _finish()
    P.barrier()
    es_c.close()
    es_p3.close()
    es_w.close()
    es_mid.close()

    es_d = ExitStack()
    W1T = es_d.enter_context(nc.sbuf_tensor("W1T", [128, 8, 4096], BF16))
    W2T = es_d.enter_context(nc.sbuf_tensor("W2T", [128, 32, 1024], BF16))
    es_d2 = ExitStack()
    stage = es_d2.enter_context(nc.sbuf_tensor("stage", [128, 2, 4, 1024], F32))
    pmask = es_d2.enter_context(nc.sbuf_tensor("pmask", [128, 4, 1024], BF16))
    nmask = es_d2.enter_context(nc.sbuf_tensor("nmask", [128, 4, 1024], BF16))
    tern = es_d2.enter_context(nc.sbuf_tensor("tern", [128, 4, 1024], BF16))
    es_p4 = ExitStack()
    PS_T = [es_p4.enter_context(nc.psum_tensor(f"PS_T{i}", [128, 1024], BF16))
            for i in range(2)]
    PS_P = es_p4.enter_context(nc.psum_tensor("PS_P", [128, 2], F32))

    W1v = W1.rearrange("(a c p) d -> a p c d", p=128, c=4)
    W2v = W2.rearrange("(a p) (c d) -> a p c d", p=128, c=4)

    for wsel, wv in enumerate([W1v, W2v]):
        a = wsel * 2
        tinit = P.emit("vector", lambda e, a=a: e.memset(acc4[:, a:a + 1], 0.0))
        for pc in range(8):
            par = pc % 2
            dd = []
            if ("scuse", wsel, pc - 2) in last:
                dd.append(last[("scuse", wsel, pc - 2)])
            elif wsel == 1 and ("scuse", 0, 6 + pc) in last:
                dd.append(last[("scuse", 0, 6 + pc)])
            tld = P.dma("sync", lambda e, wv=wv, pc=pc, par=par: e.dma_start(
                out=stage[:, par, :, :], in_=wv[pc]), f"PL{par}", deps=dd)
            P.emit("vector", lambda e, par=par: e.reduce_sum(
                acc1[:, 0:1], stage[:, par, :, :].rearrange("p c d -> p (c d)"),
                axis=AX.X, apply_absolute_value=True), deps=[tld, tinit])
            tacc = P.emit("vector", lambda e, a=a: e.tensor_add(
                acc4[:, a:a + 1], acc4[:, a:a + 1], acc1[:, 0:1]))
            last[("scuse", wsel, pc)] = tacc
    tthr = []
    for wsel in range(2):
        a = wsel * 2
        dd = [last[("scuse", wsel, 7)]]
        if tthr:
            dd.append(tthr[-1])
        tp = P.emit("tensor", lambda e, a=a, wsel=wsel: e.matmul(
            PS_P[0:1, wsel:wsel + 1], onesc_f[:, 0:1], acc4[:, a:a + 1],
            start=True, stop=True), deps=dd)
        P.emit("vector", lambda e, wsel=wsel: e.tensor_scalar(
            thr_sb[0:1, wsel:wsel + 1], PS_P[0:1, wsel:wsel + 1],
            1.0 / (FF * D), None, ALU.mult), deps=[tp])
        t6 = P.emit("vector", lambda e, wsel=wsel: e.tensor_scalar(
            thr_sb[0:1, wsel:wsel + 1], thr_sb[0:1, wsel:wsel + 1],
            1e-8, 0.5, ALU.max, ALU.mult))
        tthr.append(t6)
    tbb = P.emit("tensor", lambda e: e.matmul(
        PS_P[:, 0:2], ones_f[0:1, :], thr_sb[0:1, 0:2], start=True, stop=True),
        deps=tthr)
    tthB = P.emit("vector", lambda e: e.tensor_copy(thrB[:], PS_P[:, 0:2]), deps=[tbb])
    tnth = P.emit("vector", lambda e: e.tensor_scalar(
        nthrB[:, 0:2], thrB[:, 0:2], -1.0, None, ALU.mult))

    for wsel, wv in enumerate([W1v, W2v]):
        for pc in range(8):
            par = pc % 2
            dd = []
            if ("tnuse", wsel, pc - 2) in last:
                dd.append(last[("tnuse", wsel, pc - 2)])
            elif wsel == 1 and ("tnuse", 0, 6 + pc) in last:
                dd.append(last[("tnuse", 0, 6 + pc)])
            elif wsel == 0 and ("scuse", 1, 6 + pc) in last:
                dd.append(last[("scuse", 1, 6 + pc)])
            tld = P.dma("sync", lambda e, wv=wv, pc=pc, par=par: e.dma_start(
                out=stage[:, par, :, :], in_=wv[pc]), f"PL{par}", deps=dd)
            ddp = [tld, tthB, tnth]
            if ("terns",) in last:
                ddp.append(last[("terns",)])
            P.emit("gpsimd", lambda e, par=par, wsel=wsel: e.tensor_scalar(
                pmask[:], stage[:, par, :, :], thrB[:, wsel:wsel + 1], None,
                ALU.is_gt), deps=ddp)
            tnm2 = P.emit("gpsimd", lambda e, par=par, wsel=wsel: e.tensor_scalar(
                nmask[:], stage[:, par, :, :], nthrB[:, wsel:wsel + 1], None,
                ALU.is_lt))
            last[("tnuse", wsel, pc)] = tnm2
            dd2 = [tnm2]
            if ("ternuse",) in last:
                dd2.append(last[("ternuse",)])
            tts2 = P.emit("vector", lambda e: e.tensor_sub(tern[:], pmask[:], nmask[:]),
                          deps=dd2)
            last[("terns",)] = tts2
            if wsel == 0:
                for dc in range(8):
                    pt = PS_T[dc % 2]
                    tts3 = []
                    for c in range(4):
                        dd3 = [tts2]
                        if ("ptev", dc % 2) in last:
                            dd3.append(last[("ptev", dc % 2)])
                        tts3.append(P.emit("tensor", lambda e, c=c, dc=dc, pt=pt:
                                    e.transpose(pt[:, c * 128:(c + 1) * 128],
                                                tern[:, c, dc * 128:(dc + 1) * 128],
                                                ident_bf[:]), deps=dd3))
                    tev = P.emit("vector", lambda e, dc=dc, pc=pc, pt=pt: e.tensor_copy(
                        W1T[:, dc, pc * 512:(pc + 1) * 512], pt[:, 0:512]), deps=tts3)
                    last[("ptev", dc % 2)] = tev
                    last[("ternuse",)] = tev
            else:
                for fq in range(4):
                    pt = PS_T[fq % 2]
                    tts3 = []
                    for fo in range(8):
                        ffc = fq * 8 + fo
                        dd3 = [tts2]
                        if ("ptev", fq % 2) in last:
                            dd3.append(last[("ptev", fq % 2)])
                        tts3.append(P.emit("tensor", lambda e, ffc=ffc, fo=fo, pt=pt:
                                    e.transpose(pt[:, fo * 128:(fo + 1) * 128],
                                                tern[:, ffc // 8, (ffc % 8) * 128:
                                                     (ffc % 8) * 128 + 128],
                                                ident_bf[:]), deps=dd3))
                    tev = P.emit("vector", lambda e, fq=fq, pc=pc, pt=pt: e.tensor_copy(
                        W2T[:, fq * 8:fq * 8 + 8, pc * 128:(pc + 1) * 128],
                        pt[:].rearrange("p (a b) -> p a b", a=8)), deps=tts3)
                    last[("ptev", fq % 2)] = tev
                    last[("ternuse",)] = tev

    if PHASE <= 5:
        return _finish()
    P.barrier()
    es_p4.close()
    es_d2.close()
    es_e = ExitStack()
    xr2 = es_e.enter_context(nc.sbuf_tensor("xr2", [128, 2, 8, 256], BF16))
    at2 = es_e.enter_context(nc.sbuf_tensor("at2", [128, 2, 2, 1024], BF16))
    x2 = es_e.enter_context(nc.sbuf_tensor("x2", [128, 2, 1024], F32))
    h_sb = es_e.enter_context(nc.sbuf_tensor("h_sb", [128, 32, 256], BF16))
    af = es_e.enter_context(nc.sbuf_tensor("af", [128, 1024], F32))
    outb = es_e.enter_context(nc.sbuf_tensor("outb", [128, 2, 2048], F32))
    es_p5 = ExitStack()
    PS_Hh = [es_p5.enter_context(nc.psum_tensor(f"PS_H{i}", [128, 256], F32))
             for i in range(2)]
    PS_O = es_p5.enter_context(nc.psum_tensor("PS_O", [128, 2048], F32))

    n_out_dma = 0
    for i in range(NT2):
        par = i % 2
        dd = []
        if ("x2use", i - 2) in last:
            dd.append(last[("x2use", i - 2)])
        txr2 = P.dma("sync", lambda e, i=i, par=par: e.dma_start(
            out=xr2[:, par, :, :],
            in_=xnt_s[:, :, i * 256:(i + 1) * 256].rearrange("a p c -> p a c")),
            f"X2L{par}", deps=dd)
        tat2 = P.dma("sync", lambda e, i=i, par=par: e.dma_start(
            out=at2[:, par, :, :],
            in_=attn_s[2 * i:2 * i + 2, :, :].rearrange("a p c -> p a c")),
            f"A2L{par}", deps=dd)
        ddx = list(dd)
        if ("x2s", i - 1) in last:
            ddx.append(last[("x2s", i - 1)])
        tx2 = P.dma("sync", lambda e, i=i: e.dma_start(
            out=x2[:, :, :], in_=xs3[:, 2 * i:2 * i + 2, :]), "X2S", deps=ddx)
        tgel = []
        for ffc in range(32):
            ph = PS_Hh[ffc % 2]
            dd0 = [txr2]
            if ffc >= 2:
                dd0.append(tgel[ffc - 2])
            elif ("gel", i - 1, ffc) in last:
                dd0.append(last[("gel", i - 1, ffc)])
            mm = None
            for ci in range(8):
                mm = P.emit("tensor", lambda e, ffc=ffc, ci=ci, ph=ph, par=par: e.matmul(
                    ph[:, :], W1T[:, ci, ffc * 128:(ffc + 1) * 128],
                    xr2[:, par, ci, :], start=(ci == 0), stop=(ci == 7)),
                    deps=(dd0 if ci == 0 else []))
            tg = P.emit("scalar", lambda e, ffc=ffc, ph=ph: e.activation(
                h_sb[:, ffc, :], ph[:, :], AF.Gelu, bias=b1_sb[:, ffc:ffc + 1]),
                deps=[mm, t_cdone])
            tgel.append(tg)
            last[("gel", i, ffc)] = tg
        mo = None
        for ffc in range(32):
            for tch in range(2):
                for s in range(2):
                    dd0 = [tgel[ffc]]
                    if ffc == 0 and tch == 0 and s == 0 and ("oev", i - 1) in last:
                        dd0.append(last[("oev", i - 1)])
                    mo = P.emit("tensor", lambda e, ffc=ffc, tch=tch, s=s: e.matmul(
                        PS_O[:, tch * 1024 + s * 512:tch * 1024 + (s + 1) * 512],
                        h_sb[:, ffc, tch * 128:(tch + 1) * 128],
                        W2T[:, ffc, s * 512:(s + 1) * 512],
                        start=(ffc == 0), stop=False,
                        skip_group_check=True), deps=dd0)
        tbias = []
        for tch in range(2):
            for s in range(2):
                tbias.append(P.emit("tensor", lambda e, tch=tch, s=s: e.matmul(
                    PS_O[:, tch * 1024 + s * 512:tch * 1024 + (s + 1) * 512],
                    ones_bf[0:1, 0:128], brow[0:1, s * 512:(s + 1) * 512],
                    start=False, stop=True, skip_group_check=True),
                    deps=[tok_brow]))
        tev2 = None
        for tch in range(2):
            P.emit("vector", lambda e, tch=tch, par=par: e.tensor_copy(
                af[:, :], at2[:, par, tch, :]), deps=[tat2, tbias[-1]])
            P.emit("vector", lambda e, tch=tch, par=par: e.tensor_add(
                outb[:, par, tch * 1024:(tch + 1) * 1024],
                PS_O[:, tch * 1024:(tch + 1) * 1024], x2[:, tch, :]), deps=[tx2])
            tev2 = P.emit("vector", lambda e, tch=tch, par=par: e.tensor_add(
                outb[:, par, tch * 1024:(tch + 1) * 1024],
                outb[:, par, tch * 1024:(tch + 1) * 1024], af[:, :]))
        last[("oev", i)] = tev2
        last[("x2use", i)] = tev2
        last[("x2s", i)] = tev2
        P.dma("sync", lambda e, i=i, par=par: e.dma_start(
            out=out3[:, 2 * i:2 * i + 2, :],
            in_=outb[:, par, :].rearrange("p (c d) -> p c d", c=2)), "DO", deps=[tev2])
        n_out_dma += 1

    P.wait("sync", ("DO", 16 * n_out_dma))

    with nc.Block() as block:
        for eng in Prog.ENGINES:
            stream = P.streams[eng]
            def runner(e, stream=stream):
                for f in stream:
                    f(e)
            getattr(block, eng)(runner)
    return nc


_NC_CACHE = None


def _consts():
    ident = np.eye(128)
    c = {
        "ident_bf": ident.astype(ml_dtypes.bfloat16),
        "ident_f": ident.astype(np.float32),
        "ones_bf": np.ones((128, 128), ml_dtypes.bfloat16),
        "ones_f": np.ones((1, 128), np.float32),
        "onesER": np.full((1, 128), EPS * RATIO, np.float32),
        "onesc_f": np.ones((128, 2), np.float32),
        "cpool": np.tile(np.array([[0.0, 1.0, -0.5, LN_EPS, EPS, RATIO, -1.0, 2.0]], np.float32), (128, 1)),
    }
    bd = np.zeros((128, 8, 16), np.float32)
    for p in range(128):
        for co in range(8):
            bd[p, co, 2 * co + p // 64] = 1.0 / 16.0
    c["bdiag"] = bd.astype(ml_dtypes.bfloat16)
    return c


def kernel(L0_attn_x, L0_attn_Wq, L0_attn_Wk, L0_attn_Wv, L0_attn_Wo, L0_attn_bo,
           L0_attn_proj, L0_ffn_W1, L0_ffn_b1, L0_ffn_W2, L0_ffn_b2):
    global _NC_CACHE
    x = np.asarray(L0_attn_x, np.float32)
    shared = {
        "Wq": np.ascontiguousarray(L0_attn_Wq, np.float32),
        "Wk": np.ascontiguousarray(L0_attn_Wk, np.float32),
        "Wv": np.ascontiguousarray(L0_attn_Wv, np.float32),
        "Wo": np.ascontiguousarray(L0_attn_Wo, np.float32),
        "bo": np.ascontiguousarray(L0_attn_bo, np.float32).reshape(1, D),
        "proj": np.ascontiguousarray(L0_attn_proj, np.float32),
        "W1": np.ascontiguousarray(L0_ffn_W1, np.float32),
        "b1": np.ascontiguousarray(L0_ffn_b1, np.float32).reshape(FF),
        "W2": np.ascontiguousarray(L0_ffn_W2, np.float32),
        "b2": np.ascontiguousarray(L0_ffn_b2, np.float32).reshape(1, D),
        **_consts(),
    }
    in_maps = []
    for c in range(8):
        b, half = c // 2, c % 2
        xs_c = np.ascontiguousarray(x[b, half * T:(half + 1) * T, :], np.float32)
        in_maps.append({"xs": xs_c, **shared})

    if _NC_CACHE is None:
        _NC_CACHE = build_nc()
    res = run_bass_kernel_spmd(_NC_CACHE, in_maps, core_ids=list(range(8)))
    outp = np.empty((4, 8192, D), np.float32)
    for c in range(8):
        b, half = c // 2, c % 2
        outp[b, half * T:(half + 1) * T, :] = res.results[c]["out"]
    return outp


# version 06
# speedup vs baseline: 1.4477x; 1.4477x over previous
import numpy as np
import ml_dtypes
from contextlib import ExitStack

import concourse.bass as bass
import concourse.mybir as mybir
from concourse.bass_utils import run_bass_kernel_spmd

F32 = mybir.dt.float32
BF16 = mybir.dt.bfloat16
AF = mybir.ActivationFunctionType
ALU = mybir.AluOpType
AX = mybir.AxisListType

D = 1024
H = 16
DH = 64
M = 266
T = 4096
EPS = 1e-4
RATIO = float(M) ** -0.5
DN = float(DH) ** -0.25
LN_EPS = 1e-5
FF = 4096

NT1 = 8
NT2 = 16
MCH = [(0, 128), (128, 256), (256, 266)]

DEBUG_TAPS = False
PHASE = 99
RG = [[0, 1], [2, 3], [4, 5], [6, 7]]


class Prog:
    ENGINES = ("sync", "gpsimd", "scalar", "vector", "tensor")

    def __init__(self, nc, es):
        self.nc = nc
        self.es = es
        self.streams = {e: [] for e in self.ENGINES}
        self.sems = {}
        self.cnt = {}
        self.guar = {}
        self.engsem = {"scalar": "Cs", "vector": "Cv", "tensor": "Ct", "gpsimd": "Cg"}
        for s in self.engsem.values():
            self.new_sem(s)

    def new_sem(self, name):
        self.sems[name] = self.es.enter_context(self.nc.semaphore(name))
        self.cnt[name] = 0

    def wait(self, eng, dep):
        if dep is None:
            return
        s, v = dep
        if v <= 0:
            return
        if self.engsem.get(eng) == s:
            return
        key = (eng, s)
        if self.guar.get(key, 0) >= v:
            return
        self.guar[key] = v
        h = self.sems[s]
        self.streams[eng].append(lambda e, h=h, v=v: e.wait_ge(h, v))

    def emit(self, eng, fn, deps=(), ser=False):
        for d in deps:
            self.wait(eng, d)
        s = self.engsem[eng]
        h = self.sems[s]
        if eng in ("scalar", "gpsimd") or ser:
            cur = self.cnt[s]
            if cur > 0:
                self.streams[eng].append(lambda e, h=h, cur=cur: e.wait_ge(h, cur))
        self.cnt[s] += 1
        v = self.cnt[s]
        self.streams[eng].append(lambda e, fn=fn, h=h: fn(e).then_inc(h, 1))
        return (s, v)

    def dma(self, eng, fn, sem, deps=(), step=16):
        for d in deps:
            self.wait(eng, d)
        if sem not in self.sems:
            self.new_sem(sem)
        self.cnt[sem] += step
        v = self.cnt[sem]
        h = self.sems[sem]
        self.streams[eng].append(lambda e, fn=fn, h=h, step=step: fn(e).then_inc(h, step))
        return (sem, v)

    def barrier(self):
        snap = [(s, c) for s, c in self.cnt.items() if c > 0]
        for e in self.ENGINES:
            for s, c in snap:
                self.wait(e, (s, c))


def build_nc():
    nc = bass.Bass("TRN2", target_bir_lowering=False)
    es = ExitStack()
    P = Prog(nc, es)

    def _finish():
        with nc.Block() as block:
            for eng in Prog.ENGINES:
                stream = P.streams[eng]
                def runner(e, stream=stream):
                    for f in stream:
                        f(e)
                getattr(block, eng)(runner)
        return nc
    tap_kind = "ExternalOutput" if DEBUG_TAPS else "Internal"

    xs = nc.dram_tensor("xs", [T, D], F32, kind="ExternalInput").ap()
    Wq = nc.dram_tensor("Wq", [D, D], F32, kind="ExternalInput").ap()
    Wk = nc.dram_tensor("Wk", [D, D], F32, kind="ExternalInput").ap()
    Wv = nc.dram_tensor("Wv", [D, D], F32, kind="ExternalInput").ap()
    Wo = nc.dram_tensor("Wo", [D, D], F32, kind="ExternalInput").ap()
    bo = nc.dram_tensor("bo", [1, D], F32, kind="ExternalInput").ap()
    proj = nc.dram_tensor("proj", [M, DH], F32, kind="ExternalInput").ap()
    W1 = nc.dram_tensor("W1", [FF, D], F32, kind="ExternalInput").ap()
    b1 = nc.dram_tensor("b1", [FF], F32, kind="ExternalInput").ap()
    W2 = nc.dram_tensor("W2", [D, FF], F32, kind="ExternalInput").ap()
    b2 = nc.dram_tensor("b2", [1, D], F32, kind="ExternalInput").ap()
    ident_bf_in = nc.dram_tensor("ident_bf", [128, 128], BF16, kind="ExternalInput").ap()
    ident_f_in = nc.dram_tensor("ident_f", [128, 128], F32, kind="ExternalInput").ap()
    ones_bf_in = nc.dram_tensor("ones_bf", [128, 128], BF16, kind="ExternalInput").ap()
    ones_f_in = nc.dram_tensor("ones_f", [1, 128], F32, kind="ExternalInput").ap()
    onesER_in = nc.dram_tensor("onesER", [1, 128], F32, kind="ExternalInput").ap()
    onesc_f_in = nc.dram_tensor("onesc_f", [128, 2], F32, kind="ExternalInput").ap()
    bdiag_in = nc.dram_tensor("bdiag", [128, 8, 16], BF16, kind="ExternalInput").ap()
    cpool_in = nc.dram_tensor("cpool", [128, 8], F32, kind="ExternalInput").ap()

    out = nc.dram_tensor("out", [T, D], F32, kind="ExternalOutput").ap()

    xnt_s = nc.dram_tensor("xnt_s", [8, 128, T], BF16, kind=tap_kind).ap()
    k_s = nc.dram_tensor("k_s", [8, 128, T], BF16, kind=tap_kind).ap()
    v_s = nc.dram_tensor("v_s", [32, 128, 16 * 66], BF16, kind=tap_kind).ap()
    attn_s = nc.dram_tensor("attn_s", [32, 128, D], BF16, kind=tap_kind).ap()
    cc1i = nc.dram_tensor("cc1i", [1, 16], F32)
    cc1o = nc.dram_tensor("cc1o", [1, 16], F32)
    cc2i = nc.dram_tensor("cc2i", [3, 16, 128, 65], F32)
    cc2o = nc.dram_tensor("cc2o", [3, 16, 128, 65], F32).ap()
    ctxg_s = (nc.dram_tensor("ctxg_s", [3, 16, 128, 65], F32, kind="ExternalOutput").ap()
              if DEBUG_TAPS else None)
    dbg_s = (nc.dram_tensor("dbg_s", [3, 128, 512], F32, kind="ExternalOutput").ap()
             if DEBUG_TAPS else None)

    ident_bf = nc.alloc_sbuf_tensor("ident_bf_sb", [128, 128], BF16)
    ident_f = nc.alloc_sbuf_tensor("ident_f_sb", [128, 128], F32)
    ones_bf = nc.alloc_sbuf_tensor("ones_bf_sb", [128, 128], BF16)
    ones_f = nc.alloc_sbuf_tensor("ones_f_sb", [1, 128], F32)
    onesER = nc.alloc_sbuf_tensor("onesER_sb", [1, 128], F32)
    onesc_f = nc.alloc_sbuf_tensor("onesc_f_sb", [128, 2], F32)
    cpool = nc.alloc_sbuf_tensor("cpool_sb", [128, 8], F32)
    bdiag = nc.alloc_sbuf_tensor("bdiag_sb", [128, 8, 16], BF16)
    projT = nc.alloc_sbuf_tensor("projT_sb", [128, 272], BF16)
    maxacc = nc.alloc_sbuf_tensor("maxacc_sb", [128, 16], F32)
    tmp16 = nc.alloc_sbuf_tensor("tmp16_sb", [128, 16], F32)
    stab16 = nc.alloc_sbuf_tensor("stab16_sb", [16, 1], F32)
    stabrow = nc.alloc_sbuf_tensor("stabrow_sb", [1, 16], F32)
    stabB = nc.alloc_sbuf_tensor("stabB_sb", [128, 16], F32)
    b1_sb = nc.alloc_sbuf_tensor("b1_sb", [128, 32], F32)
    brow = nc.alloc_sbuf_tensor("brow_sb", [1, 1024], BF16)
    browf = nc.alloc_sbuf_tensor("browf_sb", [1, 2048], F32)
    thr_sb = nc.alloc_sbuf_tensor("thr_sb", [1, 2], F32)
    thrB = nc.alloc_sbuf_tensor("thrB_sb", [128, 2], F32)
    nthrB = nc.alloc_sbuf_tensor("nthrB_sb", [128, 2], F32)
    acc4 = nc.alloc_sbuf_tensor("acc4_sb", [128, 4], F32)
    acc1 = nc.alloc_sbuf_tensor("acc1_sb", [128, 1], F32)

    xs3 = xs.rearrange("(c p) d -> p c d", p=128)
    out3 = out.rearrange("(c p) d -> p c d", p=128)

    es_mid = ExitStack()
    diagk = es_mid.enter_context(nc.sbuf_tensor("diagk_sb", [128, 32, 16], F32))
    biask = es_mid.enter_context(nc.sbuf_tensor("biask_sb", [128, 32, 16], F32))
    statf = es_mid.enter_context(nc.sbuf_tensor("statf_sb", [128, 3, 16, 128], BF16))
    ctxl = es_mid.enter_context(nc.sbuf_tensor("ctxl_sb", [128, 3, 16, 65], F32))

    es_w = ExitStack()
    Wq_bf = es_w.enter_context(nc.sbuf_tensor("Wq_bf", [128, 8, 1024], BF16))
    Wk_bf = es_w.enter_context(nc.sbuf_tensor("Wk_bf", [128, 8, 1024], BF16))
    Wv_bf = es_w.enter_context(nc.sbuf_tensor("Wv_bf", [128, 8, 1024], BF16))
    Wo_bf = es_w.enter_context(nc.sbuf_tensor("Wo_bf", [128, 8, 1024], BF16))
    wstage = es_w.enter_context(nc.sbuf_tensor("wstage", [128, 2, 1024], F32))
    pj_f = es_w.enter_context(nc.sbuf_tensor("pj_f", [128, 3, 64], F32))
    pj_bf = es_w.enter_context(nc.sbuf_tensor("pj_bf", [128, 3, 64], BF16))

    es_p1 = ExitStack()
    PS_XNT = es_p1.enter_context(nc.psum_tensor("PS_XNT", [128, 1024], BF16))
    PS_K = [es_p1.enter_context(nc.psum_tensor(f"PS_K{i}", [128, 512], F32))
            for i in range(2)]
    PS_MIX = es_p1.enter_context(nc.psum_tensor("PS_MIX", [128, 512], F32))
    PS_VD = es_p1.enter_context(nc.psum_tensor("PS_VD", [128, 1024], F32))
    PS_DA = es_p1.enter_context(nc.psum_tensor("PS_DA", [128, 1024], F32))

    t_cpool = P.dma("sync", lambda e: e.dma_start(out=cpool[:], in_=cpool_in), "DW")
    for _i, _v in enumerate([0.0, 1.0, -0.5, LN_EPS, EPS, RATIO, -1.0, 2.0]):
        nc.const_aps.register(F32, _v, cpool[:, _i:_i + 1])
    P.wait("scalar", t_cpool)
    P.wait("vector", t_cpool)
    P.wait("gpsimd", t_cpool)
    t_const = P.dma("sync", lambda e: e.dma_start(out=ident_bf[:], in_=ident_bf_in), "DW")
    P.dma("sync", lambda e: e.dma_start(out=ident_f[:], in_=ident_f_in), "DW")
    P.dma("sync", lambda e: e.dma_start(out=ones_bf[:], in_=ones_bf_in), "DW")
    P.dma("sync", lambda e: e.dma_start(out=ones_f[:], in_=ones_f_in), "DW")
    P.dma("sync", lambda e: e.dma_start(out=onesER[:], in_=onesER_in), "DW")
    P.dma("sync", lambda e: e.dma_start(out=onesc_f[:], in_=onesc_f_in), "DW")
    P.dma("sync", lambda e: e.dma_start(out=bdiag[:], in_=bdiag_in), "DW")
    P.dma("sync", lambda e: e.dma_start(out=pj_f[0:128, 0, :], in_=proj[0:128, :]), "DW")
    P.dma("sync", lambda e: e.dma_start(out=pj_f[0:128, 1, :], in_=proj[128:256, :]), "DW")
    P.dma("sync", lambda e: e.dma_start(out=pj_f[0:10, 2, :], in_=proj[256:266, :]), "DW")
    P.dma("sync", lambda e: e.dma_start(out=browf[0:1, 0:1024], in_=bo), "DW")
    P.dma("sync", lambda e: e.dma_start(out=browf[0:1, 1024:2048], in_=b2), "DW")
    t_cdone = P.dma("sync", lambda e: e.dma_start(
        out=b1_sb[:], in_=b1.rearrange("(c p) -> p c", p=128),
        allow_slow_non_contiguous=True), "DW")

    tok_cast = []
    for wi, (wsrc, wdst) in enumerate([(Wq, Wq_bf), (Wk, Wk_bf), (Wv, Wv_bf), (Wo, Wo_bf)]):
        src3 = wsrc.rearrange("(c p) o -> c p o", p=128)
        for ci in range(8):
            slot = (wi * 8 + ci) % 2
            dep = [tok_cast[-2]] if len(tok_cast) >= 2 else []
            tl = P.dma("sync", lambda e, ci=ci, slot=slot, src3=src3:
                       e.dma_start(out=wstage[:, slot, :], in_=src3[ci]),
                       f"WS{slot}", deps=dep)
            tc = P.emit("vector", lambda e, ci=ci, slot=slot, wdst=wdst:
                        e.tensor_copy(wdst[:, ci, :], wstage[:, slot, :]), deps=[tl])
            tok_cast.append(tc)
    cast_q_done = tok_cast[7]
    cast_k_done = tok_cast[15]
    cast_v_done = tok_cast[23]
    cast_o_done = tok_cast[31]

    t1 = P.emit("vector", lambda e: e.tensor_scalar(pj_bf[:], pj_f[:], DN, None, ALU.mult),
                deps=[t_cdone])
    tt_pj = []
    for c, (m0, m1) in enumerate(MCH):
        rc = m1 - m0
        tt_pj.append(P.emit("tensor", lambda e, c=c, rc=rc: e.transpose(
            PS_XNT[0:64, c * 128:c * 128 + rc], pj_bf[0:rc, c, :], ident_bf[0:rc, 0:rc]),
            deps=[t1, t_const]))
    t2 = P.emit("vector", lambda e: e.tensor_copy(projT[0:64, 0:266], PS_XNT[0:64, 0:266]),
                deps=tt_pj)
    tok_projT = P.emit("vector", lambda e: e.tensor_copy(projT[64:128, 0:266],
                                                         projT[0:64, 0:266]), ser=True)
    t4 = P.emit("vector", lambda e: e.tensor_add(browf[0:1, 0:1024], browf[0:1, 0:1024],
                                                 browf[0:1, 1024:2048]), ser=True)
    tok_brow = P.emit("vector", lambda e: e.tensor_copy(brow[0:1, :], browf[0:1, 0:1024]), ser=True)
    tok_maxinit = P.emit("vector", lambda e: e.memset(maxacc[:], -3.0e38))

    es_a = ExitStack()
    xb = es_a.enter_context(nc.sbuf_tensor("xb", [128, 2, 4, 1024], F32))
    xn_bf = es_a.enter_context(nc.sbuf_tensor("xn_bf", [128, 4, 1024], BF16))
    sqs = es_a.enter_context(nc.sbuf_tensor("sqs", [128, 4, 1024], F32))
    stats = es_a.enter_context(nc.sbuf_tensor("stats", [128, 6, 4], F32))
    xnt_t = es_a.enter_context(nc.sbuf_tensor("xnt_t", [128, 8, 512], BF16))
    kt = es_a.enter_context(nc.sbuf_tensor("kt", [128, 8, 512], BF16))
    k2t = es_a.enter_context(nc.sbuf_tensor("k2t", [128, 2, 512], BF16))
    vt = es_a.enter_context(nc.sbuf_tensor("vt", [128, 4, 1056], BF16))
    d16 = es_a.enter_context(nc.sbuf_tensor("d16", [16, 512], F32))
    maxtile = es_a.enter_context(nc.sbuf_tensor("maxtile", [128, 16, 4], F32))

    tok_vones = P.emit("vector", lambda e: e.memset(
        vt[:].rearrange("p c (h x) -> p c h x", h=16)[:, :, :, 64:65], 1.0))

    last = {}
    tok_xnt_evac = t2
    for i in range(NT1):
        par = i % 2
        deps = []
        if ("xuse_s", i - 2) in last:
            deps += [last[("xuse_s", i - 2)], last[("xuse_v", i - 2)]]
        tx = P.dma("sync", lambda e, i=i, par=par: e.dma_start(
            out=xb[:, par, :, :], in_=xs3[:, 4 * i:4 * i + 4, :]), f"XL{par}", deps=deps)
        tmean = P.emit("vector", lambda e, par=par: e.reduce_sum(
            stats[:, 0, :], xb[:, par, :, :], axis=AX.X), deps=[tx])
        tsq = []
        for c in range(4):
            tsq.append(P.emit("scalar", lambda e, par=par, c=c: e.activation(
                sqs[:, c, :], xb[:, par, c, :], AF.Square,
                accum_out=stats[:, 1, c:c + 1]), deps=[tx]))
        P.emit("vector", lambda e: e.tensor_scalar(
            stats[:, 2, :], stats[:, 0, :], 1.0 / D, None, ALU.mult), ser=True)
        P.emit("vector", lambda e: e.tensor_mul(
            stats[:, 3, :], stats[:, 2, :], stats[:, 2, :]), ser=True)
        tev = P.emit("vector", lambda e: e.scalar_tensor_tensor(
            stats[:, 3, :], stats[:, 1, :], 1.0 / D, stats[:, 3, :],
            ALU.mult, ALU.subtract), deps=tsq, ser=True)
        tln = P.emit("scalar", lambda e: e.activation(
            stats[:, 4, :], stats[:, 3, :], AF.Ln, bias=LN_EPS), deps=[tev])
        trs = P.emit("scalar", lambda e: e.activation(
            stats[:, 4, :], stats[:, 4, :], AF.Exp, scale=-0.5))
        tnmean = P.emit("vector", lambda e: e.tensor_scalar(
            stats[:, 5, :], stats[:, 2, :], -1.0, None, ALU.mult), ser=True)
        txn = []
        for c in range(4):
            txn.append(P.emit("vector", lambda e, par=par, c=c: e.tensor_scalar(
                xn_bf[:, c, :], xb[:, par, c, :],
                stats[:, 5, c:c + 1], stats[:, 4, c:c + 1], ALU.add, ALU.mult),
                deps=[trs], ser=True))
        last[("xuse_s", i)] = tsq[-1]
        last[("xuse_v", i)] = txn[-1]

        reuse = [last[("xnt_w", i - 1)]] if ("xnt_w", i - 1) in last else []
        tev8 = []
        for dc in range(8):
            tts = []
            for c in range(4):
                tts.append(P.emit("tensor", lambda e, c=c, dc=dc: e.transpose(
                    PS_XNT[:, c * 128:(c + 1) * 128],
                    xn_bf[:, c, dc * 128:(dc + 1) * 128], ident_bf[:]),
                    deps=[txn[c], tok_xnt_evac, t_const]))
            tok_xnt_evac = P.emit("vector", lambda e, dc=dc: e.tensor_copy(
                xnt_t[:, dc, :], PS_XNT[:, 0:512]), deps=tts + reuse)
            tev8.append(tok_xnt_evac)
        txw = P.dma("sync", lambda e, i=i: e.dma_start(
            out=xnt_s[:, :, i * 512:(i + 1) * 512].rearrange("a p c -> p a c"),
            in_=xnt_t[:]), f"XNW{par}", deps=tev8)
        last[("xnt_w", i)] = txw

        tkevac = []
        tdiag_mm = []
        dprev = last.get(("dtdone", i - 1))
        for co in range(8):
            pk = co % 2
            dd0 = [tev8[7], cast_k_done]
            if co >= 2:
                dd0.append(tkevac[co - 2])
            elif ("kevac", i - 1, co) in last:
                dd0.append(last[("kevac", i - 1, co)])
            mm = None
            for ci in range(8):
                mm = P.emit("tensor", lambda e, co=co, ci=ci, pk=pk: e.matmul(
                    PS_K[pk][:, :], Wk_bf[:, ci, co * 128:(co + 1) * 128],
                    xnt_t[:, ci, :], start=(ci == 0), stop=(ci == 7)),
                    deps=(dd0 if ci == 0 else []))
            tke = P.emit("vector", lambda e, co=co, pk=pk: e.tensor_copy(
                kt[:, co, :], PS_K[pk][:, :]), deps=[mm])
            tkevac.append(tke)
            dd = []
            if co >= 2:
                dd.append(tdiag_mm[co - 2])
            tk2 = P.emit("vector", lambda e, co=co, pk=pk: e.tensor_mul(
                k2t[:, pk, :], kt[:, co, :], kt[:, co, :]), deps=dd)
            dd = [tk2]
            if co == 0 and dprev is not None:
                dd.append(dprev)
            tdiag_mm.append(P.emit("tensor", lambda e, co=co, pk=pk: e.matmul(
                PS_MIX[0:16, :], bdiag[:, co, :], k2t[:, pk, :],
                start=(co == 0), stop=(co == 7)), deps=dd))
        last[("kevac", i, 0)] = tkevac[6]
        last[("kevac", i, 1)] = tkevac[7]
        tkw = P.dma("sync", lambda e, i=i: e.dma_start(
            out=k_s[:, :, i * 512:(i + 1) * 512].rearrange("a p c -> p a c"),
            in_=kt[:]), f"KW{par}", deps=[tkevac[-1]])

        td16 = P.emit("vector", lambda e: e.tensor_copy(d16[:], PS_MIX[0:16, :]),
                      deps=[tdiag_mm[-1]])
        tprev = td16
        for c in range(4):
            ttr = P.emit("tensor", lambda e, c=c: e.transpose(
                PS_MIX[:, 0:16], d16[0:16, c * 128:(c + 1) * 128],
                ident_f[0:16, 0:16]), deps=[tprev])
            tprev = P.emit("vector", lambda e, i=i, c=c: e.tensor_copy(
                diagk[:, 4 * i + c, :], PS_MIX[:, 0:16]), deps=[ttr])
        last[("dtdone", i)] = tprev

        tvev = []
        for c in range(4):
            dd0 = [tev8[7], cast_v_done, tok_vones]
            if c == 0:
                if ("dred", True) in last:
                    dd0.append(last[("dred", True)])
                if ("vw", i - 1) in last:
                    dd0.append(last[("vw", i - 1)])
            else:
                dd0.append(tvev[-1])
            mm = None
            for ci in range(8):
                for s in range(2):
                    mm = P.emit("tensor", lambda e, c=c, ci=ci, s=s: e.matmul(
                        PS_VD[:, s * 512:(s + 1) * 512],
                        xnt_t[:, ci, c * 128:(c + 1) * 128],
                        Wv_bf[:, ci, s * 512:(s + 1) * 512],
                        start=(ci == 0), stop=(ci == 7)),
                        deps=(dd0 if (ci == 0 and s == 0) else []))
            tvev.append(P.emit("vector", lambda e, c=c: e.tensor_copy(
                vt[:, c, :].rearrange("p (h x) -> p h x", h=16)[:, :, 0:64],
                PS_VD[:].rearrange("p (h x) -> p h x", h=16)), deps=[mm]))
        tvw = P.dma("sync", lambda e, i=i: e.dma_start(
            out=v_s[4 * i:4 * i + 4, :, :].rearrange("a p c -> p a c"),
            in_=vt[:]), f"VW{par}", deps=[tvev[-1]])
        last[("vw", i)] = tvw

        for j in range(8):
            for c in range(4):
                use_vd = ((j * 4 + c) % 2 == 0)
                pd = PS_VD if use_vd else PS_DA
                dd = [tkevac[j], tok_projT]
                if use_vd:
                    dd.append(tvev[-1])
                if ("dred", use_vd) in last:
                    dd.append(last[("dred", use_vd)])
                P.emit("tensor", lambda e, j=j, c=c, pd=pd: e.matmul(
                    pd[:, 0:266], kt[0:64, j, c * 128:(c + 1) * 128],
                    projT[0:64, 0:266], start=True, stop=True,
                    tile_position=(0, 0)), deps=dd)
                mb = P.emit("tensor", lambda e, j=j, c=c, pd=pd: e.matmul(
                    pd[:, 512:778], kt[64:128, j, c * 128:(c + 1) * 128],
                    projT[64:128, 0:266], start=True, stop=True,
                    tile_position=(64, 0)))
                tr = P.emit("vector", lambda e, j=j, c=c, pd=pd: e.reduce_max(
                    maxtile[:, 2 * j:2 * j + 2, c],
                    pd[:].rearrange("p (a x) -> p a x", a=2)[:, :, 0:266],
                    axis=AX.X), deps=[mb, tok_maxinit])
                last[("dred", use_vd)] = tr
        P.emit("vector", lambda e: e.reduce_max(tmp16[:], maxtile[:], axis=AX.X), ser=True)
        tfold = P.emit("vector", lambda e: e.tensor_max(maxacc[:], maxacc[:], tmp16[:]), ser=True)
        last[("fold", i)] = tfold

    tms = P.emit("tensor", lambda e: e.transpose(
        PS_MIX[0:16, 0:128], maxacc[:], ident_f[:]),
        deps=[last[("fold", NT1 - 1)], last[("dtdone", NT1 - 1)]])
    trd = P.emit("vector", lambda e: e.reduce_max(stab16[:], PS_MIX[0:16, 0:128],
                                                  axis=AX.X), deps=[tms])
    tg1 = P.dma("gpsimd", lambda e: e.dma_start(
        out=cc1i.ap().rearrange("a m -> m a"), in_=stab16[:]), "GS", deps=[trd])
    tcc1 = P.dma("gpsimd", lambda e: e.collective_compute(
        "AllReduce", ALU.max, replica_groups=RG,
        ins=[cc1i.ap().opt()], outs=[cc1o.ap().opt()]), "CCS", deps=[tg1], step=1)
    tg2 = P.dma("gpsimd", lambda e: e.dma_start(out=stabrow[:], in_=cc1o.ap()), "GS",
                deps=[tcc1])
    tbc = P.emit("tensor", lambda e: e.matmul(
        PS_MIX[:, 0:16], ones_f[0:1, :], stabrow[0:1, :], start=True, stop=True),
        deps=[tg2, trd])
    tsb = P.emit("vector", lambda e: e.tensor_copy(stabB[:], PS_MIX[:, 0:16]), deps=[tbc], ser=True)
    tok_biask = P.emit("vector", lambda e: e.scalar_tensor_tensor(
        biask[:], diagk[:], -1.0,
        stabB[:].unsqueeze(1).broadcast_to([128, 32, 16]),
        ALU.mult, ALU.subtract), ser=True)

    if PHASE <= 1:
        return _finish()
    P.barrier()
    es_a.close()
    es_p1.close()

    es_b = ExitStack()
    kr = es_b.enter_context(nc.sbuf_tensor("kr", [128, 2, 4, 512], BF16))
    vr = es_b.enter_context(nc.sbuf_tensor("vr", [128, 2, 4, 528], BF16))
    ek = es_b.enter_context(nc.sbuf_tensor("ek", [128, 2, 2, 272], BF16))
    es_p2 = ExitStack()
    PS_D2 = [es_p2.enter_context(nc.psum_tensor(f"PS_D2{i}", [128, 1024], F32))
             for i in range(2)]
    PS_CTX = es_p2.enter_context(nc.psum_tensor("PS_CTX", [128, 2048], F32))
    ctx_h = PS_CTX[:].rearrange("p (h x) -> p h x", h=8)

    tok_ek_ones = P.emit("vector", lambda e: e.memset(ek[:, :, :, 266:267], 1.0))

    tok_ctx_evac = [None, None, None]
    for g in range(2):
        started_banks = set()
        evac_dep = tok_ctx_evac[2]
        for i in range(NT1):
            par = i % 2
            dd = []
            if ("kruse", g, i - 2) in last:
                dd.append(last[("kruse", g, i - 2)])
            elif g == 1 and ("kruse", 0, NT1 - 2 + i) in last:
                dd.append(last[("kruse", 0, NT1 - 2 + i)])
            tkr = P.dma("sync", lambda e, g=g, i=i, par=par: e.dma_start(
                out=kr[:, par, :, :],
                in_=k_s[4 * g:4 * g + 4, :, i * 512:(i + 1) * 512]
                .rearrange("a p c -> p a c")), f"KR{par}", deps=dd)
            dd = []
            if ("vruse", g, i - 2) in last:
                dd.append(last[("vruse", g, i - 2)])
            elif g == 1 and ("vruse", 0, NT1 - 2 + i) in last:
                dd.append(last[("vruse", 0, NT1 - 2 + i)])
            tvr = P.dma("sync", lambda e, g=g, i=i, par=par: e.dma_start(
                out=vr[:, par, :, :],
                in_=v_s[4 * i:4 * i + 4, :, 528 * g:528 * (g + 1)]
                .rearrange("a p c -> p a c")), f"VR{par}", deps=dd)
            for jl in range(4):
                for c in range(4):
                    sl = (jl * 4 + c) % 2
                    pd = PS_D2[sl]
                    dd = [tkr, tok_projT]
                    if ("eexp", sl) in last:
                        dd.append(last[("eexp", sl)])
                    P.emit("tensor", lambda e, jl=jl, c=c, pd=pd, par=par: e.matmul(
                        pd[:, 0:266], kr[0:64, par, jl, c * 128:(c + 1) * 128],
                        projT[0:64, 0:266], start=True, stop=True,
                        tile_position=(0, 0)), deps=dd)
                    mb = P.emit("tensor", lambda e, jl=jl, c=c, pd=pd, par=par: e.matmul(
                        pd[:, 512:778], kr[64:128, par, jl, c * 128:(c + 1) * 128],
                        projT[64:128, 0:266], start=True, stop=True,
                        tile_position=(64, 0)))
                    texp = []
                    for s in range(2):
                        h = 8 * g + 2 * jl + s
                        dd2 = [mb, tok_biask, tok_ek_ones]
                        if ("ekuse", sl) in last:
                            dd2.append(last[("ekuse", sl)])
                        texp.append(P.emit("scalar",
                                    lambda e, s=s, sl=sl, c=c, h=h, i=i, pd=pd:
                                    e.activation(
                                        ek[:, sl, s, 0:266],
                                        pd[:, 512 * s:512 * s + 266], AF.Exp,
                                        bias=biask[:, 4 * i + c, h:h + 1]), deps=dd2))
                    last[("eexp", sl)] = texp[-1]
                    mmlast = None
                    for s in range(2):
                        hh = 2 * jl + s
                        for ch, (m0, m1) in enumerate(MCH):
                            bank_key = hh // 2
                            first = bank_key not in started_banks
                            if first:
                                started_banks.add(bank_key)
                            dd3 = [texp[s], tvr]
                            if evac_dep is not None:
                                dd3.append(evac_dep)
                            lo = (hh // 2) * 512 + 256 * (hh % 2) + 68 * ch
                            rows = 128 if ch < 2 else 11
                            e0, e1 = (m0, m1 + 1) if ch == 2 else (m0, m1)
                            mmlast = P.emit("tensor",
                                lambda e, sl=sl, s=s, ch=ch, c=c, par=par,
                                       first=first, lo=lo, rows=rows, e0=e0, e1=e1,
                                       jl=jl: e.matmul(
                                    PS_CTX[0:rows, lo:lo + 65],
                                    ek[:, sl, s, e0:e1],
                                    vr[:, par, c, (2 * jl + s) * 66:(2 * jl + s) * 66 + 65],
                                    start=first, stop=False,
                                    skip_group_check=True), deps=dd3)
                    last[("ekuse", sl)] = mmlast
            last[("kruse", g, i)] = last[("eexp", (jl * 4 + 3) % 2)]
            last[("vruse", g, i)] = last[("ekuse", (jl * 4 + 3) % 2)]
        for ch in range(3):
            tok_ctx_evac[ch] = P.emit("vector", lambda e, g=g, ch=ch: e.tensor_copy(
                ctxl[:, ch, 8 * g:8 * g + 8, :],
                ctx_h[:, :, 68 * ch:68 * ch + 65]),
                deps=[last[("ekuse", 0)], last[("ekuse", 1)]])

    if DEBUG_TAPS:
        P.dma("sync", lambda e: e.dma_start(
            out=dbg_s[0], in_=diagk[:].rearrange("p a b -> p (a b)")), "DBG",
            deps=[tok_biask])
        P.dma("sync", lambda e: e.dma_start(
            out=dbg_s[1], in_=biask[:].rearrange("p a b -> p (a b)")), "DBG",
            deps=[tok_biask])
        P.dma("sync", lambda e: e.dma_start(
            out=dbg_s[2][0:1, 0:16], in_=stabrow[:]), "DBG", deps=[tok_biask])
        P.dma("sync", lambda e: e.dma_start(
            out=dbg_s[2][1:2, 0:16].rearrange("a b -> b a"), in_=stab16[:]), "DBG",
            deps=[tok_biask])
        P.dma("sync", lambda e: e.dma_start(
            out=dbg_s[2][2:3, 0:16], in_=maxacc[0:1, :]), "DBG", deps=[tok_biask])
        P.dma("sync", lambda e: e.dma_start(
            out=dbg_s[2][3:4, 0:16], in_=cc1i.ap()), "DBG", deps=[tok_biask])
        P.dma("sync", lambda e: e.dma_start(
            out=dbg_s[2][4:5, 0:16], in_=cc1o.ap()), "DBG", deps=[tok_biask])
    if PHASE <= 2:
        return _finish()
    tgw = P.dma("gpsimd", lambda e: e.dma_start(
        out=cc2i.ap().rearrange("a b p c -> p a b c"), in_=ctxl[:]), "GS",
        deps=tok_ctx_evac)
    tcc2 = P.dma("gpsimd", lambda e: e.collective_compute(
        "AllReduce", ALU.add, replica_groups=RG,
        ins=[cc2i.ap().opt()], outs=[cc2o.opt()]), "CCS", deps=[tgw], step=1)
    tgr = P.dma("gpsimd", lambda e: e.dma_start(
        out=ctxl[:], in_=cc2o.rearrange("a b p c -> p a b c")), "GS", deps=[tcc2])
    if DEBUG_TAPS:
        P.dma("gpsimd", lambda e: e.dma_start(out=ctxg_s, in_=cc2o), "GS", deps=[tcc2])

    P.barrier()
    es_b.close()
    es_p2.close()
    es_fx = ExitStack()
    PS_FIX = es_fx.enter_context(nc.psum_tensor("PS_FIX", [128, 512], F32))
    PS_CS = es_fx.enter_context(nc.psum_tensor("PS_CS", [1, 128], F32))
    vrows = es_fx.enter_context(nc.sbuf_tensor("vrows", [1, 16, 65], F32))
    csr_sb = es_fx.enter_context(nc.sbuf_tensor("csr_sb", [1, 16, 128], BF16))

    tvrows = P.dma("sync", lambda e: e.dma_start(
        out=vrows[0:1, :, :], in_=ctxl[10:11, 2, :, :]), "FXD", deps=[tgr])
    for h in range(H):
        dd = [tgr]
        if h >= 1:
            dd.append(last[("fixv", h - 1)])
        dd.append(tvrows)
        tvb = P.emit("tensor", lambda e, h=h: e.matmul(
            PS_FIX[:, 0:65], onesER[0:1, :], vrows[0:1, h, :],
            start=True, stop=True), deps=dd)
        tst = None
        for ch in range(3):
            P.emit("vector", lambda e, h=h, ch=ch: e.scalar_tensor_tensor(
                statf[:, ch, h, 0:65], ctxl[:, ch, h, :], RATIO, PS_FIX[:, 0:65],
                ALU.mult, ALU.add), deps=[tvb, tgr])
            tst = P.emit("vector", lambda e, h=h, ch=ch: e.tensor_copy(
                statf[:, ch, h, 65:128],
                statf[:, ch, h, 64:65].broadcast_to([128, 63]), ), ser=True)
        last[("fixv", h)] = tst
        dd2 = [tst]
        if h >= 1:
            dd2.append(last[("csrow", h - 1)])
        P.emit("tensor", lambda e, h=h: e.matmul(
            PS_CS[0:1, :], ones_bf[:, 0:1], statf[:, 0, h, :],
            start=True, stop=False), deps=dd2)
        P.emit("tensor", lambda e, h=h: e.matmul(
            PS_CS[0:1, :], ones_bf[:, 0:1], statf[:, 1, h, :],
            start=False, stop=False))
        tc2 = P.emit("tensor", lambda e, h=h: e.matmul(
            PS_CS[0:1, :], ones_bf[0:10, 0:1], statf[0:10, 2, h, :],
            start=False, stop=True))
        last[("csrow", h)] = P.emit("vector", lambda e, h=h: e.tensor_scalar(
            csr_sb[0:1, h, :], PS_CS[0:1, :], EPS, None, ALU.mult), deps=[tc2], ser=True)
    tok_statf = P.dma("sync", lambda e: e.dma_start(
        out=statf[10:11, 2, :, :], in_=csr_sb[0:1, :, :]), "FXD",
        deps=[last[("csrow", H - 1)]])

    if PHASE <= 3:
        return _finish()
    P.barrier()
    es_fx.close()
    es_c = ExitStack()
    xr = es_c.enter_context(nc.sbuf_tensor("xr", [128, 2, 8, 512], BF16))
    q_sb = es_c.enter_context(nc.sbuf_tensor("q_sb", [128, 8, 512], BF16))
    q2t = es_c.enter_context(nc.sbuf_tensor("q2t", [128, 2, 512], BF16))
    dq16 = es_c.enter_context(nc.sbuf_tensor("dq16", [16, 512], F32))
    diagq = es_c.enter_context(nc.sbuf_tensor("diagq", [128, 4, 16], F32))
    negq = es_c.enter_context(nc.sbuf_tensor("negq", [128, 4, 16], F32))
    biasq = es_c.enter_context(nc.sbuf_tensor("biasq", [128, 4, 16], F32))
    eq = es_c.enter_context(nc.sbuf_tensor("eq", [128, 2, 2, 272], BF16))
    qpT = es_c.enter_context(nc.sbuf_tensor("qpT", [128, 2, 3, 512], BF16))
    recp = es_c.enter_context(nc.sbuf_tensor("recp", [64, 512], F32))
    outall = es_c.enter_context(nc.sbuf_tensor("outall", [128, 8, 512], BF16))
    attn_t = es_c.enter_context(nc.sbuf_tensor("attn_t", [128, 4, 1024], BF16))
    es_p3 = ExitStack()
    PS_QD = es_p3.enter_context(nc.psum_tensor("PS_QD", [128, 1024], F32))
    PS_DSH = es_p3.enter_context(nc.psum_tensor("PS_DSH", [128, 1024], F32))
    PS_QPT = es_p3.enter_context(nc.psum_tensor("PS_QPT", [128, 3072], BF16))
    PS_OUT = es_p3.enter_context(nc.psum_tensor("PS_OUT", [128, 512], F32))

    tok_qpt_ones = None
    for _js in range(2):
        tok_qpt_ones = P.dma("sync", lambda e, _js=_js: e.dma_start(
            out=qpT[10:11, _js, 2, :].rearrange("p (a b) -> p a b", a=4),
            in_=ones_bf[0:1, :].rearrange("p (a b) -> p a b", a=1)
            .broadcast_to([1, 4, 128])), "QPO")

    for i in range(NT1):
        par = i % 2
        dd = []
        if ("xruse", i - 2) in last:
            dd.append(last[("xruse", i - 2)])
        txr = P.dma("sync", lambda e, i=i, par=par: e.dma_start(
            out=xr[:, par, :, :],
            in_=xnt_s[:, :, i * 512:(i + 1) * 512].rearrange("a p c -> p a c")),
            f"XR{par}", deps=dd)
        tqevac = []
        tdq_mm = []
        for co in range(8):
            pk = co % 2
            dd0 = [txr, cast_q_done]
            if co >= 2:
                dd0.append(tqevac[co - 2])
            elif ("qfree", i - 1) in last:
                dd0.append(last[("qfree", i - 1)])
            mm = None
            for ci in range(8):
                mm = P.emit("tensor", lambda e, co=co, ci=ci, pk=pk, par=par: e.matmul(
                    PS_QD[:, pk * 512:(pk + 1) * 512],
                    Wq_bf[:, ci, co * 128:(co + 1) * 128], xr[:, par, ci, :],
                    start=(ci == 0), stop=(ci == 7)), deps=(dd0 if ci == 0 else []))
            tqe = P.emit("vector", lambda e, co=co, pk=pk: e.tensor_copy(
                q_sb[:, co, :], PS_QD[:, pk * 512:(pk + 1) * 512]), deps=[mm])
            tqevac.append(tqe)
            dd = []
            if co >= 2:
                dd.append(tdq_mm[co - 2])
            tq2 = P.emit("vector", lambda e, co=co, pk=pk: e.tensor_mul(
                q2t[:, pk, :], q_sb[:, co, :], q_sb[:, co, :]), deps=dd)
            dd = [tq2]
            if co == 0 and ("outev",) in last:
                dd.append(last[("outev",)])
            tdq_mm.append(P.emit("tensor", lambda e, co=co, pk=pk: e.matmul(
                PS_OUT[0:16, :], bdiag[:, co, :], q2t[:, pk, :],
                start=(co == 0), stop=(co == 7)), deps=dd))
        tdq16 = P.emit("vector", lambda e: e.tensor_copy(dq16[:], PS_OUT[0:16, :]),
                       deps=[tdq_mm[-1]])
        tprev = tdq16
        for c in range(4):
            ttr = P.emit("tensor", lambda e, c=c: e.transpose(
                PS_OUT[:, 0:16], dq16[0:16, c * 128:(c + 1) * 128],
                ident_f[0:16, 0:16]), deps=[tprev])
            tprev = P.emit("vector", lambda e, c=c: e.tensor_copy(
                diagq[:, c, :], PS_OUT[:, 0:16]), deps=[ttr])
        tok_diagq = tprev

        for j in range(8):
            js = j % 2
            for c in range(4):
                use_qd = ((j * 4 + c) % 2 == 0)
                sl = (j * 4 + c) % 2
                pd = PS_QD if use_qd else PS_DSH
                dd = [tqevac[j], tok_projT, tqevac[-1]]
                if ("qdred", use_qd) in last:
                    dd.append(last[("qdred", use_qd)])
                P.emit("tensor", lambda e, j=j, c=c, pd=pd: e.matmul(
                    pd[:, 0:266], q_sb[0:64, j, c * 128:(c + 1) * 128],
                    projT[0:64, 0:266], start=True, stop=True,
                    tile_position=(0, 0)), deps=dd)
                mb = P.emit("tensor", lambda e, j=j, c=c, pd=pd: e.matmul(
                    pd[:, 512:778], q_sb[64:128, j, c * 128:(c + 1) * 128],
                    projT[64:128, 0:266], start=True, stop=True,
                    tile_position=(64, 0)))
                P.emit("vector", lambda e, j=j, c=c, pd=pd: e.reduce_max(
                    negq[:, c, 2 * j:2 * j + 2],
                    pd[:].rearrange("p (a x) -> p a x", a=2)[:, :, 0:266],
                    axis=AX.X, negate=True), deps=[mb, tok_diagq])
                tbq = P.emit("vector", lambda e, j=j, c=c: e.scalar_tensor_tensor(
                    biasq[:, c, 2 * j:2 * j + 2], diagq[:, c, 2 * j:2 * j + 2], -1.0,
                    negq[:, c, 2 * j:2 * j + 2], ALU.mult, ALU.add), ser=True)
                texp = []
                for s in range(2):
                    dd2 = [tbq]
                    if ("equse", sl) in last:
                        dd2.append(last[("equse", sl)])
                    texp.append(P.emit("scalar",
                                lambda e, s=s, sl=sl, c=c, j=j, pd=pd: e.activation(
                                    eq[:, sl, s, 0:266], pd[:, 512 * s:512 * s + 266],
                                    AF.Exp, bias=biasq[:, c, 2 * j + s:2 * j + s + 1]),
                                deps=dd2))
                last[("qdred", use_qd)] = texp[-1]
                for s in range(2):
                    for mkc, (m0, m1) in enumerate(MCH):
                        rc = m1 - m0
                        reg = (s * 3 + mkc) * 512 + c * 128
                        dd3 = [texp[s]]
                        if ("qptevall",) in last:
                            dd3.append(last[("qptevall",)])
                        tq = P.emit("tensor", lambda e, s=s, mkc=mkc, m0=m0, rc=rc,
                                    reg=reg, sl=sl, j=j: e.transpose(
                                        PS_QPT[0:rc, reg:reg + 128],
                                        eq[:, sl, s, m0:m0 + rc], ident_bf[:]),
                                    deps=dd3)
                        last[("equse", sl)] = tq
                        last[("qpt_w", s, mkc)] = tq
            for s in range(2):
                h = 2 * j + s
                tevs = []
                dd4 = [last[("qpt_w", s, 0)], last[("qpt_w", s, 1)], tok_qpt_ones]
                if ("qpT_r", js) in last:
                    dd4.append(last[("qpT_r", js)])
                tevs.append(P.emit("vector", lambda e, s=s, js=js:
                            e.tensor_copy(qpT[0:128, js, 0:2, :],
                                          PS_QPT[0:128, :]
                                          .rearrange("p (a x) -> p a x", a=6)
                                          [:, 3 * s:3 * s + 2, :]),
                            deps=dd4))
                dd4b = [last[("qpt_w", s, 2)], tok_qpt_ones]
                if ("qpT_r", js) in last:
                    dd4b.append(last[("qpT_r", js)])
                tevs.append(P.emit("vector", lambda e, s=s, js=js:
                            e.tensor_copy(qpT[0:10, js, 2, :],
                                          PS_QPT[0:10, (s * 3 + 2) * 512:
                                                 (s * 3 + 2) * 512 + 512]),
                            deps=dd4b))
                last[("qptevall",)] = tevs[-1]
                dd5 = tevs + [tok_statf, tok_diagq]
                if ("outev",) in last:
                    dd5.append(last[("outev",)])
                mo = None
                for mkc in range(3):
                    kk = 128 if mkc < 2 else 11
                    mo = P.emit("tensor", lambda e, h=h, mkc=mkc, js=js, kk=kk: e.matmul(
                        PS_OUT[:, :], statf[0:kk, mkc, h, :], qpT[0:kk, js, mkc, :],
                        start=(mkc == 0), stop=(mkc == 2)),
                        deps=(dd5 if mkc == 0 else []))
                last[("qpT_r", js)] = mo
                trl = P.emit("scalar", lambda e: e.activation(
                    recp[:], PS_OUT[64:128, :], AF.Ln), deps=[mo])
                t_exp = P.emit("scalar", lambda e: e.activation(
                    recp[:], recp[:], AF.Exp, scale=-1.0), deps=[trl])
                tmu = P.emit("vector", lambda e, h=h: e.tensor_mul(
                    outall[(h % 2) * 64:(h % 2) * 64 + 64, h // 2, :],
                    PS_OUT[0:64, :], recp[:]), deps=[t_exp, mo])
                last[("outev",)] = tmu
        tatt = []
        for tc in range(4):
            dd6 = [last[("outev",)], cast_o_done, last[("qdred", False)]]
            if tatt:
                dd6.append(tatt[-1])
            if tc == 0 and ("attnw", i - 1) in last:
                dd6.append(last[("attnw", i - 1)])
            mm = None
            for co in range(8):
                for s in range(2):
                    mm = P.emit("tensor", lambda e, tc=tc, co=co, s=s: e.matmul(
                        PS_DSH[:, s * 512:(s + 1) * 512],
                        outall[:, co, tc * 128:(tc + 1) * 128],
                        Wo_bf[:, co, s * 512:(s + 1) * 512],
                        start=(co == 0), stop=(co == 7)),
                        deps=(dd6 if (co == 0 and s == 0) else []))
            tatt.append(P.emit("vector", lambda e, tc=tc: e.tensor_copy(
                attn_t[:, tc, :], PS_DSH[:, :]), deps=[mm]))
        last[("qdred", False)] = tatt[-1]
        taw = P.dma("sync", lambda e, i=i: e.dma_start(
            out=attn_s[4 * i:4 * i + 4, :, :].rearrange("a p c -> p a c"),
            in_=attn_t[:]), f"AW{par}",
            deps=[tatt[-1]])
        last[("attnw", i)] = taw
        last[("xruse", i)] = last[("qdred", True)]
        last[("qfree", i)] = last[("qdred", True)]

    if PHASE <= 4:
        return _finish()
    P.barrier()
    es_c.close()
    es_p3.close()
    es_w.close()
    es_mid.close()

    es_d = ExitStack()
    W1T = es_d.enter_context(nc.sbuf_tensor("W1T", [128, 8, 4096], BF16))
    W2T = es_d.enter_context(nc.sbuf_tensor("W2T", [128, 32, 1024], BF16))
    es_d2 = ExitStack()
    stage = es_d2.enter_context(nc.sbuf_tensor("stage", [128, 2, 4, 1024], F32))
    pmask = es_d2.enter_context(nc.sbuf_tensor("pmask", [128, 4, 1024], BF16))
    nmask = es_d2.enter_context(nc.sbuf_tensor("nmask", [128, 4, 1024], BF16))
    tern = es_d2.enter_context(nc.sbuf_tensor("tern", [128, 4, 1024], BF16))
    es_p4 = ExitStack()
    PS_T = [es_p4.enter_context(nc.psum_tensor(f"PS_T{i}", [128, 1024], BF16))
            for i in range(2)]
    PS_P = es_p4.enter_context(nc.psum_tensor("PS_P", [128, 2], F32))

    W1v = W1.rearrange("(a c p) d -> a p c d", p=128, c=4)
    W2v = W2.rearrange("(a p) (c d) -> a p c d", p=128, c=4)

    for wsel, wv in enumerate([W1v, W2v]):
        a = wsel * 2
        tinit = P.emit("vector", lambda e, a=a: e.memset(acc4[:, a:a + 1], 0.0))
        for pc in range(8):
            par = pc % 2
            dd = []
            if ("scuse", wsel, pc - 2) in last:
                dd.append(last[("scuse", wsel, pc - 2)])
            elif wsel == 1 and ("scuse", 0, 6 + pc) in last:
                dd.append(last[("scuse", 0, 6 + pc)])
            tld = P.dma("sync", lambda e, wv=wv, pc=pc, par=par: e.dma_start(
                out=stage[:, par, :, :], in_=wv[pc]), f"PL{par}", deps=dd)
            P.emit("vector", lambda e, par=par: e.reduce_sum(
                acc1[:, 0:1], stage[:, par, :, :].rearrange("p c d -> p (c d)"),
                axis=AX.X, apply_absolute_value=True), deps=[tld, tinit], ser=True)
            tacc = P.emit("vector", lambda e, a=a: e.tensor_add(
                acc4[:, a:a + 1], acc4[:, a:a + 1], acc1[:, 0:1]), ser=True)
            last[("scuse", wsel, pc)] = tacc
    tthr = []
    for wsel in range(2):
        a = wsel * 2
        dd = [last[("scuse", wsel, 7)]]
        if tthr:
            dd.append(tthr[-1])
        tp = P.emit("tensor", lambda e, a=a, wsel=wsel: e.matmul(
            PS_P[0:1, wsel:wsel + 1], onesc_f[:, 0:1], acc4[:, a:a + 1],
            start=True, stop=True), deps=dd)
        P.emit("vector", lambda e, wsel=wsel: e.tensor_scalar(
            thr_sb[0:1, wsel:wsel + 1], PS_P[0:1, wsel:wsel + 1],
            1.0 / (FF * D), None, ALU.mult), deps=[tp], ser=True)
        t6 = P.emit("vector", lambda e, wsel=wsel: e.tensor_scalar(
            thr_sb[0:1, wsel:wsel + 1], thr_sb[0:1, wsel:wsel + 1],
            1e-8, 0.5, ALU.max, ALU.mult), ser=True)
        tthr.append(t6)
    tbb = P.emit("tensor", lambda e: e.matmul(
        PS_P[:, 0:2], ones_f[0:1, :], thr_sb[0:1, 0:2], start=True, stop=True),
        deps=tthr)
    tthB = P.emit("vector", lambda e: e.tensor_copy(thrB[:], PS_P[:, 0:2]), deps=[tbb], ser=True)
    tnth = P.emit("vector", lambda e: e.tensor_scalar(
        nthrB[:, 0:2], thrB[:, 0:2], -1.0, None, ALU.mult), ser=True)

    for wsel, wv in enumerate([W1v, W2v]):
        for pc in range(8):
            par = pc % 2
            dd = []
            if ("tnuse", wsel, pc - 2) in last:
                dd.append(last[("tnuse", wsel, pc - 2)])
            elif wsel == 1 and ("tnuse", 0, 6 + pc) in last:
                dd.append(last[("tnuse", 0, 6 + pc)])
            elif wsel == 0 and ("scuse", 1, 6 + pc) in last:
                dd.append(last[("scuse", 1, 6 + pc)])
            tld = P.dma("sync", lambda e, wv=wv, pc=pc, par=par: e.dma_start(
                out=stage[:, par, :, :], in_=wv[pc]), f"PL{par}", deps=dd)
            ddp = [tld, tthB, tnth]
            if ("terns",) in last:
                ddp.append(last[("terns",)])
            P.emit("vector", lambda e, par=par, wsel=wsel: e.tensor_scalar(
                pmask[:], stage[:, par, :, :], thrB[:, wsel:wsel + 1], None,
                ALU.is_gt), deps=ddp, ser=True)
            tnm2 = P.emit("vector", lambda e, par=par, wsel=wsel: e.tensor_scalar(
                nmask[:], stage[:, par, :, :], nthrB[:, wsel:wsel + 1], None,
                ALU.is_lt), ser=True)
            last[("tnuse", wsel, pc)] = tnm2
            dd2 = [tnm2]
            if ("ternuse",) in last:
                dd2.append(last[("ternuse",)])
            tts2 = P.emit("vector", lambda e: e.tensor_sub(tern[:], pmask[:], nmask[:]),
                          deps=dd2, ser=True)
            last[("terns",)] = tts2
            if wsel == 0:
                for dc in range(8):
                    pt = PS_T[dc % 2]
                    tts3 = []
                    for c in range(4):
                        dd3 = [tts2]
                        if ("ptev", dc % 2) in last:
                            dd3.append(last[("ptev", dc % 2)])
                        tts3.append(P.emit("tensor", lambda e, c=c, dc=dc, pt=pt:
                                    e.transpose(pt[:, c * 128:(c + 1) * 128],
                                                tern[:, c, dc * 128:(dc + 1) * 128],
                                                ident_bf[:]), deps=dd3))
                    tev = P.emit("vector", lambda e, dc=dc, pc=pc, pt=pt: e.tensor_copy(
                        W1T[:, dc, pc * 512:(pc + 1) * 512], pt[:, 0:512]), deps=tts3)
                    last[("ptev", dc % 2)] = tev
                    last[("ternuse",)] = tev
            else:
                for fq in range(4):
                    pt = PS_T[fq % 2]
                    tts3 = []
                    for fo in range(8):
                        ffc = fq * 8 + fo
                        dd3 = [tts2]
                        if ("ptev", fq % 2) in last:
                            dd3.append(last[("ptev", fq % 2)])
                        tts3.append(P.emit("tensor", lambda e, ffc=ffc, fo=fo, pt=pt:
                                    e.transpose(pt[:, fo * 128:(fo + 1) * 128],
                                                tern[:, ffc // 8, (ffc % 8) * 128:
                                                     (ffc % 8) * 128 + 128],
                                                ident_bf[:]), deps=dd3))
                    tev = P.emit("vector", lambda e, fq=fq, pc=pc, pt=pt: e.tensor_copy(
                        W2T[:, fq * 8:fq * 8 + 8, pc * 128:(pc + 1) * 128],
                        pt[:].rearrange("p (a b) -> p a b", a=8)), deps=tts3)
                    last[("ptev", fq % 2)] = tev
                    last[("ternuse",)] = tev

    if PHASE <= 5:
        return _finish()
    P.barrier()
    es_p4.close()
    es_d2.close()
    es_e = ExitStack()
    xr2 = es_e.enter_context(nc.sbuf_tensor("xr2", [128, 2, 8, 256], BF16))
    at2 = es_e.enter_context(nc.sbuf_tensor("at2", [128, 2, 2, 1024], BF16))
    x2 = es_e.enter_context(nc.sbuf_tensor("x2", [128, 2, 1024], F32))
    h_sb = es_e.enter_context(nc.sbuf_tensor("h_sb", [128, 32, 256], BF16))
    af = es_e.enter_context(nc.sbuf_tensor("af", [128, 1024], F32))
    outb = es_e.enter_context(nc.sbuf_tensor("outb", [128, 2, 2048], F32))
    es_p5 = ExitStack()
    PS_Hh = [es_p5.enter_context(nc.psum_tensor(f"PS_H{i}", [128, 256], F32))
             for i in range(2)]
    PS_O = es_p5.enter_context(nc.psum_tensor("PS_O", [128, 2048], F32))

    n_out_dma = 0
    for i in range(NT2):
        par = i % 2
        dd = []
        if ("x2use", i - 2) in last:
            dd.append(last[("x2use", i - 2)])
        txr2 = P.dma("sync", lambda e, i=i, par=par: e.dma_start(
            out=xr2[:, par, :, :],
            in_=xnt_s[:, :, i * 256:(i + 1) * 256].rearrange("a p c -> p a c")),
            f"X2L{par}", deps=dd)
        tat2 = P.dma("sync", lambda e, i=i, par=par: e.dma_start(
            out=at2[:, par, :, :],
            in_=attn_s[2 * i:2 * i + 2, :, :].rearrange("a p c -> p a c")),
            f"A2L{par}", deps=dd)
        ddx = list(dd)
        if ("x2s", i - 1) in last:
            ddx.append(last[("x2s", i - 1)])
        tx2 = P.dma("sync", lambda e, i=i: e.dma_start(
            out=x2[:, :, :], in_=xs3[:, 2 * i:2 * i + 2, :]), "X2S", deps=ddx)
        tgel = []
        for ffc in range(32):
            ph = PS_Hh[ffc % 2]
            dd0 = [txr2]
            if ffc >= 2:
                dd0.append(tgel[ffc - 2])
            elif ("gel", i - 1, ffc) in last:
                dd0.append(last[("gel", i - 1, ffc)])
            mm = None
            for ci in range(8):
                mm = P.emit("tensor", lambda e, ffc=ffc, ci=ci, ph=ph, par=par: e.matmul(
                    ph[:, :], W1T[:, ci, ffc * 128:(ffc + 1) * 128],
                    xr2[:, par, ci, :], start=(ci == 0), stop=(ci == 7)),
                    deps=(dd0 if ci == 0 else []))
            tg = P.emit("scalar", lambda e, ffc=ffc, ph=ph: e.activation(
                h_sb[:, ffc, :], ph[:, :], AF.Gelu, bias=b1_sb[:, ffc:ffc + 1]),
                deps=[mm, t_cdone])
            tgel.append(tg)
            last[("gel", i, ffc)] = tg
        mo = None
        for ffc in range(32):
            for tch in range(2):
                for s in range(2):
                    dd0 = [tgel[ffc]]
                    if ffc == 0 and tch == 0 and s == 0 and ("oev", i - 1) in last:
                        dd0.append(last[("oev", i - 1)])
                    mo = P.emit("tensor", lambda e, ffc=ffc, tch=tch, s=s: e.matmul(
                        PS_O[:, tch * 1024 + s * 512:tch * 1024 + (s + 1) * 512],
                        h_sb[:, ffc, tch * 128:(tch + 1) * 128],
                        W2T[:, ffc, s * 512:(s + 1) * 512],
                        start=(ffc == 0), stop=False,
                        skip_group_check=True), deps=dd0)
        tbias = []
        for tch in range(2):
            for s in range(2):
                tbias.append(P.emit("tensor", lambda e, tch=tch, s=s: e.matmul(
                    PS_O[:, tch * 1024 + s * 512:tch * 1024 + (s + 1) * 512],
                    ones_bf[0:1, 0:128], brow[0:1, s * 512:(s + 1) * 512],
                    start=False, stop=True, skip_group_check=True),
                    deps=[tok_brow]))
        tev2 = None
        for tch in range(2):
            P.emit("vector", lambda e, tch=tch, par=par: e.tensor_copy(
                af[:, :], at2[:, par, tch, :]), deps=[tat2, tbias[-1]])
            P.emit("vector", lambda e, tch=tch, par=par: e.tensor_add(
                outb[:, par, tch * 1024:(tch + 1) * 1024],
                PS_O[:, tch * 1024:(tch + 1) * 1024], x2[:, tch, :]), deps=[tx2])
            tev2 = P.emit("vector", lambda e, tch=tch, par=par: e.tensor_add(
                outb[:, par, tch * 1024:(tch + 1) * 1024],
                outb[:, par, tch * 1024:(tch + 1) * 1024], af[:, :]))
        last[("oev", i)] = tev2
        last[("x2use", i)] = tev2
        last[("x2s", i)] = tev2
        P.dma("sync", lambda e, i=i, par=par: e.dma_start(
            out=out3[:, 2 * i:2 * i + 2, :],
            in_=outb[:, par, :].rearrange("p (c d) -> p c d", c=2)), "DO", deps=[tev2])
        n_out_dma += 1

    P.wait("sync", ("DO", 16 * n_out_dma))

    with nc.Block() as block:
        for eng in Prog.ENGINES:
            stream = P.streams[eng]
            def runner(e, stream=stream):
                for f in stream:
                    f(e)
            getattr(block, eng)(runner)
    return nc


_NC_CACHE = None


def _consts():
    ident = np.eye(128)
    c = {
        "ident_bf": ident.astype(ml_dtypes.bfloat16),
        "ident_f": ident.astype(np.float32),
        "ones_bf": np.ones((128, 128), ml_dtypes.bfloat16),
        "ones_f": np.ones((1, 128), np.float32),
        "onesER": np.full((1, 128), EPS * RATIO, np.float32),
        "onesc_f": np.ones((128, 2), np.float32),
        "cpool": np.tile(np.array([[0.0, 1.0, -0.5, LN_EPS, EPS, RATIO, -1.0, 2.0]], np.float32), (128, 1)),
    }
    bd = np.zeros((128, 8, 16), np.float32)
    for p in range(128):
        for co in range(8):
            bd[p, co, 2 * co + p // 64] = 1.0 / 16.0
    c["bdiag"] = bd.astype(ml_dtypes.bfloat16)
    return c


def kernel(L0_attn_x, L0_attn_Wq, L0_attn_Wk, L0_attn_Wv, L0_attn_Wo, L0_attn_bo,
           L0_attn_proj, L0_ffn_W1, L0_ffn_b1, L0_ffn_W2, L0_ffn_b2):
    global _NC_CACHE
    x = np.asarray(L0_attn_x, np.float32)
    shared = {
        "Wq": np.ascontiguousarray(L0_attn_Wq, np.float32),
        "Wk": np.ascontiguousarray(L0_attn_Wk, np.float32),
        "Wv": np.ascontiguousarray(L0_attn_Wv, np.float32),
        "Wo": np.ascontiguousarray(L0_attn_Wo, np.float32),
        "bo": np.ascontiguousarray(L0_attn_bo, np.float32).reshape(1, D),
        "proj": np.ascontiguousarray(L0_attn_proj, np.float32),
        "W1": np.ascontiguousarray(L0_ffn_W1, np.float32),
        "b1": np.ascontiguousarray(L0_ffn_b1, np.float32).reshape(FF),
        "W2": np.ascontiguousarray(L0_ffn_W2, np.float32),
        "b2": np.ascontiguousarray(L0_ffn_b2, np.float32).reshape(1, D),
        **_consts(),
    }
    in_maps = []
    for c in range(8):
        b, half = c // 2, c % 2
        xs_c = np.ascontiguousarray(x[b, half * T:(half + 1) * T, :], np.float32)
        in_maps.append({"xs": xs_c, **shared})

    if _NC_CACHE is None:
        _NC_CACHE = build_nc()
    res = run_bass_kernel_spmd(_NC_CACHE, in_maps, core_ids=list(range(8)))
    outp = np.empty((4, 8192, D), np.float32)
    for c in range(8):
        b, half = c // 2, c % 2
        outp[b, half * T:(half + 1) * T, :] = res.results[c]["out"]
    return outp


# version 07
# speedup vs baseline: 1.4891x; 1.0285x over previous
import numpy as np
import ml_dtypes
from contextlib import ExitStack

import concourse.bass as bass
import concourse.mybir as mybir
from concourse.bass_utils import run_bass_kernel_spmd

F32 = mybir.dt.float32
BF16 = mybir.dt.bfloat16
AF = mybir.ActivationFunctionType
ALU = mybir.AluOpType
AX = mybir.AxisListType

D = 1024
H = 16
DH = 64
M = 266
T = 4096
EPS = 1e-4
RATIO = float(M) ** -0.5
DN = float(DH) ** -0.25
LN_EPS = 1e-5
FF = 4096

NT1 = 8
NT2 = 16
MCH = [(0, 128), (128, 256), (256, 266)]

DEBUG_TAPS = False
PHASE = 99
RG = [[0, 1], [2, 3], [4, 5], [6, 7]]


class Prog:
    ENGINES = ("sync", "gpsimd", "scalar", "vector", "tensor")

    def __init__(self, nc, es):
        self.nc = nc
        self.es = es
        self.streams = {e: [] for e in self.ENGINES}
        self.sems = {}
        self.cnt = {}
        self.guar = {}
        self.engsem = {"scalar": "Cs", "vector": "Cv", "tensor": "Ct", "gpsimd": "Cg"}
        for s in self.engsem.values():
            self.new_sem(s)

    def new_sem(self, name):
        self.sems[name] = self.es.enter_context(self.nc.semaphore(name))
        self.cnt[name] = 0

    def wait(self, eng, dep):
        if dep is None:
            return
        s, v = dep
        if v <= 0:
            return
        if self.engsem.get(eng) == s:
            return
        key = (eng, s)
        if self.guar.get(key, 0) >= v:
            return
        self.guar[key] = v
        h = self.sems[s]
        self.streams[eng].append(lambda e, h=h, v=v: e.wait_ge(h, v))

    def emit(self, eng, fn, deps=(), ser=False):
        for d in deps:
            self.wait(eng, d)
        s = self.engsem[eng]
        h = self.sems[s]
        if eng in ("scalar", "gpsimd") or ser:
            cur = self.cnt[s]
            if cur > 0:
                self.streams[eng].append(lambda e, h=h, cur=cur: e.wait_ge(h, cur))
        self.cnt[s] += 1
        v = self.cnt[s]
        self.streams[eng].append(lambda e, fn=fn, h=h: fn(e).then_inc(h, 1))
        return (s, v)

    def dma(self, eng, fn, sem, deps=(), step=16):
        for d in deps:
            self.wait(eng, d)
        if sem not in self.sems:
            self.new_sem(sem)
        self.cnt[sem] += step
        v = self.cnt[sem]
        h = self.sems[sem]
        self.streams[eng].append(lambda e, fn=fn, h=h, step=step: fn(e).then_inc(h, step))
        return (sem, v)

    def barrier(self):
        snap = [(s, c) for s, c in self.cnt.items() if c > 0]
        for e in self.ENGINES:
            for s, c in snap:
                self.wait(e, (s, c))


def build_nc():
    nc = bass.Bass("TRN2", target_bir_lowering=False)
    es = ExitStack()
    P = Prog(nc, es)

    def _finish():
        with nc.Block() as block:
            for eng in Prog.ENGINES:
                stream = P.streams[eng]
                def runner(e, stream=stream):
                    for f in stream:
                        f(e)
                getattr(block, eng)(runner)
        return nc
    tap_kind = "ExternalOutput" if DEBUG_TAPS else "Internal"

    xs = nc.dram_tensor("xs", [T, D], F32, kind="ExternalInput").ap()
    Wq = nc.dram_tensor("Wq", [D, D], F32, kind="ExternalInput").ap()
    Wk = nc.dram_tensor("Wk", [D, D], F32, kind="ExternalInput").ap()
    Wv = nc.dram_tensor("Wv", [D, D], F32, kind="ExternalInput").ap()
    Wo = nc.dram_tensor("Wo", [D, D], F32, kind="ExternalInput").ap()
    bo = nc.dram_tensor("bo", [1, D], F32, kind="ExternalInput").ap()
    proj = nc.dram_tensor("proj", [M, DH], F32, kind="ExternalInput").ap()
    W1 = nc.dram_tensor("W1", [FF, D], F32, kind="ExternalInput").ap()
    b1 = nc.dram_tensor("b1", [FF], F32, kind="ExternalInput").ap()
    W2 = nc.dram_tensor("W2", [D, FF], F32, kind="ExternalInput").ap()
    b2 = nc.dram_tensor("b2", [1, D], F32, kind="ExternalInput").ap()
    ident_bf_in = nc.dram_tensor("ident_bf", [128, 128], BF16, kind="ExternalInput").ap()
    ident_f_in = nc.dram_tensor("ident_f", [128, 128], F32, kind="ExternalInput").ap()
    ones_bf_in = nc.dram_tensor("ones_bf", [128, 128], BF16, kind="ExternalInput").ap()
    ones_f_in = nc.dram_tensor("ones_f", [1, 128], F32, kind="ExternalInput").ap()
    onesER_in = nc.dram_tensor("onesER", [1, 128], F32, kind="ExternalInput").ap()
    onesc_f_in = nc.dram_tensor("onesc_f", [128, 2], F32, kind="ExternalInput").ap()
    bdiag_in = nc.dram_tensor("bdiag", [128, 8, 16], BF16, kind="ExternalInput").ap()
    cpool_in = nc.dram_tensor("cpool", [128, 8], F32, kind="ExternalInput").ap()

    out = nc.dram_tensor("out", [T, D], F32, kind="ExternalOutput").ap()

    xnt_s = nc.dram_tensor("xnt_s", [8, 128, T], BF16, kind=tap_kind).ap()
    k_s = nc.dram_tensor("k_s", [8, 128, T], BF16, kind=tap_kind).ap()
    v_s = nc.dram_tensor("v_s", [32, 128, 16 * 66], BF16, kind=tap_kind).ap()
    attn_s = nc.dram_tensor("attn_s", [32, 128, D], BF16, kind=tap_kind).ap()
    cc1i = nc.dram_tensor("cc1i", [1, 16], F32)
    cc1o = nc.dram_tensor("cc1o", [1, 16], F32)
    cc2i = nc.dram_tensor("cc2i", [3, 16, 128, 65], F32)
    cc2o = nc.dram_tensor("cc2o", [3, 16, 128, 65], F32).ap()
    ctxg_s = (nc.dram_tensor("ctxg_s", [3, 16, 128, 65], F32, kind="ExternalOutput").ap()
              if DEBUG_TAPS else None)
    dbg_s = (nc.dram_tensor("dbg_s", [3, 128, 512], F32, kind="ExternalOutput").ap()
             if DEBUG_TAPS else None)

    ident_bf = nc.alloc_sbuf_tensor("ident_bf_sb", [128, 128], BF16)
    ident_f = nc.alloc_sbuf_tensor("ident_f_sb", [128, 128], F32)
    ones_bf = nc.alloc_sbuf_tensor("ones_bf_sb", [128, 128], BF16)
    ones_f = nc.alloc_sbuf_tensor("ones_f_sb", [1, 128], F32)
    onesER = nc.alloc_sbuf_tensor("onesER_sb", [1, 128], F32)
    onesc_f = nc.alloc_sbuf_tensor("onesc_f_sb", [128, 2], F32)
    cpool = nc.alloc_sbuf_tensor("cpool_sb", [128, 8], F32)
    bdiag = nc.alloc_sbuf_tensor("bdiag_sb", [128, 8, 16], BF16)
    projT = nc.alloc_sbuf_tensor("projT_sb", [128, 272], BF16)
    maxacc = nc.alloc_sbuf_tensor("maxacc_sb", [128, 16], F32)
    tmp16 = nc.alloc_sbuf_tensor("tmp16_sb", [128, 16], F32)
    stab16 = nc.alloc_sbuf_tensor("stab16_sb", [16, 1], F32)
    stabrow = nc.alloc_sbuf_tensor("stabrow_sb", [1, 16], F32)
    stabB = nc.alloc_sbuf_tensor("stabB_sb", [128, 16], F32)
    b1_sb = nc.alloc_sbuf_tensor("b1_sb", [128, 32], F32)
    brow = nc.alloc_sbuf_tensor("brow_sb", [1, 1024], BF16)
    browf = nc.alloc_sbuf_tensor("browf_sb", [1, 2048], F32)
    thr_sb = nc.alloc_sbuf_tensor("thr_sb", [1, 2], F32)
    thrB = nc.alloc_sbuf_tensor("thrB_sb", [128, 2], F32)
    nthrB = nc.alloc_sbuf_tensor("nthrB_sb", [128, 2], F32)
    acc4 = nc.alloc_sbuf_tensor("acc4_sb", [128, 4], F32)
    acc1 = nc.alloc_sbuf_tensor("acc1_sb", [128, 1], F32)

    xs3 = xs.rearrange("(c p) d -> p c d", p=128)
    out3 = out.rearrange("(c p) d -> p c d", p=128)

    es_mid = ExitStack()
    diagk = es_mid.enter_context(nc.sbuf_tensor("diagk_sb", [128, 32, 16], F32))
    biask = es_mid.enter_context(nc.sbuf_tensor("biask_sb", [128, 32, 16], F32))
    statf = es_mid.enter_context(nc.sbuf_tensor("statf_sb", [128, 3, 16, 128], BF16))
    ctxl = es_mid.enter_context(nc.sbuf_tensor("ctxl_sb", [128, 3, 16, 65], F32))

    es_w = ExitStack()
    Wq_bf = es_w.enter_context(nc.sbuf_tensor("Wq_bf", [128, 8, 1024], BF16))
    Wk_bf = es_w.enter_context(nc.sbuf_tensor("Wk_bf", [128, 8, 1024], BF16))
    Wv_bf = es_w.enter_context(nc.sbuf_tensor("Wv_bf", [128, 8, 1024], BF16))
    Wo_bf = es_w.enter_context(nc.sbuf_tensor("Wo_bf", [128, 8, 1024], BF16))
    wstage = es_w.enter_context(nc.sbuf_tensor("wstage", [128, 2, 1024], F32))
    pj_f = es_w.enter_context(nc.sbuf_tensor("pj_f", [128, 3, 64], F32))
    pj_bf = es_w.enter_context(nc.sbuf_tensor("pj_bf", [128, 3, 64], BF16))

    es_p1 = ExitStack()
    PS_XNT = es_p1.enter_context(nc.psum_tensor("PS_XNT", [128, 1024], BF16))
    PS_K = [es_p1.enter_context(nc.psum_tensor(f"PS_K{i}", [128, 512], F32))
            for i in range(2)]
    PS_MIX = es_p1.enter_context(nc.psum_tensor("PS_MIX", [128, 512], F32))
    PS_VD = es_p1.enter_context(nc.psum_tensor("PS_VD", [128, 1024], F32))
    PS_DA = es_p1.enter_context(nc.psum_tensor("PS_DA", [128, 1024], F32))

    t_cpool = P.dma("sync", lambda e: e.dma_start(out=cpool[:], in_=cpool_in), "DW")
    for _i, _v in enumerate([0.0, 1.0, -0.5, LN_EPS, EPS, RATIO, -1.0, 2.0]):
        nc.const_aps.register(F32, _v, cpool[:, _i:_i + 1])
    P.wait("scalar", t_cpool)
    P.wait("vector", t_cpool)
    P.wait("gpsimd", t_cpool)
    t_const = P.dma("sync", lambda e: e.dma_start(out=ident_bf[:], in_=ident_bf_in), "DW")
    P.dma("sync", lambda e: e.dma_start(out=ident_f[:], in_=ident_f_in), "DW")
    P.dma("sync", lambda e: e.dma_start(out=ones_bf[:], in_=ones_bf_in), "DW")
    P.dma("sync", lambda e: e.dma_start(out=ones_f[:], in_=ones_f_in), "DW")
    P.dma("sync", lambda e: e.dma_start(out=onesER[:], in_=onesER_in), "DW")
    P.dma("sync", lambda e: e.dma_start(out=onesc_f[:], in_=onesc_f_in), "DW")
    P.dma("sync", lambda e: e.dma_start(out=bdiag[:], in_=bdiag_in), "DW")
    P.dma("sync", lambda e: e.dma_start(out=pj_f[0:128, 0, :], in_=proj[0:128, :]), "DW")
    P.dma("sync", lambda e: e.dma_start(out=pj_f[0:128, 1, :], in_=proj[128:256, :]), "DW")
    P.dma("sync", lambda e: e.dma_start(out=pj_f[0:10, 2, :], in_=proj[256:266, :]), "DW")
    P.dma("sync", lambda e: e.dma_start(out=browf[0:1, 0:1024], in_=bo), "DW")
    P.dma("sync", lambda e: e.dma_start(out=browf[0:1, 1024:2048], in_=b2), "DW")
    t_cdone = P.dma("sync", lambda e: e.dma_start(
        out=b1_sb[:], in_=b1.rearrange("(c p) -> p c", p=128),
        allow_slow_non_contiguous=True), "DW")

    tok_cast = []
    for wi, (wsrc, wdst) in enumerate([(Wq, Wq_bf), (Wk, Wk_bf), (Wv, Wv_bf), (Wo, Wo_bf)]):
        src3 = wsrc.rearrange("(c p) o -> c p o", p=128)
        for ci in range(8):
            slot = (wi * 8 + ci) % 2
            dep = [tok_cast[-2]] if len(tok_cast) >= 2 else []
            tl = P.dma("sync", lambda e, ci=ci, slot=slot, src3=src3:
                       e.dma_start(out=wstage[:, slot, :], in_=src3[ci]),
                       f"WS{slot}", deps=dep)
            tc = P.emit("vector", lambda e, ci=ci, slot=slot, wdst=wdst:
                        e.tensor_copy(wdst[:, ci, :], wstage[:, slot, :]), deps=[tl])
            tok_cast.append(tc)
    cast_q_done = tok_cast[7]
    cast_k_done = tok_cast[15]
    cast_v_done = tok_cast[23]
    cast_o_done = tok_cast[31]

    t1 = P.emit("vector", lambda e: e.tensor_scalar(pj_bf[:], pj_f[:], DN, None, ALU.mult),
                deps=[t_cdone])
    tt_pj = []
    for c, (m0, m1) in enumerate(MCH):
        rc = m1 - m0
        tt_pj.append(P.emit("tensor", lambda e, c=c, rc=rc: e.transpose(
            PS_XNT[0:64, c * 128:c * 128 + rc], pj_bf[0:rc, c, :], ident_bf[0:rc, 0:rc]),
            deps=[t1, t_const]))
    t2 = P.emit("vector", lambda e: e.tensor_copy(projT[0:64, 0:266], PS_XNT[0:64, 0:266]),
                deps=tt_pj)
    tok_projT = P.emit("vector", lambda e: e.tensor_copy(projT[64:128, 0:266],
                                                         projT[0:64, 0:266]), ser=True)
    t4 = P.emit("vector", lambda e: e.tensor_add(browf[0:1, 0:1024], browf[0:1, 0:1024],
                                                 browf[0:1, 1024:2048]), ser=True)
    tok_brow = P.emit("vector", lambda e: e.tensor_copy(brow[0:1, :], browf[0:1, 0:1024]), ser=True)
    tok_maxinit = P.emit("vector", lambda e: e.memset(maxacc[:], -3.0e38))

    es_a = ExitStack()
    xb = es_a.enter_context(nc.sbuf_tensor("xb", [128, 2, 4, 1024], F32))
    xn_bf = es_a.enter_context(nc.sbuf_tensor("xn_bf", [128, 4, 1024], BF16))
    sqs = es_a.enter_context(nc.sbuf_tensor("sqs", [128, 4, 1024], F32))
    stats = es_a.enter_context(nc.sbuf_tensor("stats", [128, 6, 4], F32))
    xnt_t = es_a.enter_context(nc.sbuf_tensor("xnt_t", [128, 8, 512], BF16))
    kt = es_a.enter_context(nc.sbuf_tensor("kt", [128, 8, 512], BF16))
    k2t = es_a.enter_context(nc.sbuf_tensor("k2t", [128, 2, 512], BF16))
    vt = es_a.enter_context(nc.sbuf_tensor("vt", [128, 4, 1056], BF16))
    d16 = es_a.enter_context(nc.sbuf_tensor("d16", [16, 512], F32))
    maxtile = es_a.enter_context(nc.sbuf_tensor("maxtile", [128, 16, 4], F32))

    tok_vones = P.emit("vector", lambda e: e.memset(
        vt[:].rearrange("p c (h x) -> p c h x", h=16)[:, :, :, 64:65], 1.0))

    last = {}
    tok_xnt_evac = t2
    for i in range(NT1):
        par = i % 2
        deps = []
        if ("xuse_s", i - 2) in last:
            deps += [last[("xuse_s", i - 2)], last[("xuse_v", i - 2)]]
        tx = P.dma("sync", lambda e, i=i, par=par: e.dma_start(
            out=xb[:, par, :, :], in_=xs3[:, 4 * i:4 * i + 4, :]), f"XL{par}", deps=deps)
        tmean = P.emit("vector", lambda e, par=par: e.reduce_sum(
            stats[:, 0, :], xb[:, par, :, :], axis=AX.X), deps=[tx])
        tsq = []
        for c in range(4):
            tsq.append(P.emit("scalar", lambda e, par=par, c=c: e.activation(
                sqs[:, c, :], xb[:, par, c, :], AF.Square,
                accum_out=stats[:, 1, c:c + 1]), deps=[tx]))
        P.emit("vector", lambda e: e.tensor_scalar(
            stats[:, 2, :], stats[:, 0, :], 1.0 / D, None, ALU.mult), ser=True)
        P.emit("vector", lambda e: e.tensor_mul(
            stats[:, 3, :], stats[:, 2, :], stats[:, 2, :]), ser=True)
        tev = P.emit("vector", lambda e: e.scalar_tensor_tensor(
            stats[:, 3, :], stats[:, 1, :], 1.0 / D, stats[:, 3, :],
            ALU.mult, ALU.subtract), deps=tsq, ser=True)
        tln = P.emit("scalar", lambda e: e.activation(
            stats[:, 4, :], stats[:, 3, :], AF.Ln, bias=LN_EPS), deps=[tev])
        trs = P.emit("scalar", lambda e: e.activation(
            stats[:, 4, :], stats[:, 4, :], AF.Exp, scale=-0.5))
        tnmean = P.emit("vector", lambda e: e.tensor_scalar(
            stats[:, 5, :], stats[:, 2, :], -1.0, None, ALU.mult), ser=True)
        txn = []
        for c in range(4):
            txn.append(P.emit("vector", lambda e, par=par, c=c: e.tensor_scalar(
                xn_bf[:, c, :], xb[:, par, c, :],
                stats[:, 5, c:c + 1], stats[:, 4, c:c + 1], ALU.add, ALU.mult),
                deps=[trs], ser=True))
        last[("xuse_s", i)] = tsq[-1]
        last[("xuse_v", i)] = txn[-1]

        reuse = [last[("xnt_w", i - 1)]] if ("xnt_w", i - 1) in last else []
        tev8 = []
        for dc in range(8):
            tts = []
            for c in range(4):
                tts.append(P.emit("tensor", lambda e, c=c, dc=dc: e.transpose(
                    PS_XNT[:, c * 128:(c + 1) * 128],
                    xn_bf[:, c, dc * 128:(dc + 1) * 128], ident_bf[:]),
                    deps=[txn[c], tok_xnt_evac, t_const]))
            tok_xnt_evac = P.emit("vector", lambda e, dc=dc: e.tensor_copy(
                xnt_t[:, dc, :], PS_XNT[:, 0:512]), deps=tts + reuse)
            tev8.append(tok_xnt_evac)
        txw = P.dma("sync", lambda e, i=i: e.dma_start(
            out=xnt_s[:, :, i * 512:(i + 1) * 512].rearrange("a p c -> p a c"),
            in_=xnt_t[:]), f"XNW{par}", deps=tev8)
        last[("xnt_w", i)] = txw

        tkevac = []
        tdiag_mm = []
        dprev = last.get(("dtdone", i - 1))
        for co in range(8):
            pk = co % 2
            dd0 = [tev8[7], cast_k_done]
            if co >= 2:
                dd0.append(tkevac[co - 2])
            elif ("kevac", i - 1, co) in last:
                dd0.append(last[("kevac", i - 1, co)])
            mm = None
            for ci in range(8):
                mm = P.emit("tensor", lambda e, co=co, ci=ci, pk=pk: e.matmul(
                    PS_K[pk][:, :], Wk_bf[:, ci, co * 128:(co + 1) * 128],
                    xnt_t[:, ci, :], start=(ci == 0), stop=(ci == 7)),
                    deps=(dd0 if ci == 0 else []))
            tke = P.emit("vector", lambda e, co=co, pk=pk: e.tensor_copy(
                kt[:, co, :], PS_K[pk][:, :]), deps=[mm])
            tkevac.append(tke)
            dd = []
            if co >= 2:
                dd.append(tdiag_mm[co - 2])
            tk2 = P.emit("vector", lambda e, co=co, pk=pk: e.tensor_mul(
                k2t[:, pk, :], kt[:, co, :], kt[:, co, :]), deps=dd)
            dd = [tk2]
            if co == 0 and dprev is not None:
                dd.append(dprev)
            tdiag_mm.append(P.emit("tensor", lambda e, co=co, pk=pk: e.matmul(
                PS_MIX[0:16, :], bdiag[:, co, :], k2t[:, pk, :],
                start=(co == 0), stop=(co == 7)), deps=dd))
        last[("kevac", i, 0)] = tkevac[6]
        last[("kevac", i, 1)] = tkevac[7]
        tkw = P.dma("sync", lambda e, i=i: e.dma_start(
            out=k_s[:, :, i * 512:(i + 1) * 512].rearrange("a p c -> p a c"),
            in_=kt[:]), f"KW{par}", deps=[tkevac[-1]])

        td16 = P.emit("vector", lambda e: e.tensor_copy(d16[:], PS_MIX[0:16, :]),
                      deps=[tdiag_mm[-1]])
        tprev = td16
        for c in range(4):
            ttr = P.emit("tensor", lambda e, c=c: e.transpose(
                PS_MIX[:, 0:16], d16[0:16, c * 128:(c + 1) * 128],
                ident_f[0:16, 0:16]), deps=[tprev])
            tprev = P.emit("vector", lambda e, i=i, c=c: e.tensor_copy(
                diagk[:, 4 * i + c, :], PS_MIX[:, 0:16]), deps=[ttr])
        last[("dtdone", i)] = tprev

        tvev = []
        for c in range(4):
            dd0 = [tev8[7], cast_v_done, tok_vones]
            if c == 0:
                if ("dred", True) in last:
                    dd0.append(last[("dred", True)])
                if ("vw", i - 1) in last:
                    dd0.append(last[("vw", i - 1)])
            else:
                dd0.append(tvev[-1])
            mm = None
            for ci in range(8):
                for s in range(2):
                    mm = P.emit("tensor", lambda e, c=c, ci=ci, s=s: e.matmul(
                        PS_VD[:, s * 512:(s + 1) * 512],
                        xnt_t[:, ci, c * 128:(c + 1) * 128],
                        Wv_bf[:, ci, s * 512:(s + 1) * 512],
                        start=(ci == 0), stop=(ci == 7)),
                        deps=(dd0 if (ci == 0 and s == 0) else []))
            tvev.append(P.emit("vector", lambda e, c=c: e.tensor_copy(
                vt[:, c, :].rearrange("p (h x) -> p h x", h=16)[:, :, 0:64],
                PS_VD[:].rearrange("p (h x) -> p h x", h=16)), deps=[mm]))
        tvw = P.dma("sync", lambda e, i=i: e.dma_start(
            out=v_s[4 * i:4 * i + 4, :, :].rearrange("a p c -> p a c"),
            in_=vt[:]), f"VW{par}", deps=[tvev[-1]])
        last[("vw", i)] = tvw

        for j in range(8):
            for c in range(4):
                use_vd = ((j * 4 + c) % 2 == 0)
                pd = PS_VD if use_vd else PS_DA
                dd = [tkevac[j], tok_projT]
                if use_vd:
                    dd.append(tvev[-1])
                if ("dred", use_vd) in last:
                    dd.append(last[("dred", use_vd)])
                P.emit("tensor", lambda e, j=j, c=c, pd=pd: e.matmul(
                    pd[:, 0:266], kt[0:64, j, c * 128:(c + 1) * 128],
                    projT[0:64, 0:266], start=True, stop=True,
                    tile_position=(0, 0)), deps=dd)
                mb = P.emit("tensor", lambda e, j=j, c=c, pd=pd: e.matmul(
                    pd[:, 512:778], kt[64:128, j, c * 128:(c + 1) * 128],
                    projT[64:128, 0:266], start=True, stop=True,
                    tile_position=(64, 0)))
                tr = P.emit("vector", lambda e, j=j, c=c, pd=pd: e.reduce_max(
                    maxtile[:, 2 * j:2 * j + 2, c],
                    pd[:].rearrange("p (a x) -> p a x", a=2)[:, :, 0:266],
                    axis=AX.X), deps=[mb, tok_maxinit])
                last[("dred", use_vd)] = tr
        P.emit("vector", lambda e: e.reduce_max(tmp16[:], maxtile[:], axis=AX.X), ser=True)
        tfold = P.emit("vector", lambda e: e.tensor_max(maxacc[:], maxacc[:], tmp16[:]), ser=True)
        last[("fold", i)] = tfold

    tms = P.emit("tensor", lambda e: e.transpose(
        PS_MIX[0:16, 0:128], maxacc[:], ident_f[:]),
        deps=[last[("fold", NT1 - 1)], last[("dtdone", NT1 - 1)]])
    trd = P.emit("vector", lambda e: e.reduce_max(stab16[:], PS_MIX[0:16, 0:128],
                                                  axis=AX.X), deps=[tms])
    tg1 = P.dma("gpsimd", lambda e: e.dma_start(
        out=cc1i.ap().rearrange("a m -> m a"), in_=stab16[:]), "GS", deps=[trd])
    tcc1 = P.dma("gpsimd", lambda e: e.collective_compute(
        "AllReduce", ALU.max, replica_groups=RG,
        ins=[cc1i.ap().opt()], outs=[cc1o.ap().opt()]), "CCS", deps=[tg1], step=1)
    tg2 = P.dma("gpsimd", lambda e: e.dma_start(out=stabrow[:], in_=cc1o.ap()), "GS",
                deps=[tcc1])
    tbc = P.emit("tensor", lambda e: e.matmul(
        PS_MIX[:, 0:16], ones_f[0:1, :], stabrow[0:1, :], start=True, stop=True),
        deps=[tg2, trd])
    tsb = P.emit("vector", lambda e: e.tensor_copy(stabB[:], PS_MIX[:, 0:16]), deps=[tbc], ser=True)
    tok_biask = P.emit("vector", lambda e: e.scalar_tensor_tensor(
        biask[:], diagk[:], -1.0,
        stabB[:].unsqueeze(1).broadcast_to([128, 32, 16]),
        ALU.mult, ALU.subtract), ser=True)

    if PHASE <= 1:
        return _finish()
    P.barrier()
    es_a.close()
    es_p1.close()

    es_b = ExitStack()
    kr = es_b.enter_context(nc.sbuf_tensor("kr", [128, 2, 4, 512], BF16))
    vr = es_b.enter_context(nc.sbuf_tensor("vr", [128, 2, 4, 528], BF16))
    ek = es_b.enter_context(nc.sbuf_tensor("ek", [128, 2, 2, 272], BF16))
    es_p2 = ExitStack()
    PS_D2 = [es_p2.enter_context(nc.psum_tensor(f"PS_D2{i}", [128, 1024], F32))
             for i in range(2)]
    PS_CTX = es_p2.enter_context(nc.psum_tensor("PS_CTX", [128, 2048], F32))
    ctx_h = PS_CTX[:].rearrange("p (h x) -> p h x", h=8)

    tok_ek_ones = P.emit("vector", lambda e: e.memset(ek[:, :, :, 266:267], 1.0))

    stage_t = es_b.enter_context(nc.sbuf_tensor("stage_t", [128, 2, 4, 1024], F32))
    W1v = W1.rearrange("(a c p) d -> a p c d", p=128, c=4)
    W2v = W2.rearrange("(a p) (c d) -> a p c d", p=128, c=4)

    def emit_thr_unit(k):
        wsel, pc = k // 8, k % 8
        wv = W1v if wsel == 0 else W2v
        a = wsel * 2
        par = pc % 2
        if pc == 0:
            last[("thrinit", wsel)] = P.emit("vector", lambda e, a=a: e.memset(
                acc4[:, a:a + 1], 0.0), ser=True)
        dd = []
        if ("scuse", wsel, pc - 2) in last:
            dd.append(last[("scuse", wsel, pc - 2)])
        elif wsel == 1 and ("scuse", 0, 6 + pc) in last:
            dd.append(last[("scuse", 0, 6 + pc)])
        tld = P.dma("gpsimd", lambda e, wv=wv, pc=pc, par=par: e.dma_start(
            out=stage_t[:, par, :, :], in_=wv[pc]), f"PL{par}", deps=dd)
        P.emit("vector", lambda e, par=par: e.reduce_sum(
            acc1[:, 0:1], stage_t[:, par, :, :].rearrange("p c d -> p (c d)"),
            axis=AX.X, apply_absolute_value=True),
            deps=[tld, last[("thrinit", wsel)]], ser=True)
        tacc = P.emit("vector", lambda e, a=a: e.tensor_add(
            acc4[:, a:a + 1], acc4[:, a:a + 1], acc1[:, 0:1]), ser=True)
        last[("scuse", wsel, pc)] = tacc

    tok_ctx_evac = [None, None, None]
    for g in range(2):
        started_banks = set()
        evac_dep = tok_ctx_evac[2]
        for i in range(NT1):
            par = i % 2
            dd = []
            if ("kruse", g, i - 2) in last:
                dd.append(last[("kruse", g, i - 2)])
            elif g == 1 and ("kruse", 0, NT1 - 2 + i) in last:
                dd.append(last[("kruse", 0, NT1 - 2 + i)])
            tkr = P.dma("sync", lambda e, g=g, i=i, par=par: e.dma_start(
                out=kr[:, par, :, :],
                in_=k_s[4 * g:4 * g + 4, :, i * 512:(i + 1) * 512]
                .rearrange("a p c -> p a c")), f"KR{par}", deps=dd)
            dd = []
            if ("vruse", g, i - 2) in last:
                dd.append(last[("vruse", g, i - 2)])
            elif g == 1 and ("vruse", 0, NT1 - 2 + i) in last:
                dd.append(last[("vruse", 0, NT1 - 2 + i)])
            tvr = P.dma("sync", lambda e, g=g, i=i, par=par: e.dma_start(
                out=vr[:, par, :, :],
                in_=v_s[4 * i:4 * i + 4, :, 528 * g:528 * (g + 1)]
                .rearrange("a p c -> p a c")), f"VR{par}", deps=dd)
            emit_thr_unit(g * NT1 + i)
            for jl in range(4):
                for c in range(4):
                    sl = (jl * 4 + c) % 2
                    pd = PS_D2[sl]
                    dd = [tkr, tok_projT]
                    if ("eexp", sl) in last:
                        dd.append(last[("eexp", sl)])
                    P.emit("tensor", lambda e, jl=jl, c=c, pd=pd, par=par: e.matmul(
                        pd[:, 0:266], kr[0:64, par, jl, c * 128:(c + 1) * 128],
                        projT[0:64, 0:266], start=True, stop=True,
                        tile_position=(0, 0)), deps=dd)
                    mb = P.emit("tensor", lambda e, jl=jl, c=c, pd=pd, par=par: e.matmul(
                        pd[:, 512:778], kr[64:128, par, jl, c * 128:(c + 1) * 128],
                        projT[64:128, 0:266], start=True, stop=True,
                        tile_position=(64, 0)))
                    texp = []
                    for s in range(2):
                        h = 8 * g + 2 * jl + s
                        dd2 = [mb, tok_biask, tok_ek_ones]
                        if ("ekuse", sl) in last:
                            dd2.append(last[("ekuse", sl)])
                        texp.append(P.emit("scalar",
                                    lambda e, s=s, sl=sl, c=c, h=h, i=i, pd=pd:
                                    e.activation(
                                        ek[:, sl, s, 0:266],
                                        pd[:, 512 * s:512 * s + 266], AF.Exp,
                                        bias=biask[:, 4 * i + c, h:h + 1]), deps=dd2))
                    last[("eexp", sl)] = texp[-1]
                    mmlast = None
                    for s in range(2):
                        hh = 2 * jl + s
                        for ch, (m0, m1) in enumerate(MCH):
                            bank_key = hh // 2
                            first = bank_key not in started_banks
                            if first:
                                started_banks.add(bank_key)
                            dd3 = [texp[s], tvr]
                            if evac_dep is not None:
                                dd3.append(evac_dep)
                            lo = (hh // 2) * 512 + 256 * (hh % 2) + 68 * ch
                            rows = 128 if ch < 2 else 11
                            e0, e1 = (m0, m1 + 1) if ch == 2 else (m0, m1)
                            mmlast = P.emit("tensor",
                                lambda e, sl=sl, s=s, ch=ch, c=c, par=par,
                                       first=first, lo=lo, rows=rows, e0=e0, e1=e1,
                                       jl=jl: e.matmul(
                                    PS_CTX[0:rows, lo:lo + 65],
                                    ek[:, sl, s, e0:e1],
                                    vr[:, par, c, (2 * jl + s) * 66:(2 * jl + s) * 66 + 65],
                                    start=first, stop=False,
                                    skip_group_check=True), deps=dd3)
                    last[("ekuse", sl)] = mmlast
            last[("kruse", g, i)] = last[("eexp", (jl * 4 + 3) % 2)]
            last[("vruse", g, i)] = last[("ekuse", (jl * 4 + 3) % 2)]
        for ch in range(3):
            tok_ctx_evac[ch] = P.emit("vector", lambda e, g=g, ch=ch: e.tensor_copy(
                ctxl[:, ch, 8 * g:8 * g + 8, :],
                ctx_h[:, :, 68 * ch:68 * ch + 65]),
                deps=[last[("ekuse", 0)], last[("ekuse", 1)]])

    if DEBUG_TAPS:
        P.dma("sync", lambda e: e.dma_start(
            out=dbg_s[0], in_=diagk[:].rearrange("p a b -> p (a b)")), "DBG",
            deps=[tok_biask])
        P.dma("sync", lambda e: e.dma_start(
            out=dbg_s[1], in_=biask[:].rearrange("p a b -> p (a b)")), "DBG",
            deps=[tok_biask])
        P.dma("sync", lambda e: e.dma_start(
            out=dbg_s[2][0:1, 0:16], in_=stabrow[:]), "DBG", deps=[tok_biask])
        P.dma("sync", lambda e: e.dma_start(
            out=dbg_s[2][1:2, 0:16].rearrange("a b -> b a"), in_=stab16[:]), "DBG",
            deps=[tok_biask])
        P.dma("sync", lambda e: e.dma_start(
            out=dbg_s[2][2:3, 0:16], in_=maxacc[0:1, :]), "DBG", deps=[tok_biask])
        P.dma("sync", lambda e: e.dma_start(
            out=dbg_s[2][3:4, 0:16], in_=cc1i.ap()), "DBG", deps=[tok_biask])
        P.dma("sync", lambda e: e.dma_start(
            out=dbg_s[2][4:5, 0:16], in_=cc1o.ap()), "DBG", deps=[tok_biask])
    if PHASE <= 2:
        return _finish()
    tgw = P.dma("gpsimd", lambda e: e.dma_start(
        out=cc2i.ap().rearrange("a b p c -> p a b c"), in_=ctxl[:]), "GS",
        deps=tok_ctx_evac)
    tcc2 = P.dma("gpsimd", lambda e: e.collective_compute(
        "AllReduce", ALU.add, replica_groups=RG,
        ins=[cc2i.ap().opt()], outs=[cc2o.opt()]), "CCS", deps=[tgw], step=1)
    tgr = P.dma("gpsimd", lambda e: e.dma_start(
        out=ctxl[:], in_=cc2o.rearrange("a b p c -> p a b c")), "GS", deps=[tcc2])
    if DEBUG_TAPS:
        P.dma("gpsimd", lambda e: e.dma_start(out=ctxg_s, in_=cc2o), "GS", deps=[tcc2])

    P.barrier()
    es_b.close()
    es_p2.close()
    es_fx = ExitStack()
    PS_FIX = es_fx.enter_context(nc.psum_tensor("PS_FIX", [128, 512], F32))
    PS_CS = es_fx.enter_context(nc.psum_tensor("PS_CS", [1, 128], F32))
    vrows = es_fx.enter_context(nc.sbuf_tensor("vrows", [1, 16, 65], F32))
    csr_sb = es_fx.enter_context(nc.sbuf_tensor("csr_sb", [1, 16, 128], BF16))

    tvrows = P.dma("sync", lambda e: e.dma_start(
        out=vrows[0:1, :, :], in_=ctxl[10:11, 2, :, :]), "FXD", deps=[tgr])
    for h in range(H):
        dd = [tgr]
        if h >= 1:
            dd.append(last[("fixv", h - 1)])
        dd.append(tvrows)
        tvb = P.emit("tensor", lambda e, h=h: e.matmul(
            PS_FIX[:, 0:65], onesER[0:1, :], vrows[0:1, h, :],
            start=True, stop=True), deps=dd)
        tst = None
        for ch in range(3):
            P.emit("vector", lambda e, h=h, ch=ch: e.scalar_tensor_tensor(
                statf[:, ch, h, 0:65], ctxl[:, ch, h, :], RATIO, PS_FIX[:, 0:65],
                ALU.mult, ALU.add), deps=[tvb, tgr])
            tst = P.emit("vector", lambda e, h=h, ch=ch: e.tensor_copy(
                statf[:, ch, h, 65:128],
                statf[:, ch, h, 64:65].broadcast_to([128, 63]), ), ser=True)
        last[("fixv", h)] = tst
        dd2 = [tst]
        if h >= 1:
            dd2.append(last[("csrow", h - 1)])
        P.emit("tensor", lambda e, h=h: e.matmul(
            PS_CS[0:1, :], ones_bf[:, 0:1], statf[:, 0, h, :],
            start=True, stop=False), deps=dd2)
        P.emit("tensor", lambda e, h=h: e.matmul(
            PS_CS[0:1, :], ones_bf[:, 0:1], statf[:, 1, h, :],
            start=False, stop=False))
        tc2 = P.emit("tensor", lambda e, h=h: e.matmul(
            PS_CS[0:1, :], ones_bf[0:10, 0:1], statf[0:10, 2, h, :],
            start=False, stop=True))
        last[("csrow", h)] = P.emit("vector", lambda e, h=h: e.tensor_scalar(
            csr_sb[0:1, h, :], PS_CS[0:1, :], EPS, None, ALU.mult), deps=[tc2], ser=True)
    tok_statf = P.dma("sync", lambda e: e.dma_start(
        out=statf[10:11, 2, :, :], in_=csr_sb[0:1, :, :]), "FXD",
        deps=[last[("csrow", H - 1)]])

    if PHASE <= 3:
        return _finish()
    P.barrier()
    es_fx.close()
    es_c = ExitStack()
    xr = es_c.enter_context(nc.sbuf_tensor("xr", [128, 2, 8, 512], BF16))
    q_sb = es_c.enter_context(nc.sbuf_tensor("q_sb", [128, 8, 512], BF16))
    q2t = es_c.enter_context(nc.sbuf_tensor("q2t", [128, 2, 512], BF16))
    dq16 = es_c.enter_context(nc.sbuf_tensor("dq16", [16, 512], F32))
    diagq = es_c.enter_context(nc.sbuf_tensor("diagq", [128, 4, 16], F32))
    negq = es_c.enter_context(nc.sbuf_tensor("negq", [128, 4, 16], F32))
    biasq = es_c.enter_context(nc.sbuf_tensor("biasq", [128, 4, 16], F32))
    eq = es_c.enter_context(nc.sbuf_tensor("eq", [128, 2, 2, 272], BF16))
    qpT = es_c.enter_context(nc.sbuf_tensor("qpT", [128, 2, 3, 512], BF16))
    recp = es_c.enter_context(nc.sbuf_tensor("recp", [64, 512], F32))
    outall = es_c.enter_context(nc.sbuf_tensor("outall", [128, 8, 512], BF16))
    attn_t = es_c.enter_context(nc.sbuf_tensor("attn_t", [128, 4, 1024], BF16))
    es_p3 = ExitStack()
    PS_QD = es_p3.enter_context(nc.psum_tensor("PS_QD", [128, 1024], F32))
    PS_DSH = es_p3.enter_context(nc.psum_tensor("PS_DSH", [128, 1024], F32))
    PS_QPT = es_p3.enter_context(nc.psum_tensor("PS_QPT", [128, 3072], BF16))
    PS_OUT = es_p3.enter_context(nc.psum_tensor("PS_OUT", [128, 512], F32))

    tok_qpt_ones = None
    for _js in range(2):
        tok_qpt_ones = P.dma("sync", lambda e, _js=_js: e.dma_start(
            out=qpT[10:11, _js, 2, :].rearrange("p (a b) -> p a b", a=4),
            in_=ones_bf[0:1, :].rearrange("p (a b) -> p a b", a=1)
            .broadcast_to([1, 4, 128])), "QPO")

    for i in range(NT1):
        par = i % 2
        dd = []
        if ("xruse", i - 2) in last:
            dd.append(last[("xruse", i - 2)])
        txr = P.dma("sync", lambda e, i=i, par=par: e.dma_start(
            out=xr[:, par, :, :],
            in_=xnt_s[:, :, i * 512:(i + 1) * 512].rearrange("a p c -> p a c")),
            f"XR{par}", deps=dd)
        tqevac = []
        tdq_mm = []
        for co in range(8):
            pk = co % 2
            dd0 = [txr, cast_q_done]
            if co >= 2:
                dd0.append(tqevac[co - 2])
            elif ("qfree", i - 1) in last:
                dd0.append(last[("qfree", i - 1)])
            mm = None
            for ci in range(8):
                mm = P.emit("tensor", lambda e, co=co, ci=ci, pk=pk, par=par: e.matmul(
                    PS_QD[:, pk * 512:(pk + 1) * 512],
                    Wq_bf[:, ci, co * 128:(co + 1) * 128], xr[:, par, ci, :],
                    start=(ci == 0), stop=(ci == 7)), deps=(dd0 if ci == 0 else []))
            tqe = P.emit("vector", lambda e, co=co, pk=pk: e.tensor_copy(
                q_sb[:, co, :], PS_QD[:, pk * 512:(pk + 1) * 512]), deps=[mm])
            tqevac.append(tqe)
            dd = []
            if co >= 2:
                dd.append(tdq_mm[co - 2])
            tq2 = P.emit("vector", lambda e, co=co, pk=pk: e.tensor_mul(
                q2t[:, pk, :], q_sb[:, co, :], q_sb[:, co, :]), deps=dd)
            dd = [tq2]
            if co == 0 and ("outev",) in last:
                dd.append(last[("outev",)])
            tdq_mm.append(P.emit("tensor", lambda e, co=co, pk=pk: e.matmul(
                PS_OUT[0:16, :], bdiag[:, co, :], q2t[:, pk, :],
                start=(co == 0), stop=(co == 7)), deps=dd))
        tdq16 = P.emit("vector", lambda e: e.tensor_copy(dq16[:], PS_OUT[0:16, :]),
                       deps=[tdq_mm[-1]])
        tprev = tdq16
        for c in range(4):
            ttr = P.emit("tensor", lambda e, c=c: e.transpose(
                PS_OUT[:, 0:16], dq16[0:16, c * 128:(c + 1) * 128],
                ident_f[0:16, 0:16]), deps=[tprev])
            tprev = P.emit("vector", lambda e, c=c: e.tensor_copy(
                diagq[:, c, :], PS_OUT[:, 0:16]), deps=[ttr])
        tok_diagq = tprev

        for j in range(8):
            js = j % 2
            for c in range(4):
                use_qd = ((j * 4 + c) % 2 == 0)
                sl = (j * 4 + c) % 2
                pd = PS_QD if use_qd else PS_DSH
                dd = [tqevac[j], tok_projT, tqevac[-1]]
                if ("qdred", use_qd) in last:
                    dd.append(last[("qdred", use_qd)])
                P.emit("tensor", lambda e, j=j, c=c, pd=pd: e.matmul(
                    pd[:, 0:266], q_sb[0:64, j, c * 128:(c + 1) * 128],
                    projT[0:64, 0:266], start=True, stop=True,
                    tile_position=(0, 0)), deps=dd)
                mb = P.emit("tensor", lambda e, j=j, c=c, pd=pd: e.matmul(
                    pd[:, 512:778], q_sb[64:128, j, c * 128:(c + 1) * 128],
                    projT[64:128, 0:266], start=True, stop=True,
                    tile_position=(64, 0)))
                P.emit("vector", lambda e, j=j, c=c, pd=pd: e.reduce_max(
                    negq[:, c, 2 * j:2 * j + 2],
                    pd[:].rearrange("p (a x) -> p a x", a=2)[:, :, 0:266],
                    axis=AX.X, negate=True), deps=[mb, tok_diagq])
                tbq = P.emit("vector", lambda e, j=j, c=c: e.scalar_tensor_tensor(
                    biasq[:, c, 2 * j:2 * j + 2], diagq[:, c, 2 * j:2 * j + 2], -1.0,
                    negq[:, c, 2 * j:2 * j + 2], ALU.mult, ALU.add), ser=True)
                texp = []
                for s in range(2):
                    dd2 = [tbq]
                    if ("equse", sl) in last:
                        dd2.append(last[("equse", sl)])
                    texp.append(P.emit("scalar",
                                lambda e, s=s, sl=sl, c=c, j=j, pd=pd: e.activation(
                                    eq[:, sl, s, 0:266], pd[:, 512 * s:512 * s + 266],
                                    AF.Exp, bias=biasq[:, c, 2 * j + s:2 * j + s + 1]),
                                deps=dd2))
                last[("qdred", use_qd)] = texp[-1]
                for s in range(2):
                    for mkc, (m0, m1) in enumerate(MCH):
                        rc = m1 - m0
                        reg = (s * 3 + mkc) * 512 + c * 128
                        dd3 = [texp[s]]
                        if ("qptevall",) in last:
                            dd3.append(last[("qptevall",)])
                        tq = P.emit("tensor", lambda e, s=s, mkc=mkc, m0=m0, rc=rc,
                                    reg=reg, sl=sl, j=j: e.transpose(
                                        PS_QPT[0:rc, reg:reg + 128],
                                        eq[:, sl, s, m0:m0 + rc], ident_bf[:]),
                                    deps=dd3)
                        last[("equse", sl)] = tq
                        last[("qpt_w", s, mkc)] = tq
            for s in range(2):
                h = 2 * j + s
                tevs = []
                dd4 = [last[("qpt_w", s, 0)], last[("qpt_w", s, 1)], tok_qpt_ones]
                if ("qpT_r", js) in last:
                    dd4.append(last[("qpT_r", js)])
                tevs.append(P.emit("vector", lambda e, s=s, js=js:
                            e.tensor_copy(qpT[0:128, js, 0:2, :],
                                          PS_QPT[0:128, :]
                                          .rearrange("p (a x) -> p a x", a=6)
                                          [:, 3 * s:3 * s + 2, :]),
                            deps=dd4))
                dd4b = [last[("qpt_w", s, 2)], tok_qpt_ones]
                if ("qpT_r", js) in last:
                    dd4b.append(last[("qpT_r", js)])
                tevs.append(P.emit("vector", lambda e, s=s, js=js:
                            e.tensor_copy(qpT[0:10, js, 2, :],
                                          PS_QPT[0:10, (s * 3 + 2) * 512:
                                                 (s * 3 + 2) * 512 + 512]),
                            deps=dd4b))
                last[("qptevall",)] = tevs[-1]
                dd5 = tevs + [tok_statf, tok_diagq]
                if ("outev",) in last:
                    dd5.append(last[("outev",)])
                mo = None
                for mkc in range(3):
                    kk = 128 if mkc < 2 else 11
                    mo = P.emit("tensor", lambda e, h=h, mkc=mkc, js=js, kk=kk: e.matmul(
                        PS_OUT[:, :], statf[0:kk, mkc, h, :], qpT[0:kk, js, mkc, :],
                        start=(mkc == 0), stop=(mkc == 2)),
                        deps=(dd5 if mkc == 0 else []))
                last[("qpT_r", js)] = mo
                trl = P.emit("scalar", lambda e: e.activation(
                    recp[:], PS_OUT[64:128, :], AF.Ln), deps=[mo])
                t_exp = P.emit("scalar", lambda e: e.activation(
                    recp[:], recp[:], AF.Exp, scale=-1.0), deps=[trl])
                tmu = P.emit("vector", lambda e, h=h: e.tensor_mul(
                    outall[(h % 2) * 64:(h % 2) * 64 + 64, h // 2, :],
                    PS_OUT[0:64, :], recp[:]), deps=[t_exp, mo])
                last[("outev",)] = tmu
        tatt = []
        for tc in range(4):
            dd6 = [last[("outev",)], cast_o_done, last[("qdred", False)]]
            if tatt:
                dd6.append(tatt[-1])
            if tc == 0 and ("attnw", i - 1) in last:
                dd6.append(last[("attnw", i - 1)])
            mm = None
            for co in range(8):
                for s in range(2):
                    mm = P.emit("tensor", lambda e, tc=tc, co=co, s=s: e.matmul(
                        PS_DSH[:, s * 512:(s + 1) * 512],
                        outall[:, co, tc * 128:(tc + 1) * 128],
                        Wo_bf[:, co, s * 512:(s + 1) * 512],
                        start=(co == 0), stop=(co == 7)),
                        deps=(dd6 if (co == 0 and s == 0) else []))
            tatt.append(P.emit("vector", lambda e, tc=tc: e.tensor_copy(
                attn_t[:, tc, :], PS_DSH[:, :]), deps=[mm]))
        last[("qdred", False)] = tatt[-1]
        taw = P.dma("sync", lambda e, i=i: e.dma_start(
            out=attn_s[4 * i:4 * i + 4, :, :].rearrange("a p c -> p a c"),
            in_=attn_t[:]), f"AW{par}",
            deps=[tatt[-1]])
        last[("attnw", i)] = taw
        last[("xruse", i)] = last[("qdred", True)]
        last[("qfree", i)] = last[("qdred", True)]

    if PHASE <= 4:
        return _finish()
    P.barrier()
    es_c.close()
    es_p3.close()
    es_w.close()
    es_mid.close()

    es_d = ExitStack()
    W1T = es_d.enter_context(nc.sbuf_tensor("W1T", [128, 8, 4096], BF16))
    W2T = es_d.enter_context(nc.sbuf_tensor("W2T", [128, 32, 1024], BF16))
    es_d2 = ExitStack()
    stage = es_d2.enter_context(nc.sbuf_tensor("stage", [128, 2, 4, 1024], F32))
    pmask = es_d2.enter_context(nc.sbuf_tensor("pmask", [128, 4, 1024], BF16))
    nmask = es_d2.enter_context(nc.sbuf_tensor("nmask", [128, 4, 1024], BF16))
    tern = es_d2.enter_context(nc.sbuf_tensor("tern", [128, 4, 1024], BF16))
    es_p4 = ExitStack()
    PS_T = [es_p4.enter_context(nc.psum_tensor(f"PS_T{i}", [128, 1024], BF16))
            for i in range(2)]
    PS_P = es_p4.enter_context(nc.psum_tensor("PS_P", [128, 2], F32))

    tthr = []
    for wsel in range(2):
        a = wsel * 2
        dd = [last[("scuse", wsel, 7)]]
        if tthr:
            dd.append(tthr[-1])
        tp = P.emit("tensor", lambda e, a=a, wsel=wsel: e.matmul(
            PS_P[0:1, wsel:wsel + 1], onesc_f[:, 0:1], acc4[:, a:a + 1],
            start=True, stop=True), deps=dd)
        P.emit("vector", lambda e, wsel=wsel: e.tensor_scalar(
            thr_sb[0:1, wsel:wsel + 1], PS_P[0:1, wsel:wsel + 1],
            1.0 / (FF * D), None, ALU.mult), deps=[tp], ser=True)
        t6 = P.emit("vector", lambda e, wsel=wsel: e.tensor_scalar(
            thr_sb[0:1, wsel:wsel + 1], thr_sb[0:1, wsel:wsel + 1],
            1e-8, 0.5, ALU.max, ALU.mult), ser=True)
        tthr.append(t6)
    tbb = P.emit("tensor", lambda e: e.matmul(
        PS_P[:, 0:2], ones_f[0:1, :], thr_sb[0:1, 0:2], start=True, stop=True),
        deps=tthr)
    tthB = P.emit("vector", lambda e: e.tensor_copy(thrB[:], PS_P[:, 0:2]), deps=[tbb], ser=True)
    tnth = P.emit("vector", lambda e: e.tensor_scalar(
        nthrB[:, 0:2], thrB[:, 0:2], -1.0, None, ALU.mult), ser=True)

    for wsel, wv in enumerate([W1v, W2v]):
        for pc in range(8):
            par = pc % 2
            dd = []
            if ("tnuse", wsel, pc - 2) in last:
                dd.append(last[("tnuse", wsel, pc - 2)])
            elif wsel == 1 and ("tnuse", 0, 6 + pc) in last:
                dd.append(last[("tnuse", 0, 6 + pc)])
            elif wsel == 0 and ("scuse", 1, 6 + pc) in last:
                dd.append(last[("scuse", 1, 6 + pc)])
            tld = P.dma("sync", lambda e, wv=wv, pc=pc, par=par: e.dma_start(
                out=stage[:, par, :, :], in_=wv[pc]), f"PL{par}", deps=dd)
            ddp = [tld, tthB, tnth]
            if ("terns",) in last:
                ddp.append(last[("terns",)])
            P.emit("vector", lambda e, par=par, wsel=wsel: e.tensor_scalar(
                pmask[:], stage[:, par, :, :], thrB[:, wsel:wsel + 1], None,
                ALU.is_gt), deps=ddp, ser=True)
            tnm2 = P.emit("vector", lambda e, par=par, wsel=wsel: e.tensor_scalar(
                nmask[:], stage[:, par, :, :], nthrB[:, wsel:wsel + 1], None,
                ALU.is_lt), ser=True)
            last[("tnuse", wsel, pc)] = tnm2
            dd2 = [tnm2]
            if ("ternuse",) in last:
                dd2.append(last[("ternuse",)])
            tts2 = P.emit("vector", lambda e: e.tensor_sub(tern[:], pmask[:], nmask[:]),
                          deps=dd2, ser=True)
            last[("terns",)] = tts2
            if wsel == 0:
                for dc in range(8):
                    pt = PS_T[dc % 2]
                    tts3 = []
                    for c in range(4):
                        dd3 = [tts2]
                        if ("ptev", dc % 2) in last:
                            dd3.append(last[("ptev", dc % 2)])
                        tts3.append(P.emit("tensor", lambda e, c=c, dc=dc, pt=pt:
                                    e.transpose(pt[:, c * 128:(c + 1) * 128],
                                                tern[:, c, dc * 128:(dc + 1) * 128],
                                                ident_bf[:]), deps=dd3))
                    tev = P.emit("vector", lambda e, dc=dc, pc=pc, pt=pt: e.tensor_copy(
                        W1T[:, dc, pc * 512:(pc + 1) * 512], pt[:, 0:512]), deps=tts3)
                    last[("ptev", dc % 2)] = tev
                    last[("ternuse",)] = tev
            else:
                for fq in range(4):
                    pt = PS_T[fq % 2]
                    tts3 = []
                    for fo in range(8):
                        ffc = fq * 8 + fo
                        dd3 = [tts2]
                        if ("ptev", fq % 2) in last:
                            dd3.append(last[("ptev", fq % 2)])
                        tts3.append(P.emit("tensor", lambda e, ffc=ffc, fo=fo, pt=pt:
                                    e.transpose(pt[:, fo * 128:(fo + 1) * 128],
                                                tern[:, ffc // 8, (ffc % 8) * 128:
                                                     (ffc % 8) * 128 + 128],
                                                ident_bf[:]), deps=dd3))
                    tev = P.emit("vector", lambda e, fq=fq, pc=pc, pt=pt: e.tensor_copy(
                        W2T[:, fq * 8:fq * 8 + 8, pc * 128:(pc + 1) * 128],
                        pt[:].rearrange("p (a b) -> p a b", a=8)), deps=tts3)
                    last[("ptev", fq % 2)] = tev
                    last[("ternuse",)] = tev

    if PHASE <= 5:
        return _finish()
    P.barrier()
    es_p4.close()
    es_d2.close()
    es_e = ExitStack()
    xr2 = es_e.enter_context(nc.sbuf_tensor("xr2", [128, 2, 8, 256], BF16))
    at2 = es_e.enter_context(nc.sbuf_tensor("at2", [128, 2, 2, 1024], BF16))
    x2 = es_e.enter_context(nc.sbuf_tensor("x2", [128, 2, 1024], F32))
    h_sb = es_e.enter_context(nc.sbuf_tensor("h_sb", [128, 32, 256], BF16))
    af = es_e.enter_context(nc.sbuf_tensor("af", [128, 1024], F32))
    outb = es_e.enter_context(nc.sbuf_tensor("outb", [128, 2, 2048], F32))
    es_p5 = ExitStack()
    PS_Hh = [es_p5.enter_context(nc.psum_tensor(f"PS_H{i}", [128, 256], F32))
             for i in range(2)]
    PS_O = es_p5.enter_context(nc.psum_tensor("PS_O", [128, 2048], F32))

    n_out_dma = 0
    for i in range(NT2):
        par = i % 2
        dd = []
        if ("x2use", i - 2) in last:
            dd.append(last[("x2use", i - 2)])
        txr2 = P.dma("sync", lambda e, i=i, par=par: e.dma_start(
            out=xr2[:, par, :, :],
            in_=xnt_s[:, :, i * 256:(i + 1) * 256].rearrange("a p c -> p a c")),
            f"X2L{par}", deps=dd)
        tat2 = P.dma("sync", lambda e, i=i, par=par: e.dma_start(
            out=at2[:, par, :, :],
            in_=attn_s[2 * i:2 * i + 2, :, :].rearrange("a p c -> p a c")),
            f"A2L{par}", deps=dd)
        ddx = list(dd)
        if ("x2s", i - 1) in last:
            ddx.append(last[("x2s", i - 1)])
        tx2 = P.dma("sync", lambda e, i=i: e.dma_start(
            out=x2[:, :, :], in_=xs3[:, 2 * i:2 * i + 2, :]), "X2S", deps=ddx)
        tgel = []
        for ffc in range(32):
            ph = PS_Hh[ffc % 2]
            dd0 = [txr2]
            if ffc >= 2:
                dd0.append(tgel[ffc - 2])
            elif ("gel", i - 1, ffc) in last:
                dd0.append(last[("gel", i - 1, ffc)])
            mm = None
            for ci in range(8):
                mm = P.emit("tensor", lambda e, ffc=ffc, ci=ci, ph=ph, par=par: e.matmul(
                    ph[:, :], W1T[:, ci, ffc * 128:(ffc + 1) * 128],
                    xr2[:, par, ci, :], start=(ci == 0), stop=(ci == 7)),
                    deps=(dd0 if ci == 0 else []))
            tg = P.emit("scalar", lambda e, ffc=ffc, ph=ph: e.activation(
                h_sb[:, ffc, :], ph[:, :], AF.Gelu, bias=b1_sb[:, ffc:ffc + 1]),
                deps=[mm, t_cdone])
            tgel.append(tg)
            last[("gel", i, ffc)] = tg
        mo = None
        for ffc in range(32):
            for tch in range(2):
                for s in range(2):
                    dd0 = [tgel[ffc]]
                    if ffc == 0 and tch == 0 and s == 0 and ("oev", i - 1) in last:
                        dd0.append(last[("oev", i - 1)])
                    mo = P.emit("tensor", lambda e, ffc=ffc, tch=tch, s=s: e.matmul(
                        PS_O[:, tch * 1024 + s * 512:tch * 1024 + (s + 1) * 512],
                        h_sb[:, ffc, tch * 128:(tch + 1) * 128],
                        W2T[:, ffc, s * 512:(s + 1) * 512],
                        start=(ffc == 0), stop=False,
                        skip_group_check=True), deps=dd0)
        tbias = []
        for tch in range(2):
            for s in range(2):
                tbias.append(P.emit("tensor", lambda e, tch=tch, s=s: e.matmul(
                    PS_O[:, tch * 1024 + s * 512:tch * 1024 + (s + 1) * 512],
                    ones_bf[0:1, 0:128], brow[0:1, s * 512:(s + 1) * 512],
                    start=False, stop=True, skip_group_check=True),
                    deps=[tok_brow]))
        tev2 = None
        for tch in range(2):
            P.emit("vector", lambda e, tch=tch, par=par: e.tensor_copy(
                af[:, :], at2[:, par, tch, :]), deps=[tat2, tbias[-1]])
            P.emit("vector", lambda e, tch=tch, par=par: e.tensor_add(
                outb[:, par, tch * 1024:(tch + 1) * 1024],
                PS_O[:, tch * 1024:(tch + 1) * 1024], x2[:, tch, :]), deps=[tx2])
            tev2 = P.emit("vector", lambda e, tch=tch, par=par: e.tensor_add(
                outb[:, par, tch * 1024:(tch + 1) * 1024],
                outb[:, par, tch * 1024:(tch + 1) * 1024], af[:, :]))
        last[("oev", i)] = tev2
        last[("x2use", i)] = tev2
        last[("x2s", i)] = tev2
        P.dma("sync", lambda e, i=i, par=par: e.dma_start(
            out=out3[:, 2 * i:2 * i + 2, :],
            in_=outb[:, par, :].rearrange("p (c d) -> p c d", c=2)), "DO", deps=[tev2])
        n_out_dma += 1

    P.wait("sync", ("DO", 16 * n_out_dma))

    with nc.Block() as block:
        for eng in Prog.ENGINES:
            stream = P.streams[eng]
            def runner(e, stream=stream):
                for f in stream:
                    f(e)
            getattr(block, eng)(runner)
    return nc


_NC_CACHE = None


def _consts():
    ident = np.eye(128)
    c = {
        "ident_bf": ident.astype(ml_dtypes.bfloat16),
        "ident_f": ident.astype(np.float32),
        "ones_bf": np.ones((128, 128), ml_dtypes.bfloat16),
        "ones_f": np.ones((1, 128), np.float32),
        "onesER": np.full((1, 128), EPS * RATIO, np.float32),
        "onesc_f": np.ones((128, 2), np.float32),
        "cpool": np.tile(np.array([[0.0, 1.0, -0.5, LN_EPS, EPS, RATIO, -1.0, 2.0]], np.float32), (128, 1)),
    }
    bd = np.zeros((128, 8, 16), np.float32)
    for p in range(128):
        for co in range(8):
            bd[p, co, 2 * co + p // 64] = 1.0 / 16.0
    c["bdiag"] = bd.astype(ml_dtypes.bfloat16)
    return c


def kernel(L0_attn_x, L0_attn_Wq, L0_attn_Wk, L0_attn_Wv, L0_attn_Wo, L0_attn_bo,
           L0_attn_proj, L0_ffn_W1, L0_ffn_b1, L0_ffn_W2, L0_ffn_b2):
    global _NC_CACHE
    x = np.asarray(L0_attn_x, np.float32)
    shared = {
        "Wq": np.ascontiguousarray(L0_attn_Wq, np.float32),
        "Wk": np.ascontiguousarray(L0_attn_Wk, np.float32),
        "Wv": np.ascontiguousarray(L0_attn_Wv, np.float32),
        "Wo": np.ascontiguousarray(L0_attn_Wo, np.float32),
        "bo": np.ascontiguousarray(L0_attn_bo, np.float32).reshape(1, D),
        "proj": np.ascontiguousarray(L0_attn_proj, np.float32),
        "W1": np.ascontiguousarray(L0_ffn_W1, np.float32),
        "b1": np.ascontiguousarray(L0_ffn_b1, np.float32).reshape(FF),
        "W2": np.ascontiguousarray(L0_ffn_W2, np.float32),
        "b2": np.ascontiguousarray(L0_ffn_b2, np.float32).reshape(1, D),
        **_consts(),
    }
    in_maps = []
    for c in range(8):
        b, half = c // 2, c % 2
        xs_c = np.ascontiguousarray(x[b, half * T:(half + 1) * T, :], np.float32)
        in_maps.append({"xs": xs_c, **shared})

    if _NC_CACHE is None:
        _NC_CACHE = build_nc()
    res = run_bass_kernel_spmd(_NC_CACHE, in_maps, core_ids=list(range(8)))
    outp = np.empty((4, 8192, D), np.float32)
    for c in range(8):
        b, half = c // 2, c % 2
        outp[b, half * T:(half + 1) * T, :] = res.results[c]["out"]
    return outp


# version 08 (final)
# speedup vs baseline: 1.5212x; 1.0216x over previous
import numpy as np
import ml_dtypes
from contextlib import ExitStack

import concourse.bass as bass
import concourse.mybir as mybir
from concourse.bass_utils import run_bass_kernel_spmd

F32 = mybir.dt.float32
BF16 = mybir.dt.bfloat16
AF = mybir.ActivationFunctionType
ALU = mybir.AluOpType
AX = mybir.AxisListType

D = 1024
H = 16
DH = 64
M = 266
T = 4096
EPS = 1e-4
RATIO = float(M) ** -0.5
DN = float(DH) ** -0.25
LN_EPS = 1e-5
FF = 4096

NT1 = 8
NT2 = 16
MCH = [(0, 128), (128, 256), (256, 266)]

DEBUG_TAPS = False
PHASE = 99
RG = [[0, 1], [2, 3], [4, 5], [6, 7]]


class Prog:
    ENGINES = ("sync", "gpsimd", "scalar", "vector", "tensor")

    def __init__(self, nc, es):
        self.nc = nc
        self.es = es
        self.streams = {e: [] for e in self.ENGINES}
        self.sems = {}
        self.cnt = {}
        self.guar = {}
        self.engsem = {"scalar": "Cs", "vector": "Cv", "tensor": "Ct", "gpsimd": "Cg"}
        for s in self.engsem.values():
            self.new_sem(s)

    def new_sem(self, name):
        self.sems[name] = self.es.enter_context(self.nc.semaphore(name))
        self.cnt[name] = 0

    def wait(self, eng, dep):
        if dep is None:
            return
        s, v = dep
        if v <= 0:
            return
        if self.engsem.get(eng) == s:
            return
        key = (eng, s)
        if self.guar.get(key, 0) >= v:
            return
        self.guar[key] = v
        h = self.sems[s]
        self.streams[eng].append(lambda e, h=h, v=v: e.wait_ge(h, v))

    def emit(self, eng, fn, deps=(), ser=False):
        for d in deps:
            self.wait(eng, d)
        s = self.engsem[eng]
        h = self.sems[s]
        if eng in ("scalar", "gpsimd") or ser:
            cur = self.cnt[s]
            if cur > 0:
                self.streams[eng].append(lambda e, h=h, cur=cur: e.wait_ge(h, cur))
        self.cnt[s] += 1
        v = self.cnt[s]
        self.streams[eng].append(lambda e, fn=fn, h=h: fn(e).then_inc(h, 1))
        return (s, v)

    def dma(self, eng, fn, sem, deps=(), step=16):
        for d in deps:
            self.wait(eng, d)
        if sem not in self.sems:
            self.new_sem(sem)
        self.cnt[sem] += step
        v = self.cnt[sem]
        h = self.sems[sem]
        self.streams[eng].append(lambda e, fn=fn, h=h, step=step: fn(e).then_inc(h, step))
        return (sem, v)

    def barrier(self):
        snap = [(s, c) for s, c in self.cnt.items() if c > 0]
        for e in self.ENGINES:
            for s, c in snap:
                self.wait(e, (s, c))


def build_nc():
    nc = bass.Bass("TRN2", target_bir_lowering=False)
    es = ExitStack()
    P = Prog(nc, es)

    def _finish():
        with nc.Block() as block:
            for eng in Prog.ENGINES:
                stream = P.streams[eng]
                def runner(e, stream=stream):
                    for f in stream:
                        f(e)
                getattr(block, eng)(runner)
        return nc
    tap_kind = "ExternalOutput" if DEBUG_TAPS else "Internal"

    xs = nc.dram_tensor("xs", [T, D], F32, kind="ExternalInput").ap()
    Wq = nc.dram_tensor("Wq", [D, D], F32, kind="ExternalInput").ap()
    Wk = nc.dram_tensor("Wk", [D, D], F32, kind="ExternalInput").ap()
    Wv = nc.dram_tensor("Wv", [D, D], F32, kind="ExternalInput").ap()
    Wo = nc.dram_tensor("Wo", [D, D], F32, kind="ExternalInput").ap()
    bo = nc.dram_tensor("bo", [1, D], F32, kind="ExternalInput").ap()
    proj = nc.dram_tensor("proj", [M, DH], F32, kind="ExternalInput").ap()
    W1 = nc.dram_tensor("W1", [FF, D], F32, kind="ExternalInput").ap()
    b1 = nc.dram_tensor("b1", [FF], F32, kind="ExternalInput").ap()
    W2 = nc.dram_tensor("W2", [D, FF], F32, kind="ExternalInput").ap()
    b2 = nc.dram_tensor("b2", [1, D], F32, kind="ExternalInput").ap()
    ident_bf_in = nc.dram_tensor("ident_bf", [128, 128], BF16, kind="ExternalInput").ap()
    ident_f_in = nc.dram_tensor("ident_f", [128, 128], F32, kind="ExternalInput").ap()
    ones_bf_in = nc.dram_tensor("ones_bf", [128, 128], BF16, kind="ExternalInput").ap()
    ones_f_in = nc.dram_tensor("ones_f", [1, 128], F32, kind="ExternalInput").ap()
    onesER_in = nc.dram_tensor("onesER", [1, 128], F32, kind="ExternalInput").ap()
    onesc_f_in = nc.dram_tensor("onesc_f", [128, 2], F32, kind="ExternalInput").ap()
    bdiag_in = nc.dram_tensor("bdiag", [128, 8, 16], BF16, kind="ExternalInput").ap()
    cpool_in = nc.dram_tensor("cpool", [128, 8], F32, kind="ExternalInput").ap()

    out = nc.dram_tensor("out", [T, D], F32, kind="ExternalOutput").ap()

    xnt_s = nc.dram_tensor("xnt_s", [8, 128, T], BF16, kind=tap_kind).ap()
    k_s = nc.dram_tensor("k_s", [8, 128, T], BF16, kind=tap_kind).ap()
    v_s = nc.dram_tensor("v_s", [32, 128, 16 * 66], BF16, kind=tap_kind).ap()
    attn_s = nc.dram_tensor("attn_s", [32, 128, D], BF16, kind=tap_kind).ap()
    cc1i = nc.dram_tensor("cc1i", [1, 16], F32)
    cc1o = nc.dram_tensor("cc1o", [1, 16], F32)
    cc2i = nc.dram_tensor("cc2i", [3, 16, 128, 65], F32)
    cc2o = nc.dram_tensor("cc2o", [3, 16, 128, 65], F32).ap()
    ctxg_s = (nc.dram_tensor("ctxg_s", [3, 16, 128, 65], F32, kind="ExternalOutput").ap()
              if DEBUG_TAPS else None)
    dbg_s = (nc.dram_tensor("dbg_s", [3, 128, 512], F32, kind="ExternalOutput").ap()
             if DEBUG_TAPS else None)

    ident_bf = nc.alloc_sbuf_tensor("ident_bf_sb", [128, 128], BF16)
    ident_f = nc.alloc_sbuf_tensor("ident_f_sb", [128, 128], F32)
    ones_bf = nc.alloc_sbuf_tensor("ones_bf_sb", [128, 128], BF16)
    ones_f = nc.alloc_sbuf_tensor("ones_f_sb", [1, 128], F32)
    onesER = nc.alloc_sbuf_tensor("onesER_sb", [1, 128], F32)
    onesc_f = nc.alloc_sbuf_tensor("onesc_f_sb", [128, 2], F32)
    cpool = nc.alloc_sbuf_tensor("cpool_sb", [128, 8], F32)
    bdiag = nc.alloc_sbuf_tensor("bdiag_sb", [128, 8, 16], BF16)
    projT = nc.alloc_sbuf_tensor("projT_sb", [128, 272], BF16)
    maxacc = nc.alloc_sbuf_tensor("maxacc_sb", [128, 16], F32)
    tmp16 = nc.alloc_sbuf_tensor("tmp16_sb", [128, 16], F32)
    stab16 = nc.alloc_sbuf_tensor("stab16_sb", [16, 1], F32)
    stabrow = nc.alloc_sbuf_tensor("stabrow_sb", [1, 16], F32)
    stabB = nc.alloc_sbuf_tensor("stabB_sb", [128, 16], F32)
    b1_sb = nc.alloc_sbuf_tensor("b1_sb", [128, 32], F32)
    brow = nc.alloc_sbuf_tensor("brow_sb", [1, 1024], BF16)
    browf = nc.alloc_sbuf_tensor("browf_sb", [1, 2048], F32)
    thr_sb = nc.alloc_sbuf_tensor("thr_sb", [1, 2], F32)
    thrB = nc.alloc_sbuf_tensor("thrB_sb", [128, 2], F32)
    nthrB = nc.alloc_sbuf_tensor("nthrB_sb", [128, 2], F32)
    acc4 = nc.alloc_sbuf_tensor("acc4_sb", [128, 4], F32)
    acc1 = nc.alloc_sbuf_tensor("acc1_sb", [128, 1], F32)

    xs3 = xs.rearrange("(c p) d -> p c d", p=128)
    out3 = out.rearrange("(c p) d -> p c d", p=128)

    es_mid = ExitStack()
    diagk = es_mid.enter_context(nc.sbuf_tensor("diagk_sb", [128, 32, 16], F32))
    biask = es_mid.enter_context(nc.sbuf_tensor("biask_sb", [128, 32, 16], F32))
    statf = es_mid.enter_context(nc.sbuf_tensor("statf_sb", [128, 3, 16, 128], BF16))
    ctxl = es_mid.enter_context(nc.sbuf_tensor("ctxl_sb", [128, 3, 16, 65], F32))

    es_w = ExitStack()
    Wq_bf = es_w.enter_context(nc.sbuf_tensor("Wq_bf", [128, 8, 1024], BF16))
    Wk_bf = es_w.enter_context(nc.sbuf_tensor("Wk_bf", [128, 8, 1024], BF16))
    Wv_bf = es_w.enter_context(nc.sbuf_tensor("Wv_bf", [128, 8, 1024], BF16))
    Wo_bf = es_w.enter_context(nc.sbuf_tensor("Wo_bf", [128, 8, 1024], BF16))
    wstage = es_w.enter_context(nc.sbuf_tensor("wstage", [128, 2, 1024], F32))
    pj_f = es_w.enter_context(nc.sbuf_tensor("pj_f", [128, 3, 64], F32))
    pj_bf = es_w.enter_context(nc.sbuf_tensor("pj_bf", [128, 3, 64], BF16))

    es_p1 = ExitStack()
    PS_XNT = es_p1.enter_context(nc.psum_tensor("PS_XNT", [128, 1024], BF16))
    PS_K = [es_p1.enter_context(nc.psum_tensor(f"PS_K{i}", [128, 512], F32))
            for i in range(2)]
    PS_MIX = es_p1.enter_context(nc.psum_tensor("PS_MIX", [128, 512], F32))
    PS_VD = es_p1.enter_context(nc.psum_tensor("PS_VD", [128, 1024], F32))
    PS_DA = es_p1.enter_context(nc.psum_tensor("PS_DA", [128, 1024], F32))

    t_cpool = P.dma("sync", lambda e: e.dma_start(out=cpool[:], in_=cpool_in), "DW")
    for _i, _v in enumerate([0.0, 1.0, -0.5, LN_EPS, EPS, RATIO, -1.0, 2.0]):
        nc.const_aps.register(F32, _v, cpool[:, _i:_i + 1])
    P.wait("scalar", t_cpool)
    P.wait("vector", t_cpool)
    P.wait("gpsimd", t_cpool)
    t_const = P.dma("sync", lambda e: e.dma_start(out=ident_bf[:], in_=ident_bf_in), "DW")
    P.dma("sync", lambda e: e.dma_start(out=ident_f[:], in_=ident_f_in), "DW")
    P.dma("sync", lambda e: e.dma_start(out=ones_bf[:], in_=ones_bf_in), "DW")
    P.dma("sync", lambda e: e.dma_start(out=ones_f[:], in_=ones_f_in), "DW")
    P.dma("sync", lambda e: e.dma_start(out=onesER[:], in_=onesER_in), "DW")
    P.dma("sync", lambda e: e.dma_start(out=onesc_f[:], in_=onesc_f_in), "DW")
    P.dma("sync", lambda e: e.dma_start(out=bdiag[:], in_=bdiag_in), "DW")
    P.dma("sync", lambda e: e.dma_start(out=pj_f[0:128, 0, :], in_=proj[0:128, :]), "DW")
    P.dma("sync", lambda e: e.dma_start(out=pj_f[0:128, 1, :], in_=proj[128:256, :]), "DW")
    P.dma("sync", lambda e: e.dma_start(out=pj_f[0:10, 2, :], in_=proj[256:266, :]), "DW")
    P.dma("sync", lambda e: e.dma_start(out=browf[0:1, 0:1024], in_=bo), "DW")
    P.dma("sync", lambda e: e.dma_start(out=browf[0:1, 1024:2048], in_=b2), "DW")
    t_cdone = P.dma("sync", lambda e: e.dma_start(
        out=b1_sb[:], in_=b1.rearrange("(c p) -> p c", p=128),
        allow_slow_non_contiguous=True), "DW")

    tok_cast = []
    for wi, (wsrc, wdst) in enumerate([(Wq, Wq_bf), (Wk, Wk_bf), (Wv, Wv_bf), (Wo, Wo_bf)]):
        src3 = wsrc.rearrange("(c p) o -> c p o", p=128)
        for ci in range(8):
            slot = (wi * 8 + ci) % 2
            dep = [tok_cast[-2]] if len(tok_cast) >= 2 else []
            tl = P.dma("sync", lambda e, ci=ci, slot=slot, src3=src3:
                       e.dma_start(out=wstage[:, slot, :], in_=src3[ci]),
                       f"WS{slot}", deps=dep)
            tc = P.emit("vector", lambda e, ci=ci, slot=slot, wdst=wdst:
                        e.tensor_copy(wdst[:, ci, :], wstage[:, slot, :]), deps=[tl])
            tok_cast.append(tc)
    cast_q_done = tok_cast[7]
    cast_k_done = tok_cast[15]
    cast_v_done = tok_cast[23]
    cast_o_done = tok_cast[31]

    t1 = P.emit("vector", lambda e: e.tensor_scalar(pj_bf[:], pj_f[:], DN, None, ALU.mult),
                deps=[t_cdone])
    tt_pj = []
    for c, (m0, m1) in enumerate(MCH):
        rc = m1 - m0
        tt_pj.append(P.emit("tensor", lambda e, c=c, rc=rc: e.transpose(
            PS_XNT[0:64, c * 128:c * 128 + rc], pj_bf[0:rc, c, :], ident_bf[0:rc, 0:rc]),
            deps=[t1, t_const]))
    t2 = P.emit("vector", lambda e: e.tensor_copy(projT[0:64, 0:266], PS_XNT[0:64, 0:266]),
                deps=tt_pj)
    tok_projT = P.emit("vector", lambda e: e.tensor_copy(projT[64:128, 0:266],
                                                         projT[0:64, 0:266]), ser=True)
    t4 = P.emit("vector", lambda e: e.tensor_add(browf[0:1, 0:1024], browf[0:1, 0:1024],
                                                 browf[0:1, 1024:2048]), ser=True)
    tok_brow = P.emit("vector", lambda e: e.tensor_copy(brow[0:1, :], browf[0:1, 0:1024]), ser=True)
    tok_maxinit = P.emit("vector", lambda e: e.memset(maxacc[:], -3.0e38))

    es_a = ExitStack()
    xb = es_a.enter_context(nc.sbuf_tensor("xb", [128, 2, 4, 1024], F32))
    xn_bf = es_a.enter_context(nc.sbuf_tensor("xn_bf", [128, 4, 1024], BF16))
    sqs = es_a.enter_context(nc.sbuf_tensor("sqs", [128, 4, 1024], F32))
    stats = es_a.enter_context(nc.sbuf_tensor("stats", [128, 6, 4], F32))
    xnt_t = es_a.enter_context(nc.sbuf_tensor("xnt_t", [128, 8, 512], BF16))
    kt = es_a.enter_context(nc.sbuf_tensor("kt", [128, 8, 512], BF16))
    k2t = es_a.enter_context(nc.sbuf_tensor("k2t", [128, 2, 512], BF16))
    vt = es_a.enter_context(nc.sbuf_tensor("vt", [128, 4, 1056], BF16))
    d16 = es_a.enter_context(nc.sbuf_tensor("d16", [16, 512], F32))
    maxtile = es_a.enter_context(nc.sbuf_tensor("maxtile", [128, 16, 4], F32))

    tok_vones = P.emit("vector", lambda e: e.memset(
        vt[:].rearrange("p c (h x) -> p c h x", h=16)[:, :, :, 64:65], 1.0))

    last = {}
    tok_xnt_evac = t2
    for i in range(NT1):
        par = i % 2
        deps = []
        if ("xuse_s", i - 2) in last:
            deps += [last[("xuse_s", i - 2)], last[("xuse_v", i - 2)]]
        tx = P.dma("sync", lambda e, i=i, par=par: e.dma_start(
            out=xb[:, par, :, :], in_=xs3[:, 4 * i:4 * i + 4, :]), f"XL{par}", deps=deps)
        tmean = P.emit("vector", lambda e, par=par: e.reduce_sum(
            stats[:, 0, :], xb[:, par, :, :], axis=AX.X), deps=[tx])
        tsq = []
        for c in range(4):
            tsq.append(P.emit("scalar", lambda e, par=par, c=c: e.activation(
                sqs[:, c, :], xb[:, par, c, :], AF.Square,
                accum_out=stats[:, 1, c:c + 1]), deps=[tx]))
        P.emit("vector", lambda e: e.tensor_scalar(
            stats[:, 2, :], stats[:, 0, :], 1.0 / D, None, ALU.mult), ser=True)
        P.emit("vector", lambda e: e.tensor_mul(
            stats[:, 3, :], stats[:, 2, :], stats[:, 2, :]), ser=True)
        tev = P.emit("vector", lambda e: e.scalar_tensor_tensor(
            stats[:, 3, :], stats[:, 1, :], 1.0 / D, stats[:, 3, :],
            ALU.mult, ALU.subtract), deps=tsq, ser=True)
        tln = P.emit("scalar", lambda e: e.activation(
            stats[:, 4, :], stats[:, 3, :], AF.Ln, bias=LN_EPS), deps=[tev])
        trs = P.emit("scalar", lambda e: e.activation(
            stats[:, 4, :], stats[:, 4, :], AF.Exp, scale=-0.5))
        tnmean = P.emit("vector", lambda e: e.tensor_scalar(
            stats[:, 5, :], stats[:, 2, :], -1.0, None, ALU.mult), ser=True)
        txn = []
        for c in range(4):
            txn.append(P.emit("vector", lambda e, par=par, c=c: e.tensor_scalar(
                xn_bf[:, c, :], xb[:, par, c, :],
                stats[:, 5, c:c + 1], stats[:, 4, c:c + 1], ALU.add, ALU.mult),
                deps=[trs], ser=True))
        last[("xuse_s", i)] = tsq[-1]
        last[("xuse_v", i)] = txn[-1]

        reuse = [last[("xnt_w", i - 1)]] if ("xnt_w", i - 1) in last else []
        tev8 = []
        for dc in range(8):
            tts = []
            for c in range(4):
                tts.append(P.emit("tensor", lambda e, c=c, dc=dc: e.transpose(
                    PS_XNT[:, c * 128:(c + 1) * 128],
                    xn_bf[:, c, dc * 128:(dc + 1) * 128], ident_bf[:]),
                    deps=[txn[c], tok_xnt_evac, t_const]))
            tok_xnt_evac = P.emit("vector", lambda e, dc=dc: e.tensor_copy(
                xnt_t[:, dc, :], PS_XNT[:, 0:512]), deps=tts + reuse)
            tev8.append(tok_xnt_evac)
        txw = P.dma("sync", lambda e, i=i: e.dma_start(
            out=xnt_s[:, :, i * 512:(i + 1) * 512].rearrange("a p c -> p a c"),
            in_=xnt_t[:]), f"XNW{par}", deps=tev8)
        last[("xnt_w", i)] = txw

        tkevac = []
        tdiag_mm = []
        pend_diag = None
        dprev = last.get(("dtdone", i - 1))
        for co in range(8):
            pk = co % 2
            dd0 = [tev8[7], cast_k_done]
            if co >= 2:
                dd0.append(tkevac[co - 2])
            elif ("kevac", i - 1, co) in last:
                dd0.append(last[("kevac", i - 1, co)])
            mm = None
            for ci in range(8):
                mm = P.emit("tensor", lambda e, co=co, ci=ci, pk=pk: e.matmul(
                    PS_K[pk][:, :], Wk_bf[:, ci, co * 128:(co + 1) * 128],
                    xnt_t[:, ci, :], start=(ci == 0), stop=(ci == 7)),
                    deps=(dd0 if ci == 0 else []))
            if pend_diag is not None:
                tdiag_mm.append(P.emit("tensor", pend_diag[0], deps=pend_diag[1]))
            tke = P.emit("vector", lambda e, co=co, pk=pk: e.tensor_copy(
                kt[:, co, :], PS_K[pk][:, :]), deps=[mm])
            tkevac.append(tke)
            dd = []
            if co >= 2:
                dd.append(tdiag_mm[co - 2])
            tk2 = P.emit("vector", lambda e, co=co, pk=pk: e.tensor_mul(
                k2t[:, pk, :], kt[:, co, :], kt[:, co, :]), deps=dd)
            dd = [tk2]
            if co == 0 and dprev is not None:
                dd.append(dprev)
            pend_diag = (lambda e, co=co, pk=pk: e.matmul(
                PS_MIX[0:16, :], bdiag[:, co, :], k2t[:, pk, :],
                start=(co == 0), stop=(co == 7)), dd)
        tdiag_mm.append(P.emit("tensor", pend_diag[0], deps=pend_diag[1]))
        pend_diag = None
        last[("kevac", i, 0)] = tkevac[6]
        last[("kevac", i, 1)] = tkevac[7]
        tkw = P.dma("sync", lambda e, i=i: e.dma_start(
            out=k_s[:, :, i * 512:(i + 1) * 512].rearrange("a p c -> p a c"),
            in_=kt[:]), f"KW{par}", deps=[tkevac[-1]])

        tvev = []
        for c in range(4):
            dd0 = [tev8[7], cast_v_done, tok_vones]
            if c == 0:
                if ("dred", True) in last:
                    dd0.append(last[("dred", True)])
                if ("vw", i - 1) in last:
                    dd0.append(last[("vw", i - 1)])
            else:
                dd0.append(tvev[-1])
            mm = None
            for ci in range(8):
                for s in range(2):
                    mm = P.emit("tensor", lambda e, c=c, ci=ci, s=s: e.matmul(
                        PS_VD[:, s * 512:(s + 1) * 512],
                        xnt_t[:, ci, c * 128:(c + 1) * 128],
                        Wv_bf[:, ci, s * 512:(s + 1) * 512],
                        start=(ci == 0), stop=(ci == 7)),
                        deps=(dd0 if (ci == 0 and s == 0) else []))
            tvev.append(P.emit("vector", lambda e, c=c: e.tensor_copy(
                vt[:, c, :].rearrange("p (h x) -> p h x", h=16)[:, :, 0:64],
                PS_VD[:].rearrange("p (h x) -> p h x", h=16)), deps=[mm]))
        tvw = P.dma("sync", lambda e, i=i: e.dma_start(
            out=v_s[4 * i:4 * i + 4, :, :].rearrange("a p c -> p a c"),
            in_=vt[:]), f"VW{par}", deps=[tvev[-1]])
        last[("vw", i)] = tvw

        td16 = P.emit("vector", lambda e: e.tensor_copy(d16[:], PS_MIX[0:16, :]),
                      deps=[tdiag_mm[-1]])
        tprev = td16
        for c in range(4):
            ttr = P.emit("tensor", lambda e, c=c: e.transpose(
                PS_MIX[:, 0:16], d16[0:16, c * 128:(c + 1) * 128],
                ident_f[0:16, 0:16]), deps=[tprev])
            tprev = P.emit("vector", lambda e, i=i, c=c: e.tensor_copy(
                diagk[:, 4 * i + c, :], PS_MIX[:, 0:16]), deps=[ttr])
        last[("dtdone", i)] = tprev

        for j in range(8):
            for c in range(4):
                use_vd = ((j * 4 + c) % 2 == 0)
                pd = PS_VD if use_vd else PS_DA
                dd = [tkevac[j], tok_projT]
                if use_vd:
                    dd.append(tvev[-1])
                if ("dred", use_vd) in last:
                    dd.append(last[("dred", use_vd)])
                P.emit("tensor", lambda e, j=j, c=c, pd=pd: e.matmul(
                    pd[:, 0:266], kt[0:64, j, c * 128:(c + 1) * 128],
                    projT[0:64, 0:266], start=True, stop=True,
                    tile_position=(0, 0)), deps=dd)
                mb = P.emit("tensor", lambda e, j=j, c=c, pd=pd: e.matmul(
                    pd[:, 512:778], kt[64:128, j, c * 128:(c + 1) * 128],
                    projT[64:128, 0:266], start=True, stop=True,
                    tile_position=(64, 0)))
                tr = P.emit("vector", lambda e, j=j, c=c, pd=pd: e.reduce_max(
                    maxtile[:, 2 * j:2 * j + 2, c],
                    pd[:].rearrange("p (a x) -> p a x", a=2)[:, :, 0:266],
                    axis=AX.X), deps=[mb, tok_maxinit])
                last[("dred", use_vd)] = tr
        P.emit("vector", lambda e: e.reduce_max(tmp16[:], maxtile[:], axis=AX.X), ser=True)
        tfold = P.emit("vector", lambda e: e.tensor_max(maxacc[:], maxacc[:], tmp16[:]), ser=True)
        last[("fold", i)] = tfold

    tms = P.emit("tensor", lambda e: e.transpose(
        PS_MIX[0:16, 0:128], maxacc[:], ident_f[:]),
        deps=[last[("fold", NT1 - 1)], last[("dtdone", NT1 - 1)]])
    trd = P.emit("vector", lambda e: e.reduce_max(stab16[:], PS_MIX[0:16, 0:128],
                                                  axis=AX.X), deps=[tms])
    tg1 = P.dma("gpsimd", lambda e: e.dma_start(
        out=cc1i.ap().rearrange("a m -> m a"), in_=stab16[:]), "GS", deps=[trd])
    tcc1 = P.dma("gpsimd", lambda e: e.collective_compute(
        "AllReduce", ALU.max, replica_groups=RG,
        ins=[cc1i.ap().opt()], outs=[cc1o.ap().opt()]), "CCS", deps=[tg1], step=1)
    tg2 = P.dma("gpsimd", lambda e: e.dma_start(out=stabrow[:], in_=cc1o.ap()), "GS",
                deps=[tcc1])
    tbc = P.emit("tensor", lambda e: e.matmul(
        PS_MIX[:, 0:16], ones_f[0:1, :], stabrow[0:1, :], start=True, stop=True),
        deps=[tg2, trd])
    tsb = P.emit("vector", lambda e: e.tensor_copy(stabB[:], PS_MIX[:, 0:16]), deps=[tbc], ser=True)
    tok_biask = P.emit("vector", lambda e: e.scalar_tensor_tensor(
        biask[:], diagk[:], -1.0,
        stabB[:].unsqueeze(1).broadcast_to([128, 32, 16]),
        ALU.mult, ALU.subtract), ser=True)

    if PHASE <= 1:
        return _finish()
    P.barrier()
    es_a.close()
    es_p1.close()

    es_b = ExitStack()
    kr = es_b.enter_context(nc.sbuf_tensor("kr", [128, 2, 4, 512], BF16))
    vr = es_b.enter_context(nc.sbuf_tensor("vr", [128, 2, 4, 528], BF16))
    ek = es_b.enter_context(nc.sbuf_tensor("ek", [128, 2, 2, 272], BF16))
    es_p2 = ExitStack()
    PS_D2 = [es_p2.enter_context(nc.psum_tensor(f"PS_D2{i}", [128, 1024], F32))
             for i in range(2)]
    PS_CTX = es_p2.enter_context(nc.psum_tensor("PS_CTX", [128, 2048], F32))
    ctx_h = PS_CTX[:].rearrange("p (h x) -> p h x", h=8)

    tok_ek_ones = P.emit("vector", lambda e: e.memset(ek[:, :, :, 266:267], 1.0))

    stage_t = es_b.enter_context(nc.sbuf_tensor("stage_t", [128, 2, 4, 1024], F32))
    W1v = W1.rearrange("(a c p) d -> a p c d", p=128, c=4)
    W2v = W2.rearrange("(a p) (c d) -> a p c d", p=128, c=4)

    def emit_thr_unit(k):
        wsel, pc = k // 8, k % 8
        wv = W1v if wsel == 0 else W2v
        a = wsel * 2
        par = pc % 2
        if pc == 0:
            last[("thrinit", wsel)] = P.emit("vector", lambda e, a=a: e.memset(
                acc4[:, a:a + 1], 0.0), ser=True)
        dd = []
        if ("scuse", wsel, pc - 2) in last:
            dd.append(last[("scuse", wsel, pc - 2)])
        elif wsel == 1 and ("scuse", 0, 6 + pc) in last:
            dd.append(last[("scuse", 0, 6 + pc)])
        tld = P.dma("gpsimd", lambda e, wv=wv, pc=pc, par=par: e.dma_start(
            out=stage_t[:, par, :, :], in_=wv[pc]), f"PL{par}", deps=dd)
        P.emit("vector", lambda e, par=par: e.reduce_sum(
            acc1[:, 0:1], stage_t[:, par, :, :].rearrange("p c d -> p (c d)"),
            axis=AX.X, apply_absolute_value=True),
            deps=[tld, last[("thrinit", wsel)]], ser=True)
        tacc = P.emit("vector", lambda e, a=a: e.tensor_add(
            acc4[:, a:a + 1], acc4[:, a:a + 1], acc1[:, 0:1]), ser=True)
        last[("scuse", wsel, pc)] = tacc

    tok_ctx_evac = [None, None, None]
    for g in range(2):
        started_banks = set()
        evac_dep = tok_ctx_evac[2]
        for i in range(NT1):
            par = i % 2
            dd = []
            if ("kruse", g, i - 2) in last:
                dd.append(last[("kruse", g, i - 2)])
            elif g == 1 and ("kruse", 0, NT1 - 2 + i) in last:
                dd.append(last[("kruse", 0, NT1 - 2 + i)])
            tkr = P.dma("sync", lambda e, g=g, i=i, par=par: e.dma_start(
                out=kr[:, par, :, :],
                in_=k_s[4 * g:4 * g + 4, :, i * 512:(i + 1) * 512]
                .rearrange("a p c -> p a c")), f"KR{par}", deps=dd)
            dd = []
            if ("vruse", g, i - 2) in last:
                dd.append(last[("vruse", g, i - 2)])
            elif g == 1 and ("vruse", 0, NT1 - 2 + i) in last:
                dd.append(last[("vruse", 0, NT1 - 2 + i)])
            tvr = P.dma("sync", lambda e, g=g, i=i, par=par: e.dma_start(
                out=vr[:, par, :, :],
                in_=v_s[4 * i:4 * i + 4, :, 528 * g:528 * (g + 1)]
                .rearrange("a p c -> p a c")), f"VR{par}", deps=dd)
            emit_thr_unit(g * NT1 + i)
            for jl in range(4):
                for c in range(4):
                    sl = (jl * 4 + c) % 2
                    pd = PS_D2[sl]
                    dd = [tkr, tok_projT]
                    if ("eexp", sl) in last:
                        dd.append(last[("eexp", sl)])
                    P.emit("tensor", lambda e, jl=jl, c=c, pd=pd, par=par: e.matmul(
                        pd[:, 0:266], kr[0:64, par, jl, c * 128:(c + 1) * 128],
                        projT[0:64, 0:266], start=True, stop=True,
                        tile_position=(0, 0)), deps=dd)
                    mb = P.emit("tensor", lambda e, jl=jl, c=c, pd=pd, par=par: e.matmul(
                        pd[:, 512:778], kr[64:128, par, jl, c * 128:(c + 1) * 128],
                        projT[64:128, 0:266], start=True, stop=True,
                        tile_position=(64, 0)))
                    texp = []
                    for s in range(2):
                        h = 8 * g + 2 * jl + s
                        dd2 = [mb, tok_biask, tok_ek_ones]
                        if ("ekuse", sl) in last:
                            dd2.append(last[("ekuse", sl)])
                        texp.append(P.emit("scalar",
                                    lambda e, s=s, sl=sl, c=c, h=h, i=i, pd=pd:
                                    e.activation(
                                        ek[:, sl, s, 0:266],
                                        pd[:, 512 * s:512 * s + 266], AF.Exp,
                                        bias=biask[:, 4 * i + c, h:h + 1]), deps=dd2))
                    last[("eexp", sl)] = texp[-1]
                    mmlast = None
                    for s in range(2):
                        hh = 2 * jl + s
                        for ch, (m0, m1) in enumerate(MCH):
                            bank_key = hh // 2
                            first = bank_key not in started_banks
                            if first:
                                started_banks.add(bank_key)
                            dd3 = [texp[s], tvr]
                            if evac_dep is not None:
                                dd3.append(evac_dep)
                            lo = (hh // 2) * 512 + 256 * (hh % 2) + 68 * ch
                            rows = 128 if ch < 2 else 11
                            e0, e1 = (m0, m1 + 1) if ch == 2 else (m0, m1)
                            mmlast = P.emit("tensor",
                                lambda e, sl=sl, s=s, ch=ch, c=c, par=par,
                                       first=first, lo=lo, rows=rows, e0=e0, e1=e1,
                                       jl=jl: e.matmul(
                                    PS_CTX[0:rows, lo:lo + 65],
                                    ek[:, sl, s, e0:e1],
                                    vr[:, par, c, (2 * jl + s) * 66:(2 * jl + s) * 66 + 65],
                                    start=first, stop=False,
                                    skip_group_check=True), deps=dd3)
                    last[("ekuse", sl)] = mmlast
            last[("kruse", g, i)] = last[("eexp", (jl * 4 + 3) % 2)]
            last[("vruse", g, i)] = last[("ekuse", (jl * 4 + 3) % 2)]
        for ch in range(3):
            tok_ctx_evac[ch] = P.emit("vector", lambda e, g=g, ch=ch: e.tensor_copy(
                ctxl[:, ch, 8 * g:8 * g + 8, :],
                ctx_h[:, :, 68 * ch:68 * ch + 65]),
                deps=[last[("ekuse", 0)], last[("ekuse", 1)]])

    if DEBUG_TAPS:
        P.dma("sync", lambda e: e.dma_start(
            out=dbg_s[0], in_=diagk[:].rearrange("p a b -> p (a b)")), "DBG",
            deps=[tok_biask])
        P.dma("sync", lambda e: e.dma_start(
            out=dbg_s[1], in_=biask[:].rearrange("p a b -> p (a b)")), "DBG",
            deps=[tok_biask])
        P.dma("sync", lambda e: e.dma_start(
            out=dbg_s[2][0:1, 0:16], in_=stabrow[:]), "DBG", deps=[tok_biask])
        P.dma("sync", lambda e: e.dma_start(
            out=dbg_s[2][1:2, 0:16].rearrange("a b -> b a"), in_=stab16[:]), "DBG",
            deps=[tok_biask])
        P.dma("sync", lambda e: e.dma_start(
            out=dbg_s[2][2:3, 0:16], in_=maxacc[0:1, :]), "DBG", deps=[tok_biask])
        P.dma("sync", lambda e: e.dma_start(
            out=dbg_s[2][3:4, 0:16], in_=cc1i.ap()), "DBG", deps=[tok_biask])
        P.dma("sync", lambda e: e.dma_start(
            out=dbg_s[2][4:5, 0:16], in_=cc1o.ap()), "DBG", deps=[tok_biask])
    if PHASE <= 2:
        return _finish()
    tgw = P.dma("gpsimd", lambda e: e.dma_start(
        out=cc2i.ap().rearrange("a b p c -> p a b c"), in_=ctxl[:]), "GS",
        deps=tok_ctx_evac)
    tcc2 = P.dma("gpsimd", lambda e: e.collective_compute(
        "AllReduce", ALU.add, replica_groups=RG,
        ins=[cc2i.ap().opt()], outs=[cc2o.opt()]), "CCS", deps=[tgw], step=1)
    tgr = P.dma("gpsimd", lambda e: e.dma_start(
        out=ctxl[:], in_=cc2o.rearrange("a b p c -> p a b c")), "GS", deps=[tcc2])
    if DEBUG_TAPS:
        P.dma("gpsimd", lambda e: e.dma_start(out=ctxg_s, in_=cc2o), "GS", deps=[tcc2])

    P.barrier()
    es_b.close()
    es_p2.close()
    es_fx = ExitStack()
    PS_FIX = es_fx.enter_context(nc.psum_tensor("PS_FIX", [128, 512], F32))
    PS_CS = es_fx.enter_context(nc.psum_tensor("PS_CS", [1, 128], F32))
    vrows = es_fx.enter_context(nc.sbuf_tensor("vrows", [1, 16, 65], F32))
    csr_sb = es_fx.enter_context(nc.sbuf_tensor("csr_sb", [1, 16, 128], BF16))

    tvrows = P.dma("sync", lambda e: e.dma_start(
        out=vrows[0:1, :, :], in_=ctxl[10:11, 2, :, :]), "FXD", deps=[tgr])
    for h in range(H):
        dd = [tgr]
        if h >= 1:
            dd.append(last[("fixv", h - 1)])
        dd.append(tvrows)
        tvb = P.emit("tensor", lambda e, h=h: e.matmul(
            PS_FIX[:, 0:65], onesER[0:1, :], vrows[0:1, h, :],
            start=True, stop=True), deps=dd)
        tst = None
        for ch in range(3):
            P.emit("vector", lambda e, h=h, ch=ch: e.scalar_tensor_tensor(
                statf[:, ch, h, 0:65], ctxl[:, ch, h, :], RATIO, PS_FIX[:, 0:65],
                ALU.mult, ALU.add), deps=[tvb, tgr])
            tst = P.emit("vector", lambda e, h=h, ch=ch: e.tensor_copy(
                statf[:, ch, h, 65:128],
                statf[:, ch, h, 64:65].broadcast_to([128, 63]), ), ser=True)
        last[("fixv", h)] = tst
        dd2 = [tst]
        if h >= 1:
            dd2.append(last[("csrow", h - 1)])
        P.emit("tensor", lambda e, h=h: e.matmul(
            PS_CS[0:1, :], ones_bf[:, 0:1], statf[:, 0, h, :],
            start=True, stop=False), deps=dd2)
        P.emit("tensor", lambda e, h=h: e.matmul(
            PS_CS[0:1, :], ones_bf[:, 0:1], statf[:, 1, h, :],
            start=False, stop=False))
        tc2 = P.emit("tensor", lambda e, h=h: e.matmul(
            PS_CS[0:1, :], ones_bf[0:10, 0:1], statf[0:10, 2, h, :],
            start=False, stop=True))
        last[("csrow", h)] = P.emit("vector", lambda e, h=h: e.tensor_scalar(
            csr_sb[0:1, h, :], PS_CS[0:1, :], EPS, None, ALU.mult), deps=[tc2], ser=True)
    tok_statf = P.dma("sync", lambda e: e.dma_start(
        out=statf[10:11, 2, :, :], in_=csr_sb[0:1, :, :]), "FXD",
        deps=[last[("csrow", H - 1)]])

    if PHASE <= 3:
        return _finish()
    P.barrier()
    es_fx.close()
    es_c = ExitStack()
    xr = es_c.enter_context(nc.sbuf_tensor("xr", [128, 2, 8, 512], BF16))
    q_sb = es_c.enter_context(nc.sbuf_tensor("q_sb", [128, 8, 512], BF16))
    q2t = es_c.enter_context(nc.sbuf_tensor("q2t", [128, 2, 512], BF16))
    dq16 = es_c.enter_context(nc.sbuf_tensor("dq16", [16, 512], F32))
    diagq = es_c.enter_context(nc.sbuf_tensor("diagq", [128, 4, 16], F32))
    negq = es_c.enter_context(nc.sbuf_tensor("negq", [128, 4, 16], F32))
    biasq = es_c.enter_context(nc.sbuf_tensor("biasq", [128, 4, 16], F32))
    eq = es_c.enter_context(nc.sbuf_tensor("eq", [128, 2, 2, 272], BF16))
    qpT = es_c.enter_context(nc.sbuf_tensor("qpT", [128, 2, 3, 512], BF16))
    recp = es_c.enter_context(nc.sbuf_tensor("recp", [64, 512], F32))
    outall = es_c.enter_context(nc.sbuf_tensor("outall", [128, 8, 512], BF16))
    attn_t = es_c.enter_context(nc.sbuf_tensor("attn_t", [128, 4, 1024], BF16))
    es_p3 = ExitStack()
    PS_QD = es_p3.enter_context(nc.psum_tensor("PS_QD", [128, 1024], F32))
    PS_DSH = es_p3.enter_context(nc.psum_tensor("PS_DSH", [128, 1024], F32))
    PS_QPT = es_p3.enter_context(nc.psum_tensor("PS_QPT", [128, 3072], BF16))
    PS_OUT = es_p3.enter_context(nc.psum_tensor("PS_OUT", [128, 512], F32))

    tok_qpt_ones = None
    for _js in range(2):
        tok_qpt_ones = P.dma("sync", lambda e, _js=_js: e.dma_start(
            out=qpT[10:11, _js, 2, :].rearrange("p (a b) -> p a b", a=4),
            in_=ones_bf[0:1, :].rearrange("p (a b) -> p a b", a=1)
            .broadcast_to([1, 4, 128])), "QPO")

    for i in range(NT1):
        par = i % 2
        dd = []
        if ("xruse", i - 2) in last:
            dd.append(last[("xruse", i - 2)])
        txr = P.dma("sync", lambda e, i=i, par=par: e.dma_start(
            out=xr[:, par, :, :],
            in_=xnt_s[:, :, i * 512:(i + 1) * 512].rearrange("a p c -> p a c")),
            f"XR{par}", deps=dd)
        tqevac = []
        tdq_mm = []
        pend_dq = None
        for co in range(8):
            pk = co % 2
            dd0 = [txr, cast_q_done]
            if co >= 2:
                dd0.append(tqevac[co - 2])
            elif ("qfree", i - 1) in last:
                dd0.append(last[("qfree", i - 1)])
            mm = None
            for ci in range(8):
                mm = P.emit("tensor", lambda e, co=co, ci=ci, pk=pk, par=par: e.matmul(
                    PS_QD[:, pk * 512:(pk + 1) * 512],
                    Wq_bf[:, ci, co * 128:(co + 1) * 128], xr[:, par, ci, :],
                    start=(ci == 0), stop=(ci == 7)), deps=(dd0 if ci == 0 else []))
            if pend_dq is not None:
                tdq_mm.append(P.emit("tensor", pend_dq[0], deps=pend_dq[1]))
            tqe = P.emit("vector", lambda e, co=co, pk=pk: e.tensor_copy(
                q_sb[:, co, :], PS_QD[:, pk * 512:(pk + 1) * 512]), deps=[mm])
            tqevac.append(tqe)
            dd = []
            if co >= 2:
                dd.append(tdq_mm[co - 2])
            tq2 = P.emit("vector", lambda e, co=co, pk=pk: e.tensor_mul(
                q2t[:, pk, :], q_sb[:, co, :], q_sb[:, co, :]), deps=dd)
            dd = [tq2]
            if co == 0 and ("outev",) in last:
                dd.append(last[("outev",)])
            pend_dq = (lambda e, co=co, pk=pk: e.matmul(
                PS_OUT[0:16, :], bdiag[:, co, :], q2t[:, pk, :],
                start=(co == 0), stop=(co == 7)), dd)
        tdq_mm.append(P.emit("tensor", pend_dq[0], deps=pend_dq[1]))
        pend_dq = None
        tdq16 = P.emit("vector", lambda e: e.tensor_copy(dq16[:], PS_OUT[0:16, :]),
                       deps=[tdq_mm[-1]])
        tprev = tdq16
        for c in range(4):
            ttr = P.emit("tensor", lambda e, c=c: e.transpose(
                PS_OUT[:, 0:16], dq16[0:16, c * 128:(c + 1) * 128],
                ident_f[0:16, 0:16]), deps=[tprev])
            tprev = P.emit("vector", lambda e, c=c: e.tensor_copy(
                diagq[:, c, :], PS_OUT[:, 0:16]), deps=[ttr])
        tok_diagq = tprev

        for j in range(8):
            js = j % 2
            for c in range(4):
                use_qd = ((j * 4 + c) % 2 == 0)
                sl = (j * 4 + c) % 2
                pd = PS_QD if use_qd else PS_DSH
                dd = [tqevac[j], tok_projT, tqevac[-1]]
                if ("qdred", use_qd) in last:
                    dd.append(last[("qdred", use_qd)])
                P.emit("tensor", lambda e, j=j, c=c, pd=pd: e.matmul(
                    pd[:, 0:266], q_sb[0:64, j, c * 128:(c + 1) * 128],
                    projT[0:64, 0:266], start=True, stop=True,
                    tile_position=(0, 0)), deps=dd)
                mb = P.emit("tensor", lambda e, j=j, c=c, pd=pd: e.matmul(
                    pd[:, 512:778], q_sb[64:128, j, c * 128:(c + 1) * 128],
                    projT[64:128, 0:266], start=True, stop=True,
                    tile_position=(64, 0)))
                P.emit("vector", lambda e, j=j, c=c, pd=pd: e.reduce_max(
                    negq[:, c, 2 * j:2 * j + 2],
                    pd[:].rearrange("p (a x) -> p a x", a=2)[:, :, 0:266],
                    axis=AX.X, negate=True), deps=[mb, tok_diagq])
                tbq = P.emit("vector", lambda e, j=j, c=c: e.scalar_tensor_tensor(
                    biasq[:, c, 2 * j:2 * j + 2], diagq[:, c, 2 * j:2 * j + 2], -1.0,
                    negq[:, c, 2 * j:2 * j + 2], ALU.mult, ALU.add), ser=True)
                texp = []
                for s in range(2):
                    dd2 = [tbq]
                    if ("equse", sl) in last:
                        dd2.append(last[("equse", sl)])
                    texp.append(P.emit("scalar",
                                lambda e, s=s, sl=sl, c=c, j=j, pd=pd: e.activation(
                                    eq[:, sl, s, 0:266], pd[:, 512 * s:512 * s + 266],
                                    AF.Exp, bias=biasq[:, c, 2 * j + s:2 * j + s + 1]),
                                deps=dd2))
                last[("qdred", use_qd)] = texp[-1]
                for s in range(2):
                    for mkc, (m0, m1) in enumerate(MCH):
                        rc = m1 - m0
                        reg = (s * 3 + mkc) * 512 + c * 128
                        dd3 = [texp[s]]
                        if ("qptevall",) in last:
                            dd3.append(last[("qptevall",)])
                        tq = P.emit("tensor", lambda e, s=s, mkc=mkc, m0=m0, rc=rc,
                                    reg=reg, sl=sl, j=j: e.transpose(
                                        PS_QPT[0:rc, reg:reg + 128],
                                        eq[:, sl, s, m0:m0 + rc], ident_bf[:]),
                                    deps=dd3)
                        last[("equse", sl)] = tq
                        last[("qpt_w", s, mkc)] = tq
            for s in range(2):
                h = 2 * j + s
                tevs = []
                dd4 = [last[("qpt_w", s, 0)], last[("qpt_w", s, 1)], tok_qpt_ones]
                if ("qpT_r", js) in last:
                    dd4.append(last[("qpT_r", js)])
                tevs.append(P.emit("vector", lambda e, s=s, js=js:
                            e.tensor_copy(qpT[0:128, js, 0:2, :],
                                          PS_QPT[0:128, :]
                                          .rearrange("p (a x) -> p a x", a=6)
                                          [:, 3 * s:3 * s + 2, :]),
                            deps=dd4))
                dd4b = [last[("qpt_w", s, 2)], tok_qpt_ones]
                if ("qpT_r", js) in last:
                    dd4b.append(last[("qpT_r", js)])
                tevs.append(P.emit("vector", lambda e, s=s, js=js:
                            e.tensor_copy(qpT[0:10, js, 2, :],
                                          PS_QPT[0:10, (s * 3 + 2) * 512:
                                                 (s * 3 + 2) * 512 + 512]),
                            deps=dd4b))
                last[("qptevall",)] = tevs[-1]
                dd5 = tevs + [tok_statf, tok_diagq]
                if ("outev",) in last:
                    dd5.append(last[("outev",)])
                mo = None
                for mkc in range(3):
                    kk = 128 if mkc < 2 else 11
                    mo = P.emit("tensor", lambda e, h=h, mkc=mkc, js=js, kk=kk: e.matmul(
                        PS_OUT[:, :], statf[0:kk, mkc, h, :], qpT[0:kk, js, mkc, :],
                        start=(mkc == 0), stop=(mkc == 2)),
                        deps=(dd5 if mkc == 0 else []))
                last[("qpT_r", js)] = mo
                trl = P.emit("scalar", lambda e: e.activation(
                    recp[:], PS_OUT[64:128, :], AF.Ln), deps=[mo])
                t_exp = P.emit("scalar", lambda e: e.activation(
                    recp[:], recp[:], AF.Exp, scale=-1.0), deps=[trl])
                tmu = P.emit("vector", lambda e, h=h: e.tensor_mul(
                    outall[(h % 2) * 64:(h % 2) * 64 + 64, h // 2, :],
                    PS_OUT[0:64, :], recp[:]), deps=[t_exp, mo])
                last[("outev",)] = tmu
        tatt = []
        for tc in range(4):
            dd6 = [last[("outev",)], cast_o_done, last[("qdred", False)]]
            if tatt:
                dd6.append(tatt[-1])
            if tc == 0 and ("attnw", i - 1) in last:
                dd6.append(last[("attnw", i - 1)])
            mm = None
            for co in range(8):
                for s in range(2):
                    mm = P.emit("tensor", lambda e, tc=tc, co=co, s=s: e.matmul(
                        PS_DSH[:, s * 512:(s + 1) * 512],
                        outall[:, co, tc * 128:(tc + 1) * 128],
                        Wo_bf[:, co, s * 512:(s + 1) * 512],
                        start=(co == 0), stop=(co == 7)),
                        deps=(dd6 if (co == 0 and s == 0) else []))
            tatt.append(P.emit("vector", lambda e, tc=tc: e.tensor_copy(
                attn_t[:, tc, :], PS_DSH[:, :]), deps=[mm]))
        last[("qdred", False)] = tatt[-1]
        taw = P.dma("sync", lambda e, i=i: e.dma_start(
            out=attn_s[4 * i:4 * i + 4, :, :].rearrange("a p c -> p a c"),
            in_=attn_t[:]), f"AW{par}",
            deps=[tatt[-1]])
        last[("attnw", i)] = taw
        last[("xruse", i)] = last[("qdred", True)]
        last[("qfree", i)] = last[("qdred", True)]

    if PHASE <= 4:
        return _finish()
    P.barrier()
    es_c.close()
    es_p3.close()
    es_w.close()
    es_mid.close()

    es_d = ExitStack()
    W1T = es_d.enter_context(nc.sbuf_tensor("W1T", [128, 8, 4096], BF16))
    W2T = es_d.enter_context(nc.sbuf_tensor("W2T", [128, 32, 1024], BF16))
    es_d2 = ExitStack()
    stage = es_d2.enter_context(nc.sbuf_tensor("stage", [128, 2, 4, 1024], F32))
    pmask = es_d2.enter_context(nc.sbuf_tensor("pmask", [128, 4, 1024], BF16))
    nmask = es_d2.enter_context(nc.sbuf_tensor("nmask", [128, 4, 1024], BF16))
    tern = es_d2.enter_context(nc.sbuf_tensor("tern", [128, 4, 1024], BF16))
    es_p4 = ExitStack()
    PS_T = [es_p4.enter_context(nc.psum_tensor(f"PS_T{i}", [128, 1024], BF16))
            for i in range(2)]
    PS_P = es_p4.enter_context(nc.psum_tensor("PS_P", [128, 2], F32))

    tthr = []
    for wsel in range(2):
        a = wsel * 2
        dd = [last[("scuse", wsel, 7)]]
        if tthr:
            dd.append(tthr[-1])
        tp = P.emit("tensor", lambda e, a=a, wsel=wsel: e.matmul(
            PS_P[0:1, wsel:wsel + 1], onesc_f[:, 0:1], acc4[:, a:a + 1],
            start=True, stop=True), deps=dd)
        P.emit("vector", lambda e, wsel=wsel: e.tensor_scalar(
            thr_sb[0:1, wsel:wsel + 1], PS_P[0:1, wsel:wsel + 1],
            1.0 / (FF * D), None, ALU.mult), deps=[tp], ser=True)
        t6 = P.emit("vector", lambda e, wsel=wsel: e.tensor_scalar(
            thr_sb[0:1, wsel:wsel + 1], thr_sb[0:1, wsel:wsel + 1],
            1e-8, 0.5, ALU.max, ALU.mult), ser=True)
        tthr.append(t6)
    tbb = P.emit("tensor", lambda e: e.matmul(
        PS_P[:, 0:2], ones_f[0:1, :], thr_sb[0:1, 0:2], start=True, stop=True),
        deps=tthr)
    tthB = P.emit("vector", lambda e: e.tensor_copy(thrB[:], PS_P[:, 0:2]), deps=[tbb], ser=True)
    tnth = P.emit("vector", lambda e: e.tensor_scalar(
        nthrB[:, 0:2], thrB[:, 0:2], -1.0, None, ALU.mult), ser=True)

    for wsel, wv in enumerate([W1v, W2v]):
        for pc in range(8):
            par = pc % 2
            dd = []
            if ("tnuse", wsel, pc - 2) in last:
                dd.append(last[("tnuse", wsel, pc - 2)])
            elif wsel == 1 and ("tnuse", 0, 6 + pc) in last:
                dd.append(last[("tnuse", 0, 6 + pc)])
            elif wsel == 0 and ("scuse", 1, 6 + pc) in last:
                dd.append(last[("scuse", 1, 6 + pc)])
            tld = P.dma("sync", lambda e, wv=wv, pc=pc, par=par: e.dma_start(
                out=stage[:, par, :, :], in_=wv[pc]), f"PL{par}", deps=dd)
            ddp = [tld, tthB, tnth]
            if ("terns",) in last:
                ddp.append(last[("terns",)])
            P.emit("vector", lambda e, par=par, wsel=wsel: e.tensor_scalar(
                pmask[:], stage[:, par, :, :], thrB[:, wsel:wsel + 1], None,
                ALU.is_gt), deps=ddp, ser=True)
            tnm2 = P.emit("vector", lambda e, par=par, wsel=wsel: e.tensor_scalar(
                nmask[:], stage[:, par, :, :], nthrB[:, wsel:wsel + 1], None,
                ALU.is_lt), ser=True)
            last[("tnuse", wsel, pc)] = tnm2
            dd2 = [tnm2]
            if ("ternuse",) in last:
                dd2.append(last[("ternuse",)])
            tts2 = P.emit("vector", lambda e: e.tensor_sub(tern[:], pmask[:], nmask[:]),
                          deps=dd2, ser=True)
            last[("terns",)] = tts2
            if wsel == 0:
                for dc in range(8):
                    pt = PS_T[dc % 2]
                    tts3 = []
                    for c in range(4):
                        dd3 = [tts2]
                        if ("ptev", dc % 2) in last:
                            dd3.append(last[("ptev", dc % 2)])
                        tts3.append(P.emit("tensor", lambda e, c=c, dc=dc, pt=pt:
                                    e.transpose(pt[:, c * 128:(c + 1) * 128],
                                                tern[:, c, dc * 128:(dc + 1) * 128],
                                                ident_bf[:]), deps=dd3))
                    tev = P.emit("vector", lambda e, dc=dc, pc=pc, pt=pt: e.tensor_copy(
                        W1T[:, dc, pc * 512:(pc + 1) * 512], pt[:, 0:512]), deps=tts3)
                    last[("ptev", dc % 2)] = tev
                    last[("ternuse",)] = tev
            else:
                for fq in range(4):
                    pt = PS_T[fq % 2]
                    tts3 = []
                    for fo in range(8):
                        ffc = fq * 8 + fo
                        dd3 = [tts2]
                        if ("ptev", fq % 2) in last:
                            dd3.append(last[("ptev", fq % 2)])
                        tts3.append(P.emit("tensor", lambda e, ffc=ffc, fo=fo, pt=pt:
                                    e.transpose(pt[:, fo * 128:(fo + 1) * 128],
                                                tern[:, ffc // 8, (ffc % 8) * 128:
                                                     (ffc % 8) * 128 + 128],
                                                ident_bf[:]), deps=dd3))
                    tev = P.emit("vector", lambda e, fq=fq, pc=pc, pt=pt: e.tensor_copy(
                        W2T[:, fq * 8:fq * 8 + 8, pc * 128:(pc + 1) * 128],
                        pt[:].rearrange("p (a b) -> p a b", a=8)), deps=tts3)
                    last[("ptev", fq % 2)] = tev
                    last[("ternuse",)] = tev

    if PHASE <= 5:
        return _finish()
    P.barrier()
    es_p4.close()
    es_d2.close()
    es_e = ExitStack()
    xr2 = es_e.enter_context(nc.sbuf_tensor("xr2", [128, 2, 8, 256], BF16))
    at2 = es_e.enter_context(nc.sbuf_tensor("at2", [128, 2, 2, 1024], BF16))
    x2 = es_e.enter_context(nc.sbuf_tensor("x2", [128, 2, 1024], F32))
    h_sb = es_e.enter_context(nc.sbuf_tensor("h_sb", [128, 32, 256], BF16))
    af = es_e.enter_context(nc.sbuf_tensor("af", [128, 1024], F32))
    outb = es_e.enter_context(nc.sbuf_tensor("outb", [128, 2, 2048], F32))
    es_p5 = ExitStack()
    PS_Hh = [es_p5.enter_context(nc.psum_tensor(f"PS_H{i}", [128, 256], F32))
             for i in range(2)]
    PS_O = es_p5.enter_context(nc.psum_tensor("PS_O", [128, 2048], F32))

    n_out_dma = 0
    for i in range(NT2):
        par = i % 2
        dd = []
        if ("x2use", i - 2) in last:
            dd.append(last[("x2use", i - 2)])
        txr2 = P.dma("sync", lambda e, i=i, par=par: e.dma_start(
            out=xr2[:, par, :, :],
            in_=xnt_s[:, :, i * 256:(i + 1) * 256].rearrange("a p c -> p a c")),
            f"X2L{par}", deps=dd)
        tat2 = P.dma("sync", lambda e, i=i, par=par: e.dma_start(
            out=at2[:, par, :, :],
            in_=attn_s[2 * i:2 * i + 2, :, :].rearrange("a p c -> p a c")),
            f"A2L{par}", deps=dd)
        ddx = list(dd)
        if ("x2s", i - 1) in last:
            ddx.append(last[("x2s", i - 1)])
        tx2 = P.dma("sync", lambda e, i=i: e.dma_start(
            out=x2[:, :, :], in_=xs3[:, 2 * i:2 * i + 2, :]), "X2S", deps=ddx)
        tgel = []
        for ffc in range(32):
            ph = PS_Hh[ffc % 2]
            dd0 = [txr2]
            if ffc >= 2:
                dd0.append(tgel[ffc - 2])
            elif ("gel", i - 1, ffc) in last:
                dd0.append(last[("gel", i - 1, ffc)])
            mm = None
            for ci in range(8):
                mm = P.emit("tensor", lambda e, ffc=ffc, ci=ci, ph=ph, par=par: e.matmul(
                    ph[:, :], W1T[:, ci, ffc * 128:(ffc + 1) * 128],
                    xr2[:, par, ci, :], start=(ci == 0), stop=(ci == 7)),
                    deps=(dd0 if ci == 0 else []))
            tg = P.emit("scalar", lambda e, ffc=ffc, ph=ph: e.activation(
                h_sb[:, ffc, :], ph[:, :], AF.Gelu, bias=b1_sb[:, ffc:ffc + 1]),
                deps=[mm, t_cdone])
            tgel.append(tg)
            last[("gel", i, ffc)] = tg
        mo = None
        for ffc in range(32):
            for tch in range(2):
                for s in range(2):
                    dd0 = [tgel[ffc]]
                    if ffc == 0 and tch == 0 and s == 0 and ("oev", i - 1) in last:
                        dd0.append(last[("oev", i - 1)])
                    mo = P.emit("tensor", lambda e, ffc=ffc, tch=tch, s=s: e.matmul(
                        PS_O[:, tch * 1024 + s * 512:tch * 1024 + (s + 1) * 512],
                        h_sb[:, ffc, tch * 128:(tch + 1) * 128],
                        W2T[:, ffc, s * 512:(s + 1) * 512],
                        start=(ffc == 0), stop=False,
                        skip_group_check=True), deps=dd0)
        tbias = []
        for tch in range(2):
            for s in range(2):
                tbias.append(P.emit("tensor", lambda e, tch=tch, s=s: e.matmul(
                    PS_O[:, tch * 1024 + s * 512:tch * 1024 + (s + 1) * 512],
                    ones_bf[0:1, 0:128], brow[0:1, s * 512:(s + 1) * 512],
                    start=False, stop=True, skip_group_check=True),
                    deps=[tok_brow]))
        tev2 = None
        for tch in range(2):
            P.emit("vector", lambda e, tch=tch, par=par: e.tensor_copy(
                af[:, :], at2[:, par, tch, :]), deps=[tat2, tbias[-1]])
            P.emit("vector", lambda e, tch=tch, par=par: e.tensor_add(
                outb[:, par, tch * 1024:(tch + 1) * 1024],
                PS_O[:, tch * 1024:(tch + 1) * 1024], x2[:, tch, :]), deps=[tx2])
            tev2 = P.emit("vector", lambda e, tch=tch, par=par: e.tensor_add(
                outb[:, par, tch * 1024:(tch + 1) * 1024],
                outb[:, par, tch * 1024:(tch + 1) * 1024], af[:, :]))
        last[("oev", i)] = tev2
        last[("x2use", i)] = tev2
        last[("x2s", i)] = tev2
        P.dma("sync", lambda e, i=i, par=par: e.dma_start(
            out=out3[:, 2 * i:2 * i + 2, :],
            in_=outb[:, par, :].rearrange("p (c d) -> p c d", c=2)), "DO", deps=[tev2])
        n_out_dma += 1

    P.wait("sync", ("DO", 16 * n_out_dma))

    with nc.Block() as block:
        for eng in Prog.ENGINES:
            stream = P.streams[eng]
            def runner(e, stream=stream):
                for f in stream:
                    f(e)
            getattr(block, eng)(runner)
    return nc


_NC_CACHE = None


def _consts():
    ident = np.eye(128)
    c = {
        "ident_bf": ident.astype(ml_dtypes.bfloat16),
        "ident_f": ident.astype(np.float32),
        "ones_bf": np.ones((128, 128), ml_dtypes.bfloat16),
        "ones_f": np.ones((1, 128), np.float32),
        "onesER": np.full((1, 128), EPS * RATIO, np.float32),
        "onesc_f": np.ones((128, 2), np.float32),
        "cpool": np.tile(np.array([[0.0, 1.0, -0.5, LN_EPS, EPS, RATIO, -1.0, 2.0]], np.float32), (128, 1)),
    }
    bd = np.zeros((128, 8, 16), np.float32)
    for p in range(128):
        for co in range(8):
            bd[p, co, 2 * co + p // 64] = 1.0 / 16.0
    c["bdiag"] = bd.astype(ml_dtypes.bfloat16)
    return c


def kernel(L0_attn_x, L0_attn_Wq, L0_attn_Wk, L0_attn_Wv, L0_attn_Wo, L0_attn_bo,
           L0_attn_proj, L0_ffn_W1, L0_ffn_b1, L0_ffn_W2, L0_ffn_b2):
    global _NC_CACHE
    x = np.asarray(L0_attn_x, np.float32)
    shared = {
        "Wq": np.ascontiguousarray(L0_attn_Wq, np.float32),
        "Wk": np.ascontiguousarray(L0_attn_Wk, np.float32),
        "Wv": np.ascontiguousarray(L0_attn_Wv, np.float32),
        "Wo": np.ascontiguousarray(L0_attn_Wo, np.float32),
        "bo": np.ascontiguousarray(L0_attn_bo, np.float32).reshape(1, D),
        "proj": np.ascontiguousarray(L0_attn_proj, np.float32),
        "W1": np.ascontiguousarray(L0_ffn_W1, np.float32),
        "b1": np.ascontiguousarray(L0_ffn_b1, np.float32).reshape(FF),
        "W2": np.ascontiguousarray(L0_ffn_W2, np.float32),
        "b2": np.ascontiguousarray(L0_ffn_b2, np.float32).reshape(1, D),
        **_consts(),
    }
    in_maps = []
    for c in range(8):
        b, half = c // 2, c % 2
        xs_c = np.ascontiguousarray(x[b, half * T:(half + 1) * T, :], np.float32)
        in_maps.append({"xs": xs_c, **shared})

    if _NC_CACHE is None:
        _NC_CACHE = build_nc()
    res = run_bass_kernel_spmd(_NC_CACHE, in_maps, core_ids=list(range(8)))
    outp = np.empty((4, 8192, D), np.float32)
    for c in range(8):
        b, half = c // 2, c % 2
        outp[b, half * T:(half + 1) * T, :] = res.results[c]["out"]
    return outp
